# version 02
# baseline (speedup 1.0000x reference)
import numpy as np
import ml_dtypes
import concourse.bass as bass
import concourse.mybir as mybir
from concourse.bass_utils import run_bass_kernel_spmd
from contextlib import ExitStack

F32 = mybir.dt.float32
BF16 = mybir.dt.bfloat16
AF = mybir.ActivationFunctionType
ALU = mybir.AluOpType
AX = mybir.AxisListType

B_, L_, D_ = 4, 4096, 1024
E_, HID_ = 8, 4096
TOK = 2048
CAP = 512
NEGC = -1.0e9
JP = 8
JF = 3
EPS = 1e-5
GC1 = 0.044715
GC2 = 1.5957691216057308

ENGINES = ("sync", "tensor", "vector", "scalar", "gpsimd")


class Prog:
    def __init__(self):
        self.ops = {e: [] for e in ENGINES}
        self.cnt = {s: 0 for s in ("sD", "sD2", "sD3", "sPE", "sV", "sA",
                                   "sG", "sO")}

    def emit(self, eng, fn, waits=(), inc=None, amt=1):
        waits = [w for w in waits if w is not None]
        self.ops[eng].append((waits, fn, inc, amt))
        if inc is not None:
            self.cnt[inc] += amt
            return (inc, self.cnt[inc])
        return None

    def tip(self, sem):
        return (sem, self.cnt[sem])

    def dma(self, eng, out, in_, waits=(), sem="sD", **kw):
        return self.emit(
            eng, lambda q, o=out, i=in_, kw=kw: q.dma_start(out=o, in_=i, **kw),
            waits, sem, 16)

    def dmat(self, eng, out, in_, waits=(), sem="sD"):
        return self.emit(
            eng, lambda q, o=out, i=in_: q.dma_start_transpose(o, i),
            waits, sem, 16)

    def realize(self, block, sems):
        for eng in ENGINES:
            ops = self.ops[eng]
            if not ops:
                continue

            def body(q, ops=ops):
                seen = {}
                for waits, fn, inc, amt in ops:
                    for (s, v) in waits:
                        if seen.get(s, 0) < v:
                            q.wait_ge(sems[s], v)
                            seen[s] = v
                    inst = fn(q)
                    if inc is not None:
                        inst.then_inc(sems[inc], amt)
            getattr(block, eng)(body)


def build_program():
    nc = bass.Bass("TRN2", target_bir_lowering=False, debug=False)
    dt = nc.dram_tensor
    x_h = dt("x", [L_, D_], F32, kind="ExternalInput").ap()
    wq_h = dt("wq", [D_, D_], F32, kind="ExternalInput").ap()
    wk_h = dt("wk", [D_, D_], F32, kind="ExternalInput").ap()
    wv_h = dt("wv", [D_, D_], F32, kind="ExternalInput").ap()
    wo_h = dt("wo", [D_, D_], F32, kind="ExternalInput").ap()
    w1_h = dt("w1", [E_, D_, HID_], F32, kind="ExternalInput").ap()
    w2_h = dt("w2", [E_, HID_, D_], F32, kind="ExternalInput").ap()
    wg_h = dt("wg", [D_, E_], F32, kind="ExternalInput").ap()
    bg_h = dt("bg", [1, E_], F32, kind="ExternalInput").ap()
    diagm_h = dt("diagm", [128, 128], F32, kind="ExternalInput").ap()
    peerm_h = dt("peerm", [128, 128], F32, kind="ExternalInput").ap()
    u128_h = dt("u128", [128, 128], BF16, kind="ExternalInput").ap()
    u16_h = dt("u16", [16, 16], F32, kind="ExternalInput").ap()
    bsel_h = dt("bsel", [128, 16 * 16], BF16, kind="ExternalInput").ap()
    ones1_h = dt("ones1", [1, 128], BF16, kind="ExternalInput").ap()
    ones1f_h = dt("ones1f", [1, 128], F32, kind="ExternalInput").ap()
    onesrow_h = dt("onesrow", [1, 512], BF16, kind="ExternalInput").ap()
    ecrow_h = dt("ecrow", [1, E_], BF16, kind="ExternalInput").ap()
    iota512_h = dt("iota512", [128, 512], F32, kind="ExternalInput").ap()
    iotac_h = dt("iotac", [128, 32], F32, kind="ExternalInput").ap()
    ident_h = dt("ident", [128, 128], F32, kind="ExternalInput").ap()
    out_h = dt("out", [TOK, D_], F32, kind="ExternalOutput").ap()

    P = Prog()
    st = ExitStack()

    def sb(name, p, f, dtp, stk=None, side="left"):
        return (stk or st).enter_context(
            nc.sbuf_tensor("s_" + name, [p, f], dtp, side=side))

    phx = ExitStack()
    xn = sb("xn", 128, 32 * 1024, BF16, phx, side="right")
    diagm = sb("diagm", 128, 128, F32)
    peerm = sb("peerm", 128, 128, F32)
    pos = sb("pos", 128, 16, F32)
    tfc = sb("tfc", 128, 32, F32)
    small = sb("small", 128, 64, F32)

    xv = xn.ap().rearrange("p (b c) -> p b c", c=1024)

    psum = [st.enter_context(nc.psum_tensor(f"pp{i}", [128, 512], F32))
            for i in range(8)]

    semst = st.enter_context(ExitStack())
    sems = {s: semst.enter_context(nc.semaphore("m_" + s))
            for s in ("sD", "sD2", "sD3", "sPE", "sV", "sA", "sG", "sO")}
    blk = st.enter_context(nc.Block())

    ph123 = ExitStack()
    ones1 = sb("ones1", 1, 128, BF16, ph123, side="right")
    ones1f = sb("ones1f", 1, 128, F32, ph123, side="right")
    onesrow = sb("onesrow", 1, 512, BF16, ph123, side="right")
    ecrow = sb("ecrow", 1, E_, BF16, ph123, side="right")
    ident = sb("ident", 128, 128, F32, ph123, side="right")
    u128 = sb("u128", 128, 128, BF16, ph123, side="right")
    u16 = sb("u16", 16, 16, F32, ph123, side="right")
    bsel = sb("bsel", 128, 16 * 16, BF16, ph123, side="right")
    iota512 = sb("iota512", 128, 512, F32, ph123, side="right")
    iotac = sb("iotac", 128, 32, F32, ph123, side="right")
    pos_bc = sb("pos_bc", 128, TOK, F32, ph123, side="right")
    for dst, src in [(diagm, diagm_h), (peerm, peerm_h), (ones1, ones1_h),
                     (ones1f, ones1f_h), (onesrow, onesrow_h),
                     (ecrow, ecrow_h), (ident, ident_h)]:
        P.dma("gpsimd", dst[:], src, sem="sD3")
    ev_consts = P.tip("sD3")
    bselv = bsel.ap().rearrange("p (b c) -> p b c", c=16)
    for dst, src in [(u128, u128_h), (bsel, bsel_h), (iota512, iota512_h),
                     (iotac, iotac_h)]:
        P.dma("gpsimd", dst[:], src, sem="sD3")
    P.dma("gpsimd", u16.ap(), u16_h, sem="sD3")
    ev_consts2 = P.tip("sD3")

    ph12 = ExitStack()
    xt = [sb(f"xt{i}", 128, 1024, F32, ph12, side="right") for i in range(2)]
    lnsq = sb("lnsq", 128, 1024, BF16, ph12, side="right")
    xnf = sb("xnf", 128, 1024, F32, ph12, side="right")
    xlo = sb("xlo", 128, 1024, BF16, ph12, side="right")
    xnqT = sb("xnqT", 128, 8 * TOK, BF16, ph12, side="right")
    xloT = sb("xloT", 128, 8 * TOK, BF16, ph12, side="right")
    wgs = sb("wgs", 128, 8 * E_, F32, ph12, side="right")
    wghi = sb("wghi", 128, 8 * E_, BF16, ph12, side="right")
    wglo = sb("wglo", 128, 8 * E_, BF16, ph12, side="right")
    wglof = sb("wglof", 128, 8 * E_, F32, ph12, side="right")
    bgs = sb("bgs", 1, 4 * E_, F32, ph12, side="right")
    bghi = sb("bghi", 1, E_, BF16, ph12, side="right")
    bglo = sb("bglo", 1, E_, BF16, ph12, side="right")
    lgT = sb("lgT", 8, TOK, F32, ph12, side="right")
    lg = sb("lg", 128, 16 * 8, F32, ph12, side="right")
    oh = sb("oh", 128, 16 * 8, F32, ph12, side="right")
    ohb = sb("ohb", 128, 16 * 8, BF16, ph12, side="right")
    rk = sb("rk", 128, 16 * 8, F32, ph12, side="right")
    cnt_sb = sb("cnt_sb", 16, E_, F32, ph12, side="right")
    off_sb = sb("off_sb", 16, E_, F32, ph12, side="right")
    offrow = sb("offrow", 1, 16 * E_, F32, ph12, side="right")

    posT = sb("posT", 16, 128, F32, ph12, side="right")
    posrow = lgT

    e_wg = P.dma("gpsimd", wgs.ap().rearrange("p (b c) -> p b c", c=E_),
                 wg_h.rearrange("(b p) c -> p b c", p=128), sem="sD3")
    e_bg = P.dma("gpsimd", bgs[0:1, 0:E_], bg_h, sem="sD3")
    e_wghi = P.emit("vector", lambda q: q.tensor_copy(wghi[:], wgs[:]),
                    [e_wg], "sV")
    e2 = P.emit("vector", lambda q: q.tensor_tensor(
        wglof[:], wgs[:], wghi[:], ALU.subtract), [e_wghi], "sV")
    e_wglo = P.emit("vector", lambda q: q.tensor_copy(wglo[:], wglof[:]),
                    [e2], "sV")
    e_bghi = P.emit("vector", lambda q: q.tensor_copy(
        bghi[:], bgs[0:1, 0:E_]), [e_bg], "sV")
    e2 = P.emit("vector", lambda q: q.tensor_tensor(
        bgs[0:1, E_:2 * E_], bgs[0:1, 0:E_], bghi[:], ALU.subtract),
        [e_bghi], "sV")
    e_bglo = P.emit("vector", lambda q: q.tensor_copy(
        bglo[:], bgs[0:1, E_:2 * E_]), [e2], "sV")

    xnqTv = xnqT.ap().rearrange("p (b c) -> p b c", c=TOK)
    xloTv = xloT.ap().rearrange("p (b c) -> p b c", c=TOK)
    prev_xt = [None, None]
    prev_lo_dmat = None
    ev_xn = [None] * 32
    for t in range(32):
        bi = t % 2
        e_x = P.dma("sync", xt[bi][:], x_h[t * 128:(t + 1) * 128, :],
                    [prev_xt[bi]], "sD")
        c0, c1 = 2 * bi, 2 * bi + 1
        e_s1 = P.emit("vector", lambda q, b=bi, c=c0: q.reduce_sum(
            small[:, c:c + 1], xt[b][:], AX.X), [e_x], "sV")
        e_s2 = P.emit("scalar", lambda q, b=bi, c=c1: q.activation(
            lnsq[:], xt[b][:], AF.Square, accum_out=small[:, c:c + 1]),
            [e_x], "sA")
        e_m = P.emit("vector", lambda q, c=c0: q.tensor_scalar(
            small[:, c + 4:c + 5], small[:, c:c + 1], 1.0 / D_, None,
            ALU.mult), [e_s1], "sV")
        e_ms = P.emit("scalar", lambda q, c=c0: q.activation(
            small[:, c + 8:c + 9], small[:, c + 4:c + 5], AF.Square),
            [e_m], "sA")
        e_v = P.emit("vector", lambda q, c0=c0, c1=c1: q.tensor_scalar(
            small[:, c1 + 8:c1 + 9], small[:, c1:c1 + 1], 1.0 / D_,
            small[:, c0 + 8:c0 + 9], ALU.mult, ALU.subtract),
            [e_s2, e_ms], "sV")
        e_ve = P.emit("vector", lambda q, c1=c1: q.tensor_scalar(
            small[:, c1 + 8:c1 + 9], small[:, c1 + 8:c1 + 9], EPS, None,
            ALU.add), [e_v], "sV")
        e_sq = P.emit("scalar", lambda q, c1=c1: q.activation(
            small[:, c1 + 12:c1 + 13], small[:, c1 + 8:c1 + 9], AF.Sqrt),
            [e_ve], "sA")
        e_rs = P.emit("vector", lambda q, c1=c1: q.reciprocal(
            small[:, c1 + 16:c1 + 17], small[:, c1 + 12:c1 + 13]),
            [e_sq], "sV")
        ev_xn[t] = P.emit("vector", lambda q, b=bi, t=t, c0=c0, c1=c1:
                          q.tensor_scalar(
            xv[:, t, :], xt[b][:], small[:, c0 + 4:c0 + 5],
            small[:, c1 + 16:c1 + 17], ALU.subtract, ALU.mult),
            [e_rs], "sV")
        prev_xt[bi] = ev_xn[t]
        if t < 16:
            e_f = P.emit("vector", lambda q, b=bi, c0=c0, c1=c1:
                         q.tensor_scalar(
                xnf[:], xt[b][:], small[:, c0 + 4:c0 + 5],
                small[:, c1 + 16:c1 + 17], ALU.subtract, ALU.mult),
                [e_rs], "sV")
            e_lo = P.emit("vector", lambda q, t=t: q.tensor_tensor(
                xlo[:], xnf[:], xv[:, t, :], ALU.subtract),
                [e_f, ev_xn[t], prev_lo_dmat], "sV")
            prev_xt[bi] = e_lo
            for db in range(8):
                P.dmat("sync", xnqTv[:, db, t * 128:(t + 1) * 128],
                       xv[:, t, db * 128:(db + 1) * 128], [ev_xn[t]], "sD")
                P.dmat("sync", xloTv[:, db, t * 128:(t + 1) * 128],
                       xlo[:, db * 128:(db + 1) * 128], [e_lo], "sD")
            prev_lo_dmat = P.tip("sD")
    ev_xT_all = P.tip("sD")

    wghv = wghi.ap().rearrange("p (b c) -> p b c", c=E_)
    wglv = wglo.ap().rearrange("p (b c) -> p b c", c=E_)
    lg_ev = []
    for ch in range(4):
        tl = ch * 512
        for db in range(8):
            P.emit("tensor", lambda q, db=db, tl=tl, f=(db == 0): q.matmul(
                psum[0].ap()[0:8, :], wghv[:, db, :],
                xnqTv[:, db, tl:tl + 512], start=f, stop=False),
                [ev_xT_all, e_wghi, lg_ev[-1] if lg_ev else None], None)
            P.emit("tensor", lambda q, db=db, tl=tl: q.matmul(
                psum[0].ap()[0:8, :], wglv[:, db, :],
                xnqTv[:, db, tl:tl + 512], start=False, stop=False),
                [e_wglo], None)
            P.emit("tensor", lambda q, db=db, tl=tl: q.matmul(
                psum[0].ap()[0:8, :], wghv[:, db, :],
                xloTv[:, db, tl:tl + 512], start=False, stop=False), [], None)
        P.emit("tensor", lambda q: q.matmul(
            psum[0].ap()[0:8, :], bghi[:], onesrow[:], start=False,
            stop=False), [e_bghi, ev_consts], None)
        e_mm = P.emit("tensor", lambda q: q.matmul(
            psum[0].ap()[0:8, :], bglo[:], onesrow[:], start=False,
            stop=True), [e_bglo], "sPE")
        lg_ev.append(P.emit("vector", lambda q, tl=tl: q.tensor_copy(
            lgT[:, tl:tl + 512], psum[0].ap()[0:8, :]), [e_mm], "sV"))

    lgv = lg.ap().rearrange("p (b c) -> p b c", c=8)
    ohv = oh.ap().rearrange("p (b c) -> p b c", c=8)
    ohbv = ohb.ap().rearrange("p (b c) -> p b c", c=8)
    rkv = rk.ap().rearrange("p (b c) -> p b c", c=8)
    ev_oh = [None] * 16
    ev_ohb = [None] * 16
    prev_tp = None
    for b in range(16):
        e_tp = P.emit("tensor", lambda q, b=b: q.transpose(
            psum[1].ap()[:, 0:8], lgT[:, b * 128:(b + 1) * 128],
            ident.ap()[0:8, 0:8]),
            [lg_ev[b // 4], ev_consts, prev_tp], "sPE")
        e_cp = P.emit("vector", lambda q, b=b: q.tensor_copy(
            lgv[:, b, :], psum[1].ap()[:, 0:8]), [e_tp], "sV")
        prev_tp = e_cp
        sc = 32 + (b % 8)
        e_mx = P.emit("vector", lambda q, b=b, sc=sc: q.tensor_reduce(
            small[:, sc:sc + 1], lgv[:, b, :], AX.X, ALU.max), [e_cp], "sV")
        ev_oh[b] = P.emit("vector", lambda q, b=b, sc=sc: q.tensor_scalar(
            ohv[:, b, :], lgv[:, b, :], small[:, sc:sc + 1], None, ALU.is_ge),
            [e_mx], "sV")
        ev_ohb[b] = P.emit("vector", lambda q, b=b: q.tensor_copy(
            ohbv[:, b, :], ohv[:, b, :]), [ev_oh[b]], "sV")
    for b in range(16):
        P.emit("tensor", lambda q, b=b: q.matmul(
            psum[2].ap()[0:16, 0:8], bselv[:, b, :], ohbv[:, b, :],
            start=(b == 0), stop=(b == 15)),
            [ev_ohb[b], ev_consts2], "sPE" if b == 15 else None)
    e_cnt = P.emit("vector", lambda q: q.tensor_copy(
        cnt_sb[:], psum[2].ap()[0:16, 0:8]), [P.tip("sPE")], "sV")
    e_omm = P.emit("tensor", lambda q: q.matmul(
        psum[3].ap()[0:16, 0:8], u16[:], cnt_sb[:], start=True, stop=True),
        [e_cnt], "sPE")
    e_off = P.emit("vector", lambda q: q.tensor_copy(
        off_sb[:], psum[3].ap()[0:16, 0:8]), [e_omm], "sV")
    e_orow = P.dma("sync", offrow.ap().rearrange("a (b c) -> a b c", c=E_),
                   off_sb.ap(), [e_off], "sD")

    ev_pos = [None] * 16
    for b in range(16):
        pp = psum[4 + b % 2]
        P.emit("tensor", lambda q, b=b, pp=pp: q.matmul(
            pp.ap()[:, 0:8], u128[:], ohbv[:, b, :], start=True, stop=False),
            [ev_ohb[b], ev_consts2, ev_pos[b - 2] if b >= 2 else None], None)
        P.emit("tensor", lambda q, b=b, pp=pp: q.matmul(
            pp.ap()[:, 0:8], ones1f[:], offrow[:, b * 8:(b + 1) * 8],
            start=False, stop=False), [e_orow], None)
        e3 = P.emit("tensor", lambda q, pp=pp: q.matmul(
            pp.ap()[:, 0:8], ones1[:], ecrow[:], start=False, stop=True),
            [ev_consts], "sPE")
        e_rk = P.emit("vector", lambda q, b=b, pp=pp: q.tensor_copy(
            rkv[:, b, :], pp.ap()[:, 0:8]), [e3], "sV")
        e_pr = P.emit("vector", lambda q, b=b: q.tensor_tensor(
            rkv[:, b, :], rkv[:, b, :], ohv[:, b, :], ALU.mult),
            [e_rk, ev_oh[b]], "sV")
        ev_pos[b] = P.emit("vector", lambda q, b=b: q.reduce_sum(
            pos[:, b:b + 1], rkv[:, b, :], AX.X), [e_pr], "sV")
    e_pt = P.emit("tensor", lambda q: q.transpose(
        psum[5].ap()[0:16, 0:128], pos[:], ident[:]),
        ev_pos + [P.tip("sPE")], "sPE")
    e_ptc = P.emit("vector", lambda q: q.tensor_copy(
        posT[:], psum[5].ap()[0:16, 0:128]), [e_pt], "sV")
    e_prw = P.dma("sync",
                  posrow.ap()[0:1, :].rearrange("a (b c) -> a b c", c=128),
                  posT.ap(), [e_ptc], "sD")
    ev_pbc = []
    for ch in range(4):
        e_b = P.emit("tensor", lambda q, ch=ch: q.matmul(
            psum[6].ap()[:, :], ones1f[:],
            posrow.ap()[0:1, ch * 512:(ch + 1) * 512],
            start=True, stop=True),
            [e_prw, ev_consts, ev_pbc[-1] if ev_pbc else None], "sPE")
        ev_pbc.append(P.emit("vector", lambda q, ch=ch: q.tensor_copy(
            pos_bc[:, ch * 512:(ch + 1) * 512], psum[6].ap()[:, :]),
            [e_b], "sV"))
    ev_posbc = ev_pbc[-1]

    ph12.close()
    g = sb("g", 128, 16 * 1024, BF16)
    gv = g.ap().rearrange("p (b c) -> p b c", c=1024)
    ph3 = ExitStack()
    m1s = sb("m1s", 128, 512, BF16, ph3, side="right")
    posadj = sb("posadj", 128, 16, F32, ph3, side="right")
    xns = sb("xns", 128, 8 * 512, BF16, ph3, side="right")
    w1st = [sb(f"w1st{i}", 128, 8 * 256, F32, ph3, side="right") for i in range(2)]
    w1bf = [sb(f"w1bf{i}", 128, 8 * 256, BF16, ph3, side="right") for i in range(2)]
    ghf = sb("ghf", 128, 32 * 512, BF16, ph3, side="right")
    gx2 = sb("gx2", 128, 1024, F32, ph3, side="right")
    gxa = sb("gxa", 128, 1024, F32, ph3, side="right")
    gxs = gx2
    w2st = [sb(f"w2st{i}", 128, 1024, F32, ph3, side="right") for i in range(2)]
    w2bf = [sb(f"w2bf{i}", 128, 1024, BF16, ph3, side="right") for i in range(2)]
    ysb = sb("ysb", 128, 4 * 1024, BF16, ph3, side="right")
    m1t = [sb(f"m1t{i}", 128, 128, BF16, ph3, side="right") for i in range(4)]
    xnsv = xns.ap().rearrange("p (b c) -> p b c", c=512)
    w1sv = [w.ap().rearrange("p (b c) -> p b c", c=256) for w in w1st]
    w1bv = [w.ap().rearrange("p (b c) -> p b c", c=256) for w in w1bf]
    ghv = ghf.ap().rearrange("p (b c) -> p b c", c=512)
    ysv = ysb.ap().rearrange("p (b c) -> p b c", c=1024)

    bar12 = [P.tip("sV"), P.tip("sA"), P.tip("sPE"), P.tip("sD"),
             P.tip("sD3"), P.tip("sD2")]
    e_gz = P.emit("vector", lambda q: q.memset(g[:], 0.0), [], "sV")
    ev_g = [e_gz] * 16
    ev_m1_free = bar12[0]
    ev_xns_free = None
    ev_w1free = list(bar12[:2])
    ev_w2free = list(bar12[:2])
    ev_gh_w2done = None
    ev_y_used = None
    ev_gelu_free = [None, None]
    ev_psum_free = {i: None for i in range(8)}
    ev_psum_free[6] = ev_posbc
    for e in range(E_):
        ev_sort = []
        for tb in range(16):
            e_a = P.emit("vector", lambda q, tb=tb, e=e: q.tensor_scalar(
                posadj[:, tb:tb + 1], pos[:, tb:tb + 1], -512.0 * e, None,
                ALU.add),
                (bar12 if e == 0 else []) + [ev_pos[tb] if e == 0 else None],
                "sV")
            e_m1 = P.emit("vector", lambda q, tb=tb: q.tensor_scalar(
                m1s[:], iota512[:], posadj[:, tb:tb + 1], None,
                ALU.is_equal), [e_a, ev_consts2, ev_m1_free], "sV")
            e_mm = None
            for db in range(8):
                e_mm = P.emit("tensor", lambda q, db=db, tb=tb: q.matmul(
                    psum[db].ap()[:, :], xv[:, tb, db * 128:(db + 1) * 128],
                    m1s[:], start=(tb == 0), stop=(tb == 15),
                    skip_group_check=True),
                    [e_m1, ev_xn[tb] if e == 0 else None, ev_xns_free,
                     ev_psum_free[db]], "sPE" if db == 7 else None)
            ev_m1_free = e_mm
        e_srt = P.tip("sPE")
        for db in range(8):
            ev_sort.append(P.emit("vector", lambda q, db=db: q.tensor_copy(
                xnsv[:, db, :], psum[db].ap()[:, :]), [e_srt], "sV"))
            ev_psum_free[db] = ev_sort[-1]

        ev_gelu_done = []
        for sl in range(16):
            sbi = sl % 2
            h0 = sl * 256
            e_w1d = P.dma("gpsimd", w1sv[sbi],
                          w1_h[e, :, h0:h0 + 256]
                          .rearrange("(b p) c -> p b c", p=128),
                          (bar12 if (e == 0 and sl < 2) else [])
                          + [ev_w1free[sbi]], "sD3")
            e_w1c = P.emit("vector", lambda q, sbi=sbi: q.tensor_copy(
                w1bf[sbi][:], w1st[sbi][:]), [e_w1d], "sV")
            for hb in range(2):
                hloc = sl * 2 + hb
                pb = hloc % 2
                e_mm = None
                for db in range(8):
                    e_mm = P.emit("tensor",
                                  lambda q, db=db, sbi=sbi, hb=hb, pb=pb:
                                  q.matmul(
                        psum[pb].ap()[:, :],
                        w1bv[sbi][:, db, hb * 128:(hb + 1) * 128],
                        xnsv[:, db, :], start=(db == 0), stop=(db == 7)),
                        [e_w1c, ev_sort[db], ev_psum_free[pb],
                         ev_gh_w2done], "sPE" if db == 7 else None)
                par = hloc % 2
                cs = slice(par * 512, par * 512 + 512)
                e1 = P.emit("scalar", lambda q, pb=pb, cs=cs: q.activation(
                    gx2[:, cs], psum[pb].ap()[:, :], AF.Square),
                    [e_mm, ev_gelu_free[par]], "sA")
                e2 = P.emit("gpsimd", lambda q, cs=cs: q.tensor_scalar(
                    gxa[:, cs], gx2[:, cs], GC1, 1.0, ALU.mult, ALU.add),
                    [e1], "sG")
                e3 = P.emit("vector", lambda q, pb=pb, cs=cs: q.tensor_tensor(
                    gxs[:, cs], gxa[:, cs], psum[pb].ap()[:, :], ALU.mult),
                    [e2], "sV")
                e4 = P.emit("scalar", lambda q, cs=cs: q.activation(
                    gxa[:, cs], gxs[:, cs], AF.Sigmoid, scale=GC2),
                    [e3], "sA")
                e5 = P.emit("vector", lambda q, pb=pb, cs=cs, hloc=hloc:
                            q.tensor_tensor(
                    ghv[:, hloc, :], gxa[:, cs], psum[pb].ap()[:, :],
                    ALU.mult), [e4, ev_gh_w2done], "sV")
                ev_psum_free[pb] = e5
                ev_gelu_free[par] = e5
                ev_gelu_done.append(e5)
            ev_w1free[sbi] = ev_gelu_done[-1]

        e_mm = None
        for hbw in range(32):
            sbi = hbw % 2
            e_w2d = P.dma("gpsimd", w2st[sbi][:],
                          w2_h[e, hbw * 128:(hbw + 1) * 128, :],
                          (bar12 if (e == 0 and hbw < 2) else [])
                          + [ev_w2free[sbi]], "sD3")
            e_w2c = P.emit("vector", lambda q, sbi=sbi: q.tensor_copy(
                w2bf[sbi][:], w2st[sbi][:]), [e_w2d], "sV")
            for i4 in range(4):
                for dh in range(2):
                    pb = i4 * 2 + dh
                    e_mm = P.emit("tensor",
                                  lambda q, sbi=sbi, hbw=hbw, i4=i4, dh=dh,
                                  pb=pb: q.matmul(
                        psum[pb].ap()[:, :],
                        ghv[:, hbw, i4 * 128:(i4 + 1) * 128],
                        w2bf[sbi][:, dh * 512:(dh + 1) * 512],
                        start=(hbw == 0), stop=(hbw == 31),
                        skip_group_check=True),
                        [e_w2c, ev_gelu_done[hbw], ev_psum_free[pb],
                         ev_y_used],
                        "sPE" if (hbw == 31 or (i4 == 3 and dh == 1))
                        else None)
            ev_w2free[sbi] = e_mm
        ev_gh_w2done = e_mm
        e_pe_done = P.tip("sPE")
        ev_yev = []
        for i4 in range(4):
            for dh in range(2):
                pb = i4 * 2 + dh
                e_ev = P.emit("vector", lambda q, i4=i4, dh=dh, pb=pb:
                              q.tensor_copy(
                    ysv[:, i4, dh * 512:(dh + 1) * 512],
                    psum[pb].ap()[:, :]), [e_pe_done, ev_y_used], "sV")
                ev_psum_free[pb] = e_ev
                ev_yev.append(e_ev)
        e_y = ev_yev[-1]
        ev_un = []
        prev_grp = None
        for tb in range(16):
            for dh in range(2):
                pb = 4 + (tb * 2 + dh) % 2
                e_mm = None
                for ib in range(4):
                    mb = m1t[ib]
                    if dh == 0:
                        e_mt = P.emit("vector",
                                      lambda q, tb=tb, ib=ib, e=e, mb=mb:
                                      q.tensor_scalar(
                            mb[:], pos_bc[:, tb * 128:(tb + 1) * 128],
                            iotac[:, 4 * e + ib:4 * e + ib + 1], None,
                            ALU.is_equal),
                            [ev_posbc, ev_consts2, prev_grp], "sV")
                    else:
                        e_mt = None
                    e_mm = P.emit("tensor", lambda q, ib=ib, dh=dh, mb=mb,
                                  pb=pb: q.matmul(
                        pb_ap(psum, pb), mb[:],
                        ysv[:, ib, dh * 512:(dh + 1) * 512],
                        start=(ib == 0), stop=(ib == 3)),
                        [e_mt, e_y, ev_psum_free[pb]],
                        "sPE" if ib == 3 else None)
                prev_grp = e_mm if dh == 1 else prev_grp
                e_add = P.emit("vector", lambda q, tb=tb, dh=dh, pb=pb:
                               q.tensor_tensor(
                    gv[:, tb, dh * 512:(dh + 1) * 512], pb_ap(psum, pb),
                    gv[:, tb, dh * 512:(dh + 1) * 512], ALU.add),
                    [e_mm, ev_g[tb]], "sV")
                ev_psum_free[pb] = e_add
                ev_g[tb] = e_add
                ev_un.append(e_add)
        ev_y_used = ev_un[-1]
        ev_xns_free = P.tip("sPE")
    ev_silu = [None] * 16
    for tb in range(16):
        scr = gx2 if tb % 2 == 0 else gxa
        e_s = P.emit("scalar", lambda q, tb=tb, scr=scr: q.activation(
            scr[:, 0:1024], gv[:, tb, :], AF.Sigmoid),
            [ev_g[tb], ev_silu[tb - 2] if tb >= 2 else None], "sA")
        ev_silu[tb] = P.emit("vector", lambda q, tb=tb, scr=scr: q.tensor_tensor(
            gv[:, tb, :], gv[:, tb, :], scr[:, 0:1024], ALU.mult),
            [e_s], "sV")

    ph3.close()
    ph123.close()

    ffn_bar = [P.tip("sV"), P.tip("sA"), P.tip("sPE"), P.tip("sG"),
               P.tip("sD3"), P.tip("sD"), P.tip("sD2")]
    xnT = sb("xnT", 128, 8 * 4096, BF16)
    xnTv = xnT.ap().rearrange("p (b c) -> p b c", c=4096)
    for t in range(32):
        for db in range(8):
            P.dmat("sync" if (t + db) % 2 == 0 else "scalar",
                   xnTv[:, db, t * 128:(t + 1) * 128],
                   xv[:, t, db * 128:(db + 1) * 128], ffn_bar + [ev_xn[t]],
                   "sD" if (t + db) % 2 == 0 else "sD2")
    ev_xnT = [P.tip("sD"), P.tip("sD2")]
    phx.close()

    attnout = sb("attnout", 128, 8 * TOK, BF16)
    ph5 = ExitStack()
    wst = sb("wst", 128, 512, F32, ph5, side="right")
    wsb = sb("wsb", 128, 1024, BF16, ph5, side="right")
    wqT = sb("wqT", 128, 8 * 128, BF16, ph5, side="right")
    wkT = sb("wkT", 128, 8 * 128, BF16, ph5, side="right")
    wvT = sb("wvT", 128, 8 * 128, BF16, ph5, side="right")
    ktp = sb("ktp", 128, 4096, BF16, ph5, side="right")
    vbuf = sb("vbuf", 128, 32 * 128, BF16, ph5, side="right")
    qtp = sb("qtp", 128, 256, BF16, ph5, side="right")
    zb = sb("zb", 128, 8192, F32, ph5, side="right")
    ratt = sb("ratt", 128, 4352, BF16, ph5, side="right")
    attT = sb("attT", 128, 4352, BF16, ph5, side="right")
    p8 = sb("p8", 128, 544, BF16, ph5, side="right")
    anv = attnout.ap().rearrange("p (b c) -> p b c", c=TOK)
    wqTv = wqT.ap().rearrange("p (b c) -> p b c", c=128)
    wkTv = wkT.ap().rearrange("p (b c) -> p b c", c=128)
    wvTv = wvT.ap().rearrange("p (b c) -> p b c", c=128)
    vbv = vbuf.ap().rearrange("p (b c) -> p b c", c=128)

    ev_proj_free = None
    ev_qx1_att = None
    ev_qx0_att = [None, None]
    ev_ratt_free = []
    ev_attT_free = {}
    ev_p8_free = {}
    ev_prevc = {}
    QKB = [0, 1, 2, 3]
    PVB = [4, 5]
    e_lastpv = None
    for pr in range(8):
        ev_w = []
        e_c = None
        for (wh, wTbv) in ((wq_h, wqTv), (wk_h, wkTv), (wv_h, wvTv)):
            for chh in range(2):
                e_d = P.dma("gpsimd", wst[:],
                            wh[pr * 128:(pr + 1) * 128,
                               chh * 512:(chh + 1) * 512],
                            ffn_bar + ev_xnT +
                            [e_c, ev_w[-1] if ev_w else ev_proj_free], "sD3")
                e_c = P.emit("vector", lambda q, chh=chh: q.tensor_copy(
                    wsb[:, chh * 512:(chh + 1) * 512], wst[:]), [e_d], "sV")
            for db in range(8):
                P.dmat("scalar", wTbv[:, db, :],
                       wsb[:, db * 128:(db + 1) * 128], [e_c], "sD2")
            ev_w.append(P.tip("sD2"))
        ev_wT = ev_w[-1]
        ev_kt = []
        for ch in range(8):
            pb = ch % 2
            e_mm = None
            for db in range(8):
                e_mm = P.emit("tensor", lambda q, ch=ch, db=db, pb=pb: q.matmul(
                    pb_ap(psum, pb), wkTv[:, db, :],
                    xnTv[:, db, ch * 512:(ch + 1) * 512],
                    start=(db == 0), stop=(db == 7)),
                    ([ev_wT, ev_proj_free] + ev_xnT +
                     [ev_psum_free.get(pb)]), "sPE" if db == 7 else None)
            e_ev = P.emit("vector", lambda q, ch=ch, pb=pb: q.tensor_copy(
                ktp[:, ch * 512:(ch + 1) * 512], pb_ap(psum, pb)), [e_mm], "sV")
            ev_psum_free[pb] = e_ev
            ev_kt.append(e_ev)
        ev_vt = []
        for tb in range(32):
            pb = 2 + tb % 2
            e_mm = None
            for db in range(8):
                e_mm = P.emit("tensor", lambda q, tb=tb, db=db, pb=pb: q.matmul(
                    pb_ap(psum, pb, 128), xnTv[:, db, tb * 128:(tb + 1) * 128],
                    wvTv[:, db, :], start=(db == 0), stop=(db == 7)),
                    [ev_wT, ev_proj_free, ev_psum_free.get(pb)],
                    "sPE" if db == 7 else None)
            e_ev = P.emit("vector", lambda q, tb=tb, pb=pb: q.tensor_copy(
                vbv[:, tb, :], pb_ap(psum, pb, 128)), [e_mm], "sV")
            ev_psum_free[pb] = e_ev
            ev_vt.append(e_ev)
        ev_proj = ev_kt + ev_vt + [ev_wT]

        for qp in range(8):
            W1, W2 = 128 * (qp + 1), 128 * (16 - qp)
            qbs = [qp, 15 - qp]
            ev_qt = []
            for qx in range(2):
                pb = PVB[qx]
                e_mm = None
                for db in range(8):
                    e_mm = P.emit("tensor", lambda q, qx=qx, db=db, pb=pb,
                                  i=qbs[qx]: q.matmul(
                        pb_ap(psum, pb, 128), wqTv[:, db, :],
                        xnTv[:, db, i * 128:(i + 1) * 128],
                        start=(db == 0), stop=(db == 7)),
                        ev_proj + [ev_psum_free.get(pb)],
                        "sPE" if db == 7 else None)
                e_ev = P.emit("vector", lambda q, qx=qx, pb=pb: q.tensor_scalar(
                    qtp[:, qx * 128:(qx + 1) * 128], pb_ap(psum, pb, 128),
                    0.125, None, ALU.mult),
                    [e_mm, ev_qt[-1] if ev_qt else None], "sV")
                ev_psum_free[pb] = e_ev
                ev_qt.append(e_ev)
            for h in range(2):
                qsl = slice(64 * h, 64 * h + 64)
                zos = [2048 * h, 4096]
                ros = [0, 2 * W1]
                Ws = [W1, W2]
                chains = [0, 1]
                ev_evac = {0: [], 1: []}
                for qx in chains:
                    i = qbs[qx]
                    nb = i + 1
                    zo = zos[qx]
                    pb = QKB[h * 2 + qx]
                    prev_evac = ev_psum_free.get(pb)
                    zwar = ev_qx0_att[h] if qx == 0 else ev_qx1_att
                    for rng in range(2):
                        kbase = rng * 16
                        ro = zo + rng * Ws[qx]
                        for cst in range(0, nb, 4):
                            cw = min(4, nb - cst) * 128
                            e_mm = P.emit(
                                "tensor",
                                lambda q, qx=qx, cst=cst, cw=cw, kbase=kbase,
                                pb=pb, qsl=qsl, h=h: q.matmul(
                                    pb_ap(psum, pb, cw),
                                    qtp.ap()[qsl, qx * 128:(qx + 1) * 128],
                                    ktp.ap()[qsl,
                                             (kbase + cst) * 128:
                                             (kbase + cst) * 128 + cw],
                                    start=True, stop=True,
                                    tile_position=(64 * h, 0)),
                                ev_qt + [prev_evac], "sPE")
                            last = cst + cw // 128 == nb
                            if not last:
                                e_ev = P.emit(
                                    "scalar",
                                    lambda q, ro=ro, cst=cst, cw=cw, pb=pb:
                                    q.activation(
                                        zb[:, ro + cst * 128:
                                           ro + cst * 128 + cw],
                                        pb_ap(psum, pb, cw), AF.Identity),
                                    [e_mm, zwar], "sA")
                            else:
                                e_ev0 = None
                                if cw > 128:
                                    e_ev0 = P.emit(
                                        "scalar",
                                        lambda q, ro=ro, cst=cst, cw=cw, pb=pb:
                                        q.activation(
                                            zb[:, ro + cst * 128:
                                               ro + cst * 128 + cw - 128],
                                            pb_ap(psum, pb, cw - 128),
                                            AF.Identity), [e_mm, zwar], "sA")
                                    ev_evac[qx].append(e_ev0)
                                mask = diagm if rng == 0 else peerm
                                e_ev = P.emit(
                                    "vector",
                                    lambda q, ro=ro, i=i, pb=pb, cw=cw,
                                    mask=mask: q.tensor_tensor(
                                        zb[:, ro + i * 128:ro + (i + 1) * 128],
                                        psum[pb].ap()[:, cw - 128:cw],
                                        mask[:], ALU.add),
                                    [e_mm, e_ev0, zwar], "sV")
                            ev_evac[qx].append(e_ev)
                            prev_evac = e_ev
                    ev_psum_free[pb] = prev_evac
                ev_upd = {}
                for qx in chains:
                    Wx2 = 2 * Ws[qx]
                    zo, ro = zos[qx], ros[qx]
                    e_p8 = P.emit("vector", lambda q, zo=zo, ro=ro, Wx2=Wx2:
                                  q.tensor_reduce(
                        p8[:, ro // 8:(ro + Wx2) // 8],
                        zb.ap()[:, zo:zo + Wx2]
                        .rearrange("p (n w) -> p n w", w=8), AX.X, ALU.max),
                        ev_evac[qx] + [ev_p8_free.get(qx)], "sV")
                    e_mx = P.emit("vector", lambda q, ro=ro, Wx2=Wx2, qx=qx:
                                  q.tensor_reduce(
                        small[:, 40 + qx:41 + qx],
                        p8[:, ro // 8:(ro + Wx2) // 8], AX.X, ALU.max),
                        [e_p8], "sV")
                    ev_upd[qx] = P.emit("vector", lambda q, qx=qx:
                                        q.tensor_scalar(
                        tfc[:, qx:qx + 1], small[:, 40 + qx:41 + qx],
                        -1.0, 1.0, ALU.mult, ALU.add), [e_mx], "sV")
                ev_c = dict(ev_prevc)
                for j in range(JP + JF):
                    pooled = j < JP
                    prev_c = dict(ev_c)
                    ev_f = {}
                    ev_c = {}
                    for qx in chains:
                        Wx2 = 2 * Ws[qx]
                        zo, ro = zos[qx], ros[qx]
                        if pooled:
                            srcap = p8.ap()[:, ro // 8:(ro + Wx2) // 8]
                            dstap = ratt.ap()[:, ro // 8:(ro + Wx2) // 8]
                        else:
                            srcap = zb.ap()[:, zo:zo + Wx2]
                            dstap = ratt.ap()[:, ro:ro + Wx2]
                        ev_f[qx] = P.emit(
                            "scalar", lambda q, srcap=srcap, dstap=dstap,
                            qx=qx: q.activation(
                                dstap, srcap, AF.Relu, bias=tfc[:, qx:qx + 1],
                                accum_out=tfc[:, 8 + qx:9 + qx]),
                            [ev_upd[qx], prev_c.get(qx), prev_c.get(1 - qx)]
                            + ev_ratt_free, "sA")
                        if pooled:
                            ev_p8_free[qx] = ev_f[qx]
                        ev_c[qx] = P.emit(
                            "vector", lambda q, dstap=dstap, qx=qx:
                            q.tensor_scalar(
                                dstap, dstap, 0.0, None, ALU.is_gt, ALU.add,
                                accum_out=tfc[:, 16 + qx:17 + qx]),
                            [ev_f[qx]], "sV")
                    e_rc = P.emit("vector", lambda q: q.reciprocal(
                        tfc[:, 24:26], tfc[:, 16:18]),
                        [ev_c[0], ev_c[1]], "sV")
                    e_d = P.emit("vector", lambda q: q.scalar_tensor_tensor(
                        small[:, 48:50], tfc[:, 8:10], -1.0, tfc[:, 24:26],
                        ALU.add, ALU.mult), [e_rc], "sV")
                    e_u = P.emit("vector", lambda q: q.tensor_tensor(
                        tfc[:, 0:2], tfc[:, 0:2], small[:, 48:50],
                        ALU.subtract), [e_d], "sV")
                    ev_upd = {0: e_u, 1: e_u}
                ev_prevc = dict(ev_c)
                for qx in chains:
                    i = qbs[qx]
                    Wx2 = 2 * Ws[qx]
                    zo, ro = zos[qx], ros[qx]
                    e_gp = P.emit("gpsimd", lambda q, zo=zo, ro=ro, Wx2=Wx2,
                                  qx=qx: q.tensor_scalar(
                        ratt.ap()[:, ro:ro + Wx2], zb.ap()[:, zo:zo + Wx2],
                        tfc[:, qx:qx + 1], 0.0, ALU.add, ALU.max),
                        [ev_upd[qx], ev_c[qx]], "sG")
                    if qx == 0:
                        ev_qx0_att[h] = e_gp
                    else:
                        ev_qx1_att = e_gp
                    for jb in range(2 * (i + 1)):
                        ring = "sync" if jb % 2 == 0 else "scalar"
                        P.dmat(ring,
                               attT[:, ro + jb * 128:ro + (jb + 1) * 128],
                               ratt[:, ro + jb * 128:ro + (jb + 1) * 128],
                               [e_gp, ev_attT_free.get(qx)],
                               "sD" if jb % 2 == 0 else "sD2")
                    e_tp = [P.tip("sD"), P.tip("sD2")]
                    ev_ratt_free = e_tp
                    pb = PVB[qx]
                    tbs = list(range(0, i + 1)) + list(range(16, 16 + i + 1))
                    e_mm = None
                    for jb, tb in enumerate(tbs):
                        e_mm = P.emit("tensor", lambda q, tb=tb, jb=jb, ro=ro,
                                      h=h, pb=pb: q.matmul(
                            psum[pb].ap()[64 * h:64 * h + 64, 0:128],
                            vbv[:, tb, 64 * h:64 * h + 64],
                            attT[:, ro + jb * 128:ro + (jb + 1) * 128],
                            start=(jb == 0), stop=(jb == len(tbs) - 1)),
                            e_tp + [ev_psum_free.get((pb, h)),
                                    ev_psum_free.get(pb)],
                            "sPE" if jb == len(tbs) - 1 else None)
                    ev_attT_free[qx] = e_mm
                    e_lastpv = e_mm
                    if h == 1:
                        qb = qbs[qx]
                        e_ev = P.emit("vector", lambda q, qb=qb, pr=pr, pb=pb:
                                      q.tensor_copy(
                            anv[:, pr, qb * 128:(qb + 1) * 128],
                            psum[pb].ap()[:, 0:128]), [e_mm], "sV")
                        ev_psum_free[(pb, 0)] = e_ev
                        ev_psum_free[(pb, 1)] = e_ev
                        ev_psum_free[pb] = e_ev
        ev_proj_free = e_lastpv

    att_bar = [P.tip("sV"), P.tip("sA"), P.tip("sPE"), P.tip("sG"),
               P.tip("sD3"), P.tip("sD"), P.tip("sD2")]
    ph5.close()
    ph6 = ExitStack()
    woT = sb("woT", 128, 8 * 1024, BF16, ph6, side="right")
    wst6 = sb("wst6", 128, 512, F32, ph6, side="right")
    wsb6 = sb("wsb6", 128, 1024, BF16, ph6, side="right")
    woTv = woT.ap().rearrange("p (b c) -> p b c", c=1024)
    ev_wo = []
    e_c = None
    for sl in range(8):
        for chh in range(2):
            e_d = P.dma("gpsimd", wst6[:],
                        wo_h[sl * 128:(sl + 1) * 128,
                             chh * 512:(chh + 1) * 512],
                        att_bar + [e_c, ev_wo[-1] if ev_wo else None], "sD3")
            e_c = P.emit("vector", lambda q, chh=chh: q.tensor_copy(
                wsb6[:, chh * 512:(chh + 1) * 512], wst6[:]), [e_d], "sV")
        for db in range(8):
            P.dmat("scalar", woTv[:, db, sl * 128:(sl + 1) * 128],
                   wsb6[:, db * 128:(db + 1) * 128], [e_c], "sD2")
        ev_wo.append(P.tip("sD2"))
    ev_woT = ev_wo[-1]
    xres = [sb(f"xres{i}", 128, 1024, F32, ph6, side="right") for i in range(2)]
    osb = [sb(f"osb{i}", 128, 1024, F32, ph6, side="right") for i in range(2)]
    ev_osb_free = [None, None]
    ev_xres = [None, None]
    ev_attn_all = P.tip("sPE")
    for qb in range(16):
        bi = qb % 2
        ev_xres[bi] = P.dma("sync", xres[bi][:], x_h[qb * 128:(qb + 1) * 128, :],
                            att_bar + [ev_osb_free[bi]], "sD")
        for dh in range(2):
            pb = 6 + dh
            e_mm = None
            for eb in range(8):
                e_mm = P.emit("tensor", lambda q, eb=eb, qb=qb, dh=dh, pb=pb:
                              q.matmul(
                    psum[pb].ap()[:, :], anv[:, eb, qb * 128:(qb + 1) * 128],
                    woTv[:, eb, dh * 512:(dh + 1) * 512],
                    start=(eb == 0), stop=(eb == 7)),
                    [ev_attn_all, ev_woT, ev_psum_free.get(pb)],
                    "sPE" if eb == 7 else None)
            e_m = P.emit("vector", lambda q, qb=qb, dh=dh, bi=bi, pb=pb:
                         q.tensor_tensor(
                osb[bi][:, dh * 512:(dh + 1) * 512], psum[pb].ap()[:, :],
                gv[:, qb, dh * 512:(dh + 1) * 512], ALU.mult),
                [e_mm, ev_silu[qb], ev_osb_free[bi]], "sV")
            e_a = P.emit("vector", lambda q, qb=qb, dh=dh, bi=bi:
                         q.tensor_tensor(
                osb[bi][:, dh * 512:(dh + 1) * 512],
                osb[bi][:, dh * 512:(dh + 1) * 512],
                xres[bi][:, dh * 512:(dh + 1) * 512], ALU.add),
                [e_m, ev_xres[bi]], "sV")
            ev_psum_free[pb] = e_a
        e_st = P.dma("sync", out_h[qb * 128:(qb + 1) * 128, :], osb[bi][:],
                     [P.tip("sV")], "sO")
        ev_osb_free[bi] = e_st
    P.emit("gpsimd", lambda q: q.nop(), [P.tip("sO")], None)

    ph6.close()
    P.realize(blk, sems)
    st.close()
    return nc


def pb_ap(psum, pb, w=512):
    return psum[pb].ap()[:, 0:w]


_cache = {}


def kernel(L0_attn_x, L0_attn_Wq, L0_attn_Wk, L0_attn_Wv, L0_attn_Wo,
           L0_ffn_W1, L0_ffn_W2, L0_ffn_Wg, L0_ffn_bg):
    bf = ml_dtypes.bfloat16
    x = np.ascontiguousarray(np.asarray(L0_attn_x, np.float32))
    consts = {}
    p = np.arange(128)[:, None]
    c = np.arange(128)[None, :]
    consts["diagm"] = np.where(c <= p, 0.0, NEGC).astype(np.float32)
    consts["u128"] = (np.arange(128)[:, None] < np.arange(128)[None, :]) \
        .astype(bf)
    consts["u16"] = (np.arange(16)[:, None] < np.arange(16)[None, :]) \
        .astype(np.float32)
    bselm = np.zeros((128, 16, 16), bf)
    for tb in range(16):
        bselm[:, tb, tb] = 1
    consts["bsel"] = bselm.reshape(128, 256)
    consts["ones1"] = np.ones((1, 128), bf)
    consts["ones1f"] = np.ones((1, 128), np.float32)
    consts["onesrow"] = np.ones((1, 512), bf)
    consts["ecrow"] = (np.arange(E_) * float(CAP)).astype(bf)[None, :]
    consts["iota512"] = np.broadcast_to(
        np.arange(512, dtype=np.float32)[None, :], (128, 512)).copy()
    consts["iotac"] = (np.arange(128)[:, None]
                       + 128.0 * np.arange(32)[None, :]).astype(np.float32)
    consts["ident"] = np.eye(128, dtype=np.float32)

    nc = build_program()
    in_maps = []
    for core in range(8):
        b, par = core // 2, core % 2
        xb = x[b].reshape(32, 128, D_)
        own = xb[par::2].reshape(TOK, D_)
        peer = xb[1 - par::2].reshape(TOK, D_)
        im = dict(consts)
        im["x"] = np.ascontiguousarray(np.concatenate([own, peer], 0))
        im["wq"] = np.asarray(L0_attn_Wq, np.float32)
        im["wk"] = np.asarray(L0_attn_Wk, np.float32)
        im["wv"] = np.asarray(L0_attn_Wv, np.float32)
        im["wo"] = np.asarray(L0_attn_Wo, np.float32)
        im["w1"] = np.asarray(L0_ffn_W1, np.float32)
        im["w2"] = np.asarray(L0_ffn_W2, np.float32)
        im["wg"] = np.asarray(L0_ffn_Wg, np.float32)
        im["bg"] = np.asarray(L0_ffn_bg, np.float32)[None, :]
        im["peerm"] = np.full((128, 128), NEGC if par == 0 else 0.0,
                              np.float32)
        in_maps.append(im)
    res = run_bass_kernel_spmd(nc, in_maps, core_ids=list(range(8)))
    out = np.empty((B_, L_, D_), np.float32)
    for core in range(8):
        b, par = core // 2, core % 2
        oc = np.asarray(res.results[core]["out"]).reshape(16, 128, D_)
        out[b].reshape(32, 128, D_)[par::2] = oc
    return out


# version 10
# speedup vs baseline: 1.6005x; 1.6005x over previous
import numpy as np
import ml_dtypes
import concourse.bass as bass
import concourse.mybir as mybir
from concourse.bass_utils import run_bass_kernel_spmd
from contextlib import ExitStack

F32 = mybir.dt.float32
BF16 = mybir.dt.bfloat16
AF = mybir.ActivationFunctionType
ALU = mybir.AluOpType
AX = mybir.AxisListType

B_, L_, D_ = 4, 4096, 1024
E_, HID_ = 8, 4096
TOK = 2048
CAP = 512
NEGC = -1.0e9
JP = 8
JF = 2
EPS = 1e-5
GC1 = 0.044715
GC2 = 1.5957691216057308

ENGINES = ("sync", "tensor", "vector", "scalar", "gpsimd")


class Prog:
    def __init__(self):
        self.ops = {e: [] for e in ENGINES}
        self.cnt = {s: 0 for s in ("sD", "sD2", "sD3", "sPE", "sV", "sA",
                                   "sG", "sO")}

    def emit(self, eng, fn, waits=(), inc=None, amt=1):
        waits = [w for w in waits if w is not None]
        self.ops[eng].append((waits, fn, inc, amt))
        if inc is not None:
            self.cnt[inc] += amt
            return (inc, self.cnt[inc])
        return None

    def tip(self, sem):
        return (sem, self.cnt[sem])

    def dma(self, eng, out, in_, waits=(), sem="sD", **kw):
        return self.emit(
            eng, lambda q, o=out, i=in_, kw=kw: q.dma_start(out=o, in_=i, **kw),
            waits, sem, 16)

    def dmat(self, eng, out, in_, waits=(), sem="sD"):
        return self.emit(
            eng, lambda q, o=out, i=in_: q.dma_start_transpose(o, i),
            waits, sem, 16)

    def realize(self, block, sems):
        for eng in ENGINES:
            ops = self.ops[eng]
            if not ops:
                continue

            def body(q, ops=ops):
                seen = {}
                for waits, fn, inc, amt in ops:
                    for (s, v) in waits:
                        if seen.get(s, 0) < v:
                            q.wait_ge(sems[s], v)
                            seen[s] = v
                    inst = fn(q)
                    if inc is not None:
                        inst.then_inc(sems[inc], amt)
            getattr(block, eng)(body)


def build_program():
    nc = bass.Bass("TRN2", target_bir_lowering=False, debug=False)
    dt = nc.dram_tensor
    x_h = dt("x", [L_, D_], F32, kind="ExternalInput").ap()
    wq_h = dt("wq", [D_, D_], F32, kind="ExternalInput").ap()
    wk_h = dt("wk", [D_, D_], F32, kind="ExternalInput").ap()
    wv_h = dt("wv", [D_, D_], F32, kind="ExternalInput").ap()
    wo_h = dt("wo", [D_, D_], F32, kind="ExternalInput").ap()
    w1_h = dt("w1", [E_, D_, HID_], F32, kind="ExternalInput").ap()
    w2_h = dt("w2", [E_, HID_, D_], F32, kind="ExternalInput").ap()
    wg_h = dt("wg", [D_, E_], F32, kind="ExternalInput").ap()
    bg_h = dt("bg", [1, E_], F32, kind="ExternalInput").ap()
    diagm_h = dt("diagm", [128, 128], F32, kind="ExternalInput").ap()
    diagmb_h = dt("diagmb", [128, 128], BF16, kind="ExternalInput").ap()
    peermb_h = dt("peermb", [128, 128], BF16, kind="ExternalInput").ap()
    identb_h = dt("identb", [128, 128], BF16, kind="ExternalInput").ap()
    peerm_h = dt("peerm", [128, 128], F32, kind="ExternalInput").ap()
    u128_h = dt("u128", [128, 128], BF16, kind="ExternalInput").ap()
    u16_h = dt("u16", [16, 16], F32, kind="ExternalInput").ap()
    bsel_h = dt("bsel", [128, 16 * 16], BF16, kind="ExternalInput").ap()
    ones1_h = dt("ones1", [1, 128], BF16, kind="ExternalInput").ap()
    ones1f_h = dt("ones1f", [1, 128], F32, kind="ExternalInput").ap()
    onesrow_h = dt("onesrow", [1, 512], BF16, kind="ExternalInput").ap()
    ecrow_h = dt("ecrow", [1, E_], BF16, kind="ExternalInput").ap()
    iota512_h = dt("iota512", [128, 512], F32, kind="ExternalInput").ap()
    iotac_h = dt("iotac", [128, 32], F32, kind="ExternalInput").ap()
    ident_h = dt("ident", [128, 128], F32, kind="ExternalInput").ap()
    out_h = dt("out", [TOK, D_], F32, kind="ExternalOutput").ap()

    P = Prog()
    st = ExitStack()

    def sb(name, p, f, dtp, stk=None, side="left"):
        return (stk or st).enter_context(
            nc.sbuf_tensor("s_" + name, [p, f], dtp, side=side))

    phx = ExitStack()
    xn = sb("xn", 128, 32 * 1024, BF16, phx, side="right")
    diagm = sb("diagm", 128, 128, F32)
    diagmb = sb("diagmb", 128, 128, BF16)
    peermb = sb("peermb", 128, 128, BF16)
    identb = sb("identb", 128, 128, BF16)
    peerm = sb("peerm", 128, 128, F32)
    ident = sb("ident", 128, 128, F32)
    pos = sb("pos", 128, 16, F32)
    tfc = sb("tfc", 128, 32, F32)
    small = sb("small", 128, 64, F32)

    xv = xn.ap().rearrange("p (b c) -> p b c", c=1024)

    psum = [st.enter_context(nc.psum_tensor(f"pp{i}", [128, 512], F32))
            for i in range(8)]

    semst = st.enter_context(ExitStack())
    sems = {s: semst.enter_context(nc.semaphore("m_" + s))
            for s in ("sD", "sD2", "sD3", "sPE", "sV", "sA", "sG", "sO")}
    blk = st.enter_context(nc.Block())

    ph123 = ExitStack()
    ones1 = sb("ones1", 1, 128, BF16, ph123, side="right")
    ones1f = sb("ones1f", 1, 128, F32, ph123, side="right")
    onesrow = sb("onesrow", 1, 512, BF16, ph123, side="right")
    ecrow = sb("ecrow", 1, E_, BF16, ph123, side="right")
    u128 = sb("u128", 128, 128, BF16, ph123, side="right")
    u16 = sb("u16", 16, 16, F32, ph123, side="right")
    bsel = sb("bsel", 128, 16 * 16, BF16, ph123, side="right")
    iota512 = sb("iota512", 128, 512, F32, ph123, side="right")
    iotac = sb("iotac", 128, 32, F32, ph123, side="right")
    pos_bc = sb("pos_bc", 128, TOK, F32, ph123, side="right")
    for dst, src in [(diagm, diagm_h), (diagmb, diagmb_h),
                     (peermb, peermb_h), (identb, identb_h),
                     (peerm, peerm_h), (ones1, ones1_h),
                     (ones1f, ones1f_h), (onesrow, onesrow_h),
                     (ecrow, ecrow_h), (ident, ident_h)]:
        P.dma("gpsimd", dst[:], src, sem="sD3")
    ev_consts = P.tip("sD3")
    bselv = bsel.ap().rearrange("p (b c) -> p b c", c=16)
    for dst, src in [(u128, u128_h), (bsel, bsel_h), (iota512, iota512_h),
                     (iotac, iotac_h)]:
        P.dma("gpsimd", dst[:], src, sem="sD3")
    P.dma("gpsimd", u16.ap(), u16_h, sem="sD3")
    ev_consts2 = P.tip("sD3")

    ph12 = ExitStack()
    xt = [sb(f"xt{i}", 128, 1024, F32, ph12, side="right") for i in range(2)]
    lnsq = sb("lnsq", 128, 1024, BF16, ph12, side="right")
    xnf = sb("xnf", 128, 1024, F32, ph12, side="right")
    xlo = sb("xlo", 128, 1024, BF16, ph12, side="right")
    xnqT = sb("xnqT", 128, 8 * TOK, BF16, ph12, side="right")
    xloT = sb("xloT", 128, 8 * TOK, BF16, ph12, side="right")
    wgs = sb("wgs", 128, 8 * E_, F32, ph12, side="right")
    wghi = sb("wghi", 128, 8 * E_, BF16, ph12, side="right")
    wglo = sb("wglo", 128, 8 * E_, BF16, ph12, side="right")
    wglof = sb("wglof", 128, 8 * E_, F32, ph12, side="right")
    bgs = sb("bgs", 1, 4 * E_, F32, ph12, side="right")
    bghi = sb("bghi", 1, E_, BF16, ph12, side="right")
    bglo = sb("bglo", 1, E_, BF16, ph12, side="right")
    lgT = sb("lgT", 8, TOK, F32, ph12, side="right")
    lg = sb("lg", 128, 16 * 8, F32, ph12, side="right")
    oh = sb("oh", 128, 16 * 8, F32, ph12, side="right")
    ohb = sb("ohb", 128, 16 * 8, BF16, ph12, side="right")
    rk = sb("rk", 128, 16 * 8, F32, ph12, side="right")
    cnt_sb = sb("cnt_sb", 16, E_, F32, ph12, side="right")
    off_sb = sb("off_sb", 16, E_, F32, ph12, side="right")
    offrow = sb("offrow", 1, 16 * E_, F32, ph12, side="right")

    posT = sb("posT", 16, 128, F32, ph12, side="right")
    posrow = lgT

    e_wg = P.dma("gpsimd", wgs.ap().rearrange("p (b c) -> p b c", c=E_),
                 wg_h.rearrange("(b p) c -> p b c", p=128), sem="sD3")
    e_bg = P.dma("gpsimd", bgs[0:1, 0:E_], bg_h, sem="sD3")
    e_wghi = P.emit("vector", lambda q: q.tensor_copy(wghi[:], wgs[:]),
                    [e_wg], "sV")
    e2 = P.emit("vector", lambda q: q.tensor_tensor(
        wglof[:], wgs[:], wghi[:], ALU.subtract), [e_wghi], "sV")
    e_wglo = P.emit("vector", lambda q: q.tensor_copy(wglo[:], wglof[:]),
                    [e2], "sV")
    e_bghi = P.emit("vector", lambda q: q.tensor_copy(
        bghi[:], bgs[0:1, 0:E_]), [e_bg], "sV")
    e2 = P.emit("vector", lambda q: q.tensor_tensor(
        bgs[0:1, E_:2 * E_], bgs[0:1, 0:E_], bghi[:], ALU.subtract),
        [e_bghi], "sV")
    e_bglo = P.emit("vector", lambda q: q.tensor_copy(
        bglo[:], bgs[0:1, E_:2 * E_]), [e2], "sV")

    xnqTv = xnqT.ap().rearrange("p (b c) -> p b c", c=TOK)
    xloTv = xloT.ap().rearrange("p (b c) -> p b c", c=TOK)
    prev_xt = [None, None]
    prev_lo_dmat = None
    ev_xn = [None] * 32
    for t in range(32):
        bi = t % 2
        e_x = P.dma("sync", xt[bi][:], x_h[t * 128:(t + 1) * 128, :],
                    [prev_xt[bi]], "sD")
        c0, c1 = 2 * bi, 2 * bi + 1
        e_s1 = P.emit("vector", lambda q, b=bi, c=c0: q.reduce_sum(
            small[:, c:c + 1], xt[b][:], AX.X), [e_x], "sV")
        e_s2 = P.emit("scalar", lambda q, b=bi, c=c1: q.activation(
            lnsq[:], xt[b][:], AF.Square, accum_out=small[:, c:c + 1]),
            [e_x], "sA")
        e_m = P.emit("vector", lambda q, c=c0: q.tensor_scalar(
            small[:, c + 4:c + 5], small[:, c:c + 1], 1.0 / D_, None,
            ALU.mult), [e_s1], "sV")
        e_ms = P.emit("scalar", lambda q, c=c0: q.activation(
            small[:, c + 8:c + 9], small[:, c + 4:c + 5], AF.Square),
            [e_m], "sA")
        e_v = P.emit("vector", lambda q, c0=c0, c1=c1: q.tensor_scalar(
            small[:, c1 + 8:c1 + 9], small[:, c1:c1 + 1], 1.0 / D_,
            small[:, c0 + 8:c0 + 9], ALU.mult, ALU.subtract),
            [e_s2, e_ms], "sV")
        e_ve = P.emit("vector", lambda q, c1=c1: q.tensor_scalar(
            small[:, c1 + 8:c1 + 9], small[:, c1 + 8:c1 + 9], EPS, None,
            ALU.add), [e_v], "sV")
        e_sq = P.emit("scalar", lambda q, c1=c1: q.activation(
            small[:, c1 + 12:c1 + 13], small[:, c1 + 8:c1 + 9], AF.Sqrt),
            [e_ve], "sA")
        e_rs = P.emit("vector", lambda q, c1=c1: q.reciprocal(
            small[:, c1 + 16:c1 + 17], small[:, c1 + 12:c1 + 13]),
            [e_sq], "sV")
        ev_xn[t] = P.emit("vector", lambda q, b=bi, t=t, c0=c0, c1=c1:
                          q.tensor_scalar(
            xv[:, t, :], xt[b][:], small[:, c0 + 4:c0 + 5],
            small[:, c1 + 16:c1 + 17], ALU.subtract, ALU.mult),
            [e_rs], "sV")
        prev_xt[bi] = ev_xn[t]
        if t < 16:
            e_f = P.emit("vector", lambda q, b=bi, c0=c0, c1=c1:
                         q.tensor_scalar(
                xnf[:], xt[b][:], small[:, c0 + 4:c0 + 5],
                small[:, c1 + 16:c1 + 17], ALU.subtract, ALU.mult),
                [e_rs], "sV")
            e_lo = P.emit("vector", lambda q, t=t: q.tensor_tensor(
                xlo[:], xnf[:], xv[:, t, :], ALU.subtract),
                [e_f, ev_xn[t], prev_lo_dmat], "sV")
            prev_xt[bi] = e_lo
            for db in range(8):
                P.dmat("sync", xnqTv[:, db, t * 128:(t + 1) * 128],
                       xv[:, t, db * 128:(db + 1) * 128], [ev_xn[t]], "sD")
                P.dmat("sync", xloTv[:, db, t * 128:(t + 1) * 128],
                       xlo[:, db * 128:(db + 1) * 128], [e_lo], "sD")
            prev_lo_dmat = P.tip("sD")
    ev_xT_all = P.tip("sD")

    wghv = wghi.ap().rearrange("p (b c) -> p b c", c=E_)
    wglv = wglo.ap().rearrange("p (b c) -> p b c", c=E_)
    lg_ev = []
    for ch in range(4):
        tl = ch * 512
        for db in range(8):
            P.emit("tensor", lambda q, db=db, tl=tl, f=(db == 0): q.matmul(
                psum[0].ap()[0:8, :], wghv[:, db, :],
                xnqTv[:, db, tl:tl + 512], start=f, stop=False),
                [ev_xT_all, e_wghi, lg_ev[-1] if lg_ev else None], None)
            P.emit("tensor", lambda q, db=db, tl=tl: q.matmul(
                psum[0].ap()[0:8, :], wglv[:, db, :],
                xnqTv[:, db, tl:tl + 512], start=False, stop=False),
                [e_wglo], None)
            P.emit("tensor", lambda q, db=db, tl=tl: q.matmul(
                psum[0].ap()[0:8, :], wghv[:, db, :],
                xloTv[:, db, tl:tl + 512], start=False, stop=False), [], None)
        P.emit("tensor", lambda q: q.matmul(
            psum[0].ap()[0:8, :], bghi[:], onesrow[:], start=False,
            stop=False), [e_bghi, ev_consts], None)
        e_mm = P.emit("tensor", lambda q: q.matmul(
            psum[0].ap()[0:8, :], bglo[:], onesrow[:], start=False,
            stop=True), [e_bglo], "sPE")
        lg_ev.append(P.emit("vector", lambda q, tl=tl: q.tensor_copy(
            lgT[:, tl:tl + 512], psum[0].ap()[0:8, :]), [e_mm], "sV"))

    lgv = lg.ap().rearrange("p (b c) -> p b c", c=8)
    ohv = oh.ap().rearrange("p (b c) -> p b c", c=8)
    ohbv = ohb.ap().rearrange("p (b c) -> p b c", c=8)
    rkv = rk.ap().rearrange("p (b c) -> p b c", c=8)
    ev_oh = [None] * 16
    ev_ohb = [None] * 16
    prev_tp = None
    for b in range(16):
        e_tp = P.emit("tensor", lambda q, b=b: q.transpose(
            psum[1].ap()[:, 0:8], lgT[:, b * 128:(b + 1) * 128],
            ident.ap()[0:8, 0:8]),
            [lg_ev[b // 4], ev_consts, prev_tp], "sPE")
        e_cp = P.emit("vector", lambda q, b=b: q.tensor_copy(
            lgv[:, b, :], psum[1].ap()[:, 0:8]), [e_tp], "sV")
        prev_tp = e_cp
        sc = 32 + (b % 8)
        e_mx = P.emit("vector", lambda q, b=b, sc=sc: q.tensor_reduce(
            small[:, sc:sc + 1], lgv[:, b, :], AX.X, ALU.max), [e_cp], "sV")
        ev_oh[b] = P.emit("vector", lambda q, b=b, sc=sc: q.tensor_scalar(
            ohv[:, b, :], lgv[:, b, :], small[:, sc:sc + 1], None, ALU.is_ge),
            [e_mx], "sV")
        ev_ohb[b] = P.emit("vector", lambda q, b=b: q.tensor_copy(
            ohbv[:, b, :], ohv[:, b, :]), [ev_oh[b]], "sV")
    for b in range(16):
        P.emit("tensor", lambda q, b=b: q.matmul(
            psum[2].ap()[0:16, 0:8], bselv[:, b, :], ohbv[:, b, :],
            start=(b == 0), stop=(b == 15)),
            [ev_ohb[b], ev_consts2], "sPE" if b == 15 else None)
    e_cnt = P.emit("vector", lambda q: q.tensor_copy(
        cnt_sb[:], psum[2].ap()[0:16, 0:8]), [P.tip("sPE")], "sV")
    e_omm = P.emit("tensor", lambda q: q.matmul(
        psum[3].ap()[0:16, 0:8], u16[:], cnt_sb[:], start=True, stop=True),
        [e_cnt], "sPE")
    e_off = P.emit("vector", lambda q: q.tensor_copy(
        off_sb[:], psum[3].ap()[0:16, 0:8]), [e_omm], "sV")
    e_orow = P.dma("sync", offrow.ap().rearrange("a (b c) -> a b c", c=E_),
                   off_sb.ap(), [e_off], "sD")

    ev_pos = [None] * 16
    for b in range(16):
        pp = psum[4 + b % 2]
        P.emit("tensor", lambda q, b=b, pp=pp: q.matmul(
            pp.ap()[:, 0:8], u128[:], ohbv[:, b, :], start=True, stop=False),
            [ev_ohb[b], ev_consts2, ev_pos[b - 2] if b >= 2 else None], None)
        P.emit("tensor", lambda q, b=b, pp=pp: q.matmul(
            pp.ap()[:, 0:8], ones1f[:], offrow[:, b * 8:(b + 1) * 8],
            start=False, stop=False), [e_orow], None)
        e3 = P.emit("tensor", lambda q, pp=pp: q.matmul(
            pp.ap()[:, 0:8], ones1[:], ecrow[:], start=False, stop=True),
            [ev_consts], "sPE")
        e_rk = P.emit("vector", lambda q, b=b, pp=pp: q.tensor_copy(
            rkv[:, b, :], pp.ap()[:, 0:8]), [e3], "sV")
        e_pr = P.emit("vector", lambda q, b=b: q.tensor_tensor(
            rkv[:, b, :], rkv[:, b, :], ohv[:, b, :], ALU.mult),
            [e_rk, ev_oh[b]], "sV")
        ev_pos[b] = P.emit("vector", lambda q, b=b: q.reduce_sum(
            pos[:, b:b + 1], rkv[:, b, :], AX.X), [e_pr], "sV")
    e_pt = P.emit("tensor", lambda q: q.transpose(
        psum[5].ap()[0:16, 0:128], pos[:], ident[:]),
        ev_pos + [P.tip("sPE")], "sPE")
    e_ptc = P.emit("vector", lambda q: q.tensor_copy(
        posT[:], psum[5].ap()[0:16, 0:128]), [e_pt], "sV")
    e_prw = P.dma("sync",
                  posrow.ap()[0:1, :].rearrange("a (b c) -> a b c", c=128),
                  posT.ap(), [e_ptc], "sD")
    ev_pbc = []
    for ch in range(4):
        e_b = P.emit("tensor", lambda q, ch=ch: q.matmul(
            psum[6].ap()[:, :], ones1f[:],
            posrow.ap()[0:1, ch * 512:(ch + 1) * 512],
            start=True, stop=True),
            [e_prw, ev_consts, ev_pbc[-1] if ev_pbc else None], "sPE")
        ev_pbc.append(P.emit("vector", lambda q, ch=ch: q.tensor_copy(
            pos_bc[:, ch * 512:(ch + 1) * 512], psum[6].ap()[:, :]),
            [e_b], "sV"))
    ev_posbc = ev_pbc[-1]

    ph12.close()
    g = sb("g", 128, 16 * 1024, BF16)
    gv = g.ap().rearrange("p (b c) -> p b c", c=1024)
    ph3 = ExitStack()
    m1s = sb("m1s", 128, 512, BF16, ph3, side="right")
    posadj = sb("posadj", 128, 16, F32, ph3, side="right")
    xns = sb("xns", 128, 8 * 512, BF16, ph3, side="right")
    w1st = [sb(f"w1st{i}", 128, 8 * 256, F32, ph3, side="right") for i in range(2)]
    w1bf = [sb(f"w1bf{i}", 128, 8 * 256, BF16, ph3, side="right") for i in range(2)]
    ghf = sb("ghf", 128, 32 * 512, BF16, ph3, side="right")
    gx2 = sb("gx2", 128, 1024, F32, ph3, side="right")
    gxa = sb("gxa", 128, 1024, F32, ph3, side="right")
    gxs = gx2
    w2st = [sb(f"w2st{i}", 128, 1024, F32, ph3, side="right") for i in range(2)]
    w2bf = [sb(f"w2bf{i}", 128, 1024, BF16, ph3, side="right") for i in range(2)]
    ysb = sb("ysb", 128, 4 * 1024, BF16, ph3, side="right")
    m1t = [sb(f"m1t{i}", 128, 128, BF16, ph3, side="right") for i in range(4)]
    xnsv = xns.ap().rearrange("p (b c) -> p b c", c=512)
    w1sv = [w.ap().rearrange("p (b c) -> p b c", c=256) for w in w1st]
    w1bv = [w.ap().rearrange("p (b c) -> p b c", c=256) for w in w1bf]
    ghv = ghf.ap().rearrange("p (b c) -> p b c", c=512)
    ysv = ysb.ap().rearrange("p (b c) -> p b c", c=1024)

    bar12 = [P.tip("sV"), P.tip("sA"), P.tip("sPE"), P.tip("sD"),
             P.tip("sD3"), P.tip("sD2")]
    e_gz = P.emit("vector", lambda q: q.memset(g[:], 0.0), [], "sV")
    ev_g = [e_gz] * 16
    ev_m1_free = bar12[0]
    ev_xns_free = None
    ev_w1free = list(bar12[:2])
    ev_w2free = list(bar12[:2])
    ev_gh_w2done = None
    ev_y_used = None
    ev_gelu_free = [None, None]
    ev_psum_free = {i: None for i in range(8)}
    ev_psum_free[6] = ev_posbc
    for e in range(E_):
        ev_sort = []
        for tb in range(16):
            e_a = P.emit("vector", lambda q, tb=tb, e=e: q.tensor_scalar(
                posadj[:, tb:tb + 1], pos[:, tb:tb + 1], -512.0 * e, None,
                ALU.add),
                (bar12 if e == 0 else []) + [ev_pos[tb] if e == 0 else None],
                "sV")
            e_m1 = P.emit("vector", lambda q, tb=tb: q.tensor_scalar(
                m1s[:], iota512[:], posadj[:, tb:tb + 1], None,
                ALU.is_equal), [e_a, ev_consts2, ev_m1_free], "sV")
            e_mm = None
            for db in range(8):
                e_mm = P.emit("tensor", lambda q, db=db, tb=tb: q.matmul(
                    psum[db].ap()[:, :], xv[:, tb, db * 128:(db + 1) * 128],
                    m1s[:], start=(tb == 0), stop=(tb == 15),
                    skip_group_check=True),
                    [e_m1, ev_xn[tb] if e == 0 else None, ev_xns_free,
                     ev_psum_free[db]], "sPE" if db == 7 else None)
            ev_m1_free = e_mm
        e_srt = P.tip("sPE")
        for db in range(8):
            ev_sort.append(P.emit("vector", lambda q, db=db: q.tensor_copy(
                xnsv[:, db, :], psum[db].ap()[:, :]), [e_srt], "sV"))
            ev_psum_free[db] = ev_sort[-1]

        ev_gelu_done = []
        for sl in range(16):
            sbi = sl % 2
            h0 = sl * 256
            e_w1d = P.dma("gpsimd", w1sv[sbi],
                          w1_h[e, :, h0:h0 + 256]
                          .rearrange("(b p) c -> p b c", p=128),
                          (bar12 if (e == 0 and sl < 2) else [])
                          + [ev_w1free[sbi]], "sD3")
            e_w1c = P.emit("vector", lambda q, sbi=sbi: q.tensor_copy(
                w1bf[sbi][:], w1st[sbi][:]), [e_w1d], "sV")
            for hb in range(2):
                hloc = sl * 2 + hb
                pb = hloc % 2
                e_mm = None
                for db in range(8):
                    e_mm = P.emit("tensor",
                                  lambda q, db=db, sbi=sbi, hb=hb, pb=pb:
                                  q.matmul(
                        psum[pb].ap()[:, :],
                        w1bv[sbi][:, db, hb * 128:(hb + 1) * 128],
                        xnsv[:, db, :], start=(db == 0), stop=(db == 7)),
                        [e_w1c, ev_sort[db], ev_psum_free[pb],
                         ev_gh_w2done], "sPE" if db == 7 else None)
                par = hloc % 2
                cs = slice(par * 512, par * 512 + 512)
                e1 = P.emit("scalar", lambda q, pb=pb, cs=cs: q.activation(
                    gx2[:, cs], psum[pb].ap()[:, :], AF.Square),
                    [e_mm, ev_gelu_free[par]], "sA")
                e2 = P.emit("vector", lambda q, cs=cs: q.tensor_scalar(
                    gxa[:, cs], gx2[:, cs], GC1, 1.0, ALU.mult, ALU.add),
                    [e1], "sV")
                e3 = P.emit("vector", lambda q, pb=pb, cs=cs: q.tensor_tensor(
                    gxs[:, cs], gxa[:, cs], psum[pb].ap()[:, :], ALU.mult),
                    [e2], "sV")
                e4 = P.emit("scalar", lambda q, cs=cs: q.activation(
                    gxa[:, cs], gxs[:, cs], AF.Sigmoid, scale=GC2),
                    [e3], "sA")
                e5 = P.emit("vector", lambda q, pb=pb, cs=cs, hloc=hloc:
                            q.tensor_tensor(
                    ghv[:, hloc, :], gxa[:, cs], psum[pb].ap()[:, :],
                    ALU.mult), [e4, ev_gh_w2done], "sV")
                ev_psum_free[pb] = e5
                ev_gelu_free[par] = e5
                ev_gelu_done.append(e5)
            ev_w1free[sbi] = ev_gelu_done[-1]

        e_mm = None
        for hbw in range(32):
            sbi = hbw % 2
            e_w2d = P.dma("sync", w2st[sbi][:],
                          w2_h[e, hbw * 128:(hbw + 1) * 128, :],
                          (bar12 if (e == 0 and hbw < 2) else [])
                          + [ev_w2free[sbi]], "sD")
            e_w2c = P.emit("vector", lambda q, sbi=sbi: q.tensor_copy(
                w2bf[sbi][:], w2st[sbi][:]), [e_w2d], "sV")
            for i4 in range(4):
                for dh in range(2):
                    pb = i4 * 2 + dh
                    e_mm = P.emit("tensor",
                                  lambda q, sbi=sbi, hbw=hbw, i4=i4, dh=dh,
                                  pb=pb: q.matmul(
                        psum[pb].ap()[:, :],
                        ghv[:, hbw, i4 * 128:(i4 + 1) * 128],
                        w2bf[sbi][:, dh * 512:(dh + 1) * 512],
                        start=(hbw == 0), stop=(hbw == 31),
                        skip_group_check=True),
                        [e_w2c, ev_gelu_done[hbw], ev_psum_free[pb],
                         ev_y_used],
                        "sPE" if (hbw == 31 or (i4 == 3 and dh == 1))
                        else None)
            ev_w2free[sbi] = e_mm
        ev_gh_w2done = e_mm
        e_pe_done = P.tip("sPE")
        ev_yev = []
        for i4 in range(4):
            for dh in range(2):
                pb = i4 * 2 + dh
                e_ev = P.emit("vector", lambda q, i4=i4, dh=dh, pb=pb:
                              q.tensor_copy(
                    ysv[:, i4, dh * 512:(dh + 1) * 512],
                    psum[pb].ap()[:, :]), [e_pe_done, ev_y_used], "sV")
                ev_psum_free[pb] = e_ev
                ev_yev.append(e_ev)
        e_y = ev_yev[-1]
        ev_un = []
        prev_grp = None
        for tb in range(16):
            for dh in range(2):
                pb = 4 + (tb * 2 + dh) % 2
                e_mm = None
                for ib in range(4):
                    mb = m1t[ib]
                    if dh == 0:
                        e_mt = P.emit("vector",
                                      lambda q, tb=tb, ib=ib, e=e, mb=mb:
                                      q.tensor_scalar(
                            mb[:], pos_bc[:, tb * 128:(tb + 1) * 128],
                            iotac[:, 4 * e + ib:4 * e + ib + 1], None,
                            ALU.is_equal),
                            [ev_posbc, ev_consts2, prev_grp], "sV")
                    else:
                        e_mt = None
                    e_mm = P.emit("tensor", lambda q, ib=ib, dh=dh, mb=mb,
                                  pb=pb: q.matmul(
                        pb_ap(psum, pb), mb[:],
                        ysv[:, ib, dh * 512:(dh + 1) * 512],
                        start=(ib == 0), stop=(ib == 3)),
                        [e_mt, e_y, ev_psum_free[pb]],
                        "sPE" if ib == 3 else None)
                prev_grp = e_mm if dh == 1 else prev_grp
                e_add = P.emit("vector", lambda q, tb=tb, dh=dh, pb=pb:
                               q.tensor_tensor(
                    gv[:, tb, dh * 512:(dh + 1) * 512], pb_ap(psum, pb),
                    gv[:, tb, dh * 512:(dh + 1) * 512], ALU.add),
                    [e_mm, ev_g[tb]], "sV")
                ev_psum_free[pb] = e_add
                ev_g[tb] = e_add
                ev_un.append(e_add)
        ev_y_used = ev_un[-1]
        ev_xns_free = P.tip("sPE")
    ev_silu = [None] * 16
    for tb in range(16):
        scr = gx2 if tb % 2 == 0 else gxa
        e_s = P.emit("scalar", lambda q, tb=tb, scr=scr: q.activation(
            scr[:, 0:1024], gv[:, tb, :], AF.Sigmoid),
            [ev_g[tb], ev_silu[tb - 2] if tb >= 2 else None], "sA")
        ev_silu[tb] = P.emit("vector", lambda q, tb=tb, scr=scr: q.tensor_tensor(
            gv[:, tb, :], gv[:, tb, :], scr[:, 0:1024], ALU.mult),
            [e_s], "sV")

    ph3.close()
    ph123.close()

    ffn_bar = [P.tip("sV"), P.tip("sA"), P.tip("sPE"), P.tip("sG"),
               P.tip("sD3"), P.tip("sD"), P.tip("sD2")]
    xnT = sb("xnT", 128, 8 * 4096, BF16)
    xnTv = xnT.ap().rearrange("p (b c) -> p b c", c=4096)
    for t in range(32):
        for db in range(8):
            P.dmat("sync" if (t + db) % 2 == 0 else "scalar",
                   xnTv[:, db, t * 128:(t + 1) * 128],
                   xv[:, t, db * 128:(db + 1) * 128], ffn_bar + [ev_xn[t]],
                   "sD" if (t + db) % 2 == 0 else "sD2")
    ev_xnT = [P.tip("sD"), P.tip("sD2")]
    phx.close()

    aoscr_h = dt("aoscr", [8, 128, TOK], BF16, kind="Internal").ap()
    ph5 = ExitStack()
    wst = sb("wst", 128, 512, F32, ph5, side="right")
    wsb = sb("wsb", 128, 1024, BF16, ph5, side="right")
    wqT = sb("wqT", 128, 8 * 128, BF16, ph5, side="right")
    wkT = sb("wkT", 128, 8 * 128, BF16, ph5, side="right")
    wvT = sb("wvT", 128, 8 * 128, BF16, ph5, side="right")
    ktp = sb("ktp", 128, 4096, BF16, ph5, side="right")
    vbuf = sb("vbuf", 128, 32 * 128, BF16, ph5, side="right")
    qtp = sb("qtp", 128, 256, BF16, ph5, side="right")
    zb = sb("zb", 128, 12288, F32, ph5, side="right")
    ratt = [sb(f"ratt{i}", 128, 4352, BF16, ph5, side="right") for i in range(2)]
    attT = sb("attT", 128, 4352, BF16, ph5, side="right")
    p8 = sb("p8", 128, 1536, BF16, ph5, side="right")
    rp = sb("rp", 128, 1536, BF16, ph5, side="right")
    tau2 = sb("tau2", 128, 8, BF16, ph5, side="right")
    tauf = sb("tauf", 128, 8, F32, ph5, side="right")
    tau_hl = [sb(f"tau_hl{i}", 2, 128, BF16, ph5, side="right")
              for i in range(4)]
    ones2 = sb("ones2", 2, 512, BF16, ph5, side="right")
    aot = [sb(f"aot{i}", 128, 128, BF16, ph5, side="right") for i in range(2)]
    gx_dummy = sb("gx_dummy", 128, 128, F32, ph5, side="right")
    wqTv = wqT.ap().rearrange("p (b c) -> p b c", c=128)
    wkTv = wkT.ap().rearrange("p (b c) -> p b c", c=128)
    wvTv = wvT.ap().rearrange("p (b c) -> p b c", c=128)
    vbv = vbuf.ap().rearrange("p (b c) -> p b c", c=128)

    e_o2 = P.emit("vector", lambda q: q.memset(ones2[:], 1.0), ffn_bar, "sV")

    ev_proj_free = None
    ev_zfree = {}
    ev_p8_free = {}
    ev_rp_free = {}
    ev_prevc = {}
    ev_aot_free = [None, None]
    QKB = [0, 1, 2, 3]
    PVB = [4, 5]
    e_lastpv = None
    ev_out_dma = []
    for pr in range(8):
        ev_w = []
        e_c = None
        for (wh, wTbv) in ((wq_h, wqTv), (wk_h, wkTv), (wv_h, wvTv)):
            for chh in range(2):
                e_d = P.dma("gpsimd", wst[:],
                            wh[pr * 128:(pr + 1) * 128,
                               chh * 512:(chh + 1) * 512],
                            ffn_bar + ev_xnT +
                            [e_c, ev_w[-1] if ev_w else ev_proj_free], "sD3")
                e_c = P.emit("vector", lambda q, chh=chh: q.tensor_copy(
                    wsb[:, chh * 512:(chh + 1) * 512], wst[:]), [e_d], "sV")
            for db in range(8):
                P.dmat("scalar", wTbv[:, db, :],
                       wsb[:, db * 128:(db + 1) * 128], [e_c], "sD2")
            ev_w.append(P.tip("sD2"))
        ev_wT = ev_w[-1]
        ev_kt = []
        for ch in range(8):
            pb = ch % 2
            e_mm = None
            for db in range(8):
                e_mm = P.emit("tensor", lambda q, ch=ch, db=db, pb=pb: q.matmul(
                    pb_ap(psum, pb), wkTv[:, db, :],
                    xnTv[:, db, ch * 512:(ch + 1) * 512],
                    start=(db == 0), stop=(db == 7)),
                    ([ev_wT, ev_proj_free] + ev_xnT +
                     [ev_psum_free.get(pb)]), "sPE" if db == 7 else None)
            e_ev = P.emit("vector", lambda q, ch=ch, pb=pb: q.tensor_copy(
                ktp[:, ch * 512:(ch + 1) * 512], pb_ap(psum, pb)), [e_mm], "sV")
            ev_psum_free[pb] = e_ev
            ev_kt.append(e_ev)
        ev_vt = []
        for tb in range(32):
            pb = 2 + tb % 2
            e_mm = None
            for db in range(8):
                e_mm = P.emit("tensor", lambda q, tb=tb, db=db, pb=pb: q.matmul(
                    pb_ap(psum, pb, 128), xnTv[:, db, tb * 128:(tb + 1) * 128],
                    wvTv[:, db, :], start=(db == 0), stop=(db == 7)),
                    [ev_wT, ev_proj_free, ev_psum_free.get(pb)],
                    "sPE" if db == 7 else None)
            e_ev = P.emit("vector", lambda q, tb=tb, pb=pb: q.tensor_copy(
                vbv[:, tb, :], pb_ap(psum, pb, 128)), [e_mm], "sV")
            ev_psum_free[pb] = e_ev
            ev_vt.append(e_ev)
        ev_proj = ev_kt + ev_vt + [ev_wT, e_o2]

        for qp in range(8):
            W1, W2 = 128 * (qp + 1), 128 * (16 - qp)
            qbs = [qp, 15 - qp]
            Ws = [W1, W2]
            chains = [(0, 0), (0, 1), (1, 0), (1, 1)]
            zos = {(h, qx): (h * 2048 if qx == 0 else 4096 + h * 4096)
                   for h, qx in chains}
            ros = {(h, qx): (0 if qx == 0 else 2 * W1) for h, qx in chains}
            pos8 = {(h, qx): (h * 256 if qx == 0 else 512 + h * 512)
                    for h, qx in chains}
            ev_qt = []
            for qx in range(2):
                pb = PVB[qx]
                e_mm = None
                for db in range(8):
                    e_mm = P.emit("tensor", lambda q, qx=qx, db=db, pb=pb,
                                  i=qbs[qx]: q.matmul(
                        pb_ap(psum, pb, 128), wqTv[:, db, :],
                        xnTv[:, db, i * 128:(i + 1) * 128],
                        start=(db == 0), stop=(db == 7)),
                        ev_proj + [ev_psum_free.get(pb)],
                        "sPE" if db == 7 else None)
                e_ev = P.emit("vector", lambda q, qx=qx, pb=pb: q.tensor_scalar(
                    qtp[:, qx * 128:(qx + 1) * 128], pb_ap(psum, pb, 128),
                    0.125, None, ALU.mult),
                    [e_mm, ev_qt[-1] if ev_qt else None], "sV")
                ev_psum_free[pb] = e_ev
                ev_qt.append(e_ev)
            ev_evac = {}
            for (h, qx) in chains:
                c = 2 * h + qx
                i = qbs[qx]
                nb = i + 1
                zo = zos[(h, qx)]
                pb = QKB[c]
                qsl = slice(64 * h, 64 * h + 64)
                prev_evac = ev_psum_free.get(pb)
                ev_evac[c] = []
                for rng in range(2):
                    kbase = rng * 16
                    ro = zo + rng * Ws[qx]
                    for cst in range(0, nb, 4):
                        cw = min(4, nb - cst) * 128
                        e_mm = P.emit(
                            "tensor",
                            lambda q, qx=qx, cst=cst, cw=cw, kbase=kbase,
                            pb=pb, qsl=qsl, h=h: q.matmul(
                                pb_ap(psum, pb, cw),
                                qtp.ap()[qsl, qx * 128:(qx + 1) * 128],
                                ktp.ap()[qsl,
                                         (kbase + cst) * 128:
                                         (kbase + cst) * 128 + cw],
                                start=True, stop=True,
                                tile_position=(64 * h, 0)),
                            ev_qt + [prev_evac], "sPE")
                        last = cst + cw // 128 == nb
                        zwar = ev_zfree.get(c)
                        if not last:
                            e_ev = P.emit(
                                "scalar",
                                lambda q, ro=ro, cst=cst, cw=cw, pb=pb:
                                q.activation(
                                    zb[:, ro + cst * 128:ro + cst * 128 + cw],
                                    pb_ap(psum, pb, cw), AF.Identity),
                                [e_mm, zwar], "sA")
                        else:
                            e_ev0 = None
                            if cw > 128:
                                e_ev0 = P.emit(
                                    "scalar",
                                    lambda q, ro=ro, cst=cst, cw=cw, pb=pb:
                                    q.activation(
                                        zb[:, ro + cst * 128:
                                           ro + cst * 128 + cw - 128],
                                        pb_ap(psum, pb, cw - 128),
                                        AF.Identity), [e_mm, zwar], "sA")
                                ev_evac[c].append(e_ev0)
                            mask = diagm if rng == 0 else peerm
                            e_ev = P.emit(
                                "vector",
                                lambda q, ro=ro, i=i, pb=pb, cw=cw,
                                mask=mask: q.tensor_tensor(
                                    zb[:, ro + i * 128:ro + (i + 1) * 128],
                                    psum[pb].ap()[:, cw - 128:cw],
                                    mask[:], ALU.add),
                                [e_mm, e_ev0, zwar], "sV")
                        ev_evac[c].append(e_ev)
                        prev_evac = e_ev
                ev_psum_free[pb] = prev_evac
            ev_upd = {}
            for (h, qx) in chains:
                c = 2 * h + qx
                Wx2 = 2 * Ws[qx]
                zo = zos[(h, qx)]
                po = pos8[(h, qx)]
                e_p8 = P.emit("vector", lambda q, zo=zo, po=po, Wx2=Wx2:
                              q.tensor_reduce(
                    p8[:, po:po + Wx2 // 8],
                    zb.ap()[:, zo:zo + Wx2]
                    .rearrange("p (n w) -> p n w", w=8), AX.X, ALU.max),
                    ev_evac[c] + [ev_p8_free.get(c)], "sV")
                e_mx = P.emit("vector", lambda q, po=po, Wx2=Wx2, c=c:
                              q.tensor_reduce(
                    small[:, 40 + c:41 + c], p8[:, po:po + Wx2 // 8],
                    AX.X, ALU.max), [e_p8], "sV")
                ev_upd[c] = P.emit("vector", lambda q, c=c: q.tensor_scalar(
                    tfc[:, c:c + 1], small[:, 40 + c:41 + c], -1.0, 1.0,
                    ALU.mult, ALU.add), [e_mx], "sV")
            ev_c = dict(ev_prevc)
            for j in range(JP + JF):
                pooled = j < JP
                prev_c = dict(ev_c)
                ev_f = {}
                ev_c = {}
                for (h, qx) in chains:
                    c = 2 * h + qx
                    sib = 2 * h + (1 - qx)
                    Wx2 = 2 * Ws[qx]
                    zo = zos[(h, qx)]
                    ro = ros[(h, qx)]
                    po = pos8[(h, qx)]
                    if pooled:
                        srcap = p8.ap()[:, po:po + Wx2 // 8]
                        dstap = rp.ap()[:, po:po + Wx2 // 8]
                    else:
                        srcap = zb.ap()[:, zo:zo + Wx2]
                        dstap = ratt[h].ap()[:, ro:ro + Wx2]
                    ev_f[c] = P.emit(
                        "scalar", lambda q, srcap=srcap, dstap=dstap, c=c:
                        q.activation(dstap, srcap, AF.Relu,
                                     bias=tfc[:, c:c + 1],
                                     accum_out=tfc[:, 8 + c:9 + c]),
                        [ev_upd[c], prev_c.get(c),
                         prev_c.get(sib) if not pooled else None,
                         ev_rp_free.get(c) if pooled else None], "sA")
                    if pooled:
                        ev_p8_free[c] = ev_f[c]
                        ev_c[c] = P.emit(
                            "scalar", lambda q, dstap=dstap, c=c: q.activation(
                                dstap, dstap, AF.Sign,
                                accum_out=tfc[:, 16 + c:17 + c]),
                            [ev_f[c]], "sA")
                        ev_rp_free[c] = ev_c[c]
                    else:
                        ev_c[c] = P.emit(
                            "vector", lambda q, dstap=dstap, c=c:
                            q.tensor_scalar(
                                dstap, dstap, 0.0, None, ALU.is_gt, ALU.add,
                                accum_out=tfc[:, 16 + c:17 + c]),
                            [ev_f[c]], "sV")
                    if qx == 1:
                        cl0 = 2 * h
                        e_rc = P.emit("vector", lambda q, cl0=cl0: q.reciprocal(
                            tfc[:, 24 + cl0:26 + cl0], tfc[:, 16 + cl0:18 + cl0]),
                            [ev_c[cl0], ev_c[cl0 + 1]], "sV")
                        e_d = P.emit("vector", lambda q, cl0=cl0:
                                     q.scalar_tensor_tensor(
                            small[:, 48 + cl0:50 + cl0],
                            tfc[:, 8 + cl0:10 + cl0], -1.0,
                            tfc[:, 24 + cl0:26 + cl0],
                            ALU.add, ALU.mult), [e_rc], "sV")
                        e_u = P.emit("vector", lambda q, cl0=cl0: q.tensor_tensor(
                            tfc[:, cl0:cl0 + 2], tfc[:, cl0:cl0 + 2],
                            small[:, 48 + cl0:50 + cl0], ALU.subtract),
                            [e_d], "sV")
                        ev_upd[cl0] = e_u
                        ev_upd[cl0 + 1] = e_u
            ev_prevc = dict(ev_c)
            ev_thl = {}
            prev_tp5 = None
            for (h, qx) in chains:
                c = 2 * h + qx
                e1 = P.emit("vector", lambda q, c=c: q.tensor_copy(
                    tau2[:, 2 * c:2 * c + 1], tfc[:, c:c + 1]),
                    [ev_upd[c], ev_c[c]], "sV")
                e2 = P.emit("vector", lambda q, c=c: q.tensor_copy(
                    tauf[:, 2 * c:2 * c + 1], tau2[:, 2 * c:2 * c + 1]),
                    [e1], "sV")
                e3 = P.emit("vector", lambda q, c=c: q.tensor_tensor(
                    tauf[:, 2 * c + 1:2 * c + 2], tfc[:, c:c + 1],
                    tauf[:, 2 * c:2 * c + 1], ALU.subtract), [e2], "sV")
                e_tp = P.emit("tensor", lambda q, c=c: q.transpose(
                    psum[6].ap()[0:2, 0:128], tauf[:, 2 * c:2 * c + 2],
                    ident[:]),
                    [e3, prev_tp5, ev_psum_free.get(6)], "sPE")
                e_ev = P.emit("vector", lambda q, c=c: q.tensor_copy(
                    tau_hl[c][:], psum[6].ap()[0:2, 0:128]), [e_tp], "sV")
                prev_tp5 = e_ev
                ev_psum_free[6] = e_ev
                ev_thl[c] = e_ev
            for (h, qx) in chains:
                c = 2 * h + qx
                i = qbs[qx]
                nb = i + 1
                ro = ros[(h, qx)]
                pb = QKB[c]
                qsl = slice(64 * h, 64 * h + 64)
                ev_qk2 = []
                for rng in range(2):
                    kbase = rng * 16
                    ro2 = ro + rng * Ws[qx]
                    for cst in range(0, nb, 4):
                        cw = min(4, nb - cst) * 128
                        for j4 in range(cw // 128):
                            kb = kbase + cst + j4
                            masked = (kb == kbase + i)
                            P.emit("tensor",
                                   lambda q, kb=kb, j4=j4, pb=pb, qsl=qsl,
                                   h=h, qx=qx: q.matmul(
                                psum[pb].ap()[:, j4 * 128:(j4 + 1) * 128],
                                ktp.ap()[qsl, kb * 128:(kb + 1) * 128],
                                qtp.ap()[qsl, qx * 128:(qx + 1) * 128],
                                start=True, stop=False,
                                tile_position=(64 * h, 0),
                                skip_group_check=True),
                                [ev_thl[c], ev_psum_free.get(pb)], None)
                            P.emit("tensor",
                                   lambda q, j4=j4, pb=pb, c=c, masked=masked:
                                   q.matmul(
                                psum[pb].ap()[:, j4 * 128:(j4 + 1) * 128],
                                ones2[0:2, 0:128], tau_hl[c][:],
                                start=False, stop=not masked,
                                tile_position=(0, 0),
                                skip_group_check=True),
                                [ev_thl[c]],
                                None if masked else
                                ("sPE" if j4 == cw // 128 - 1 else None))
                            if masked:
                                mb_ = diagmb if rng == 0 else peermb
                                P.emit("tensor",
                                       lambda q, j4=j4, pb=pb, mb_=mb_:
                                       q.matmul(
                                    psum[pb].ap()[:, j4 * 128:(j4 + 1) * 128],
                                    mb_[:], identb[:],
                                    start=False, stop=True,
                                    tile_position=(0, 0),
                                    skip_group_check=True),
                                    [ev_consts],
                                    "sPE" if j4 == cw // 128 - 1 else None)
                        e_mm2 = P.tip("sPE")
                        eng = "scalar" if (cst // 4) % 2 == 0 else "vector"
                        if eng == "scalar":
                            e_ev2 = P.emit(
                                "scalar",
                                lambda q, ro2=ro2, cst=cst, cw=cw, pb=pb:
                                q.activation(
                                    attT[:, ro2 + cst * 128:
                                         ro2 + cst * 128 + cw],
                                    pb_ap(psum, pb, cw), AF.Relu),
                                [e_mm2], "sA")
                        else:
                            e_ev2 = P.emit(
                                "vector",
                                lambda q, ro2=ro2, cst=cst, cw=cw, pb=pb:
                                q.tensor_scalar(
                                    attT[:, ro2 + cst * 128:
                                         ro2 + cst * 128 + cw],
                                    pb_ap(psum, pb, cw), 0.0, None, ALU.max),
                                [e_mm2], "sV")
                        ev_qk2.append(e_ev2)
                        ev_psum_free[pb] = e_ev2
                pvb = PVB[qx]
                tbs = list(range(0, i + 1)) + list(range(16, 16 + i + 1))
                e_mm = None
                for jb, tb in enumerate(tbs):
                    e_mm = P.emit("tensor", lambda q, tb=tb, jb=jb, ro=ro,
                                  h=h, pvb=pvb: q.matmul(
                        psum[pvb].ap()[64 * h:64 * h + 64, 0:128],
                        vbv[:, tb, 64 * h:64 * h + 64],
                        attT[:, ro + jb * 128:ro + (jb + 1) * 128],
                        start=(jb == 0), stop=(jb == len(tbs) - 1),
                        skip_group_check=True),
                        ev_qk2 + [ev_psum_free.get((pvb, h))],
                        "sPE" if jb == len(tbs) - 1 else None)
                e_lastpv = e_mm
                if h == 1:
                    qb = qbs[qx]
                    bi = qx
                    e_ev = P.emit("vector", lambda q, bi=bi, pvb=pvb:
                                  q.tensor_copy(aot[bi][:],
                                                psum[pvb].ap()[:, 0:128]),
                        [e_mm, ev_aot_free[bi]], "sV")
                    ev_psum_free[(pvb, 0)] = e_ev
                    ev_psum_free[(pvb, 1)] = e_ev
                    ev_psum_free[pvb] = e_ev
                    e_st = P.dma("sync", aoscr_h[pr, :, qb * 128:(qb + 1) * 128],
                                 aot[bi][:], [e_ev], "sD")
                    ev_aot_free[bi] = e_st
                    ev_out_dma.append(e_st)
        ev_proj_free = e_lastpv

    att_bar = [P.tip("sV"), P.tip("sA"), P.tip("sPE"), P.tip("sG"),
               P.tip("sD3"), P.tip("sD"), P.tip("sD2")]
    ph5.close()
    ph6 = ExitStack()
    woT = sb("woT", 128, 8 * 1024, BF16, ph6, side="right")
    wst6 = sb("wst6", 128, 512, F32, ph6, side="right")
    wsb6 = sb("wsb6", 128, 1024, BF16, ph6, side="right")
    woTv = woT.ap().rearrange("p (b c) -> p b c", c=1024)
    ev_wo = []
    e_c = None
    for sl in range(8):
        for chh in range(2):
            e_d = P.dma("gpsimd", wst6[:],
                        wo_h[sl * 128:(sl + 1) * 128,
                             chh * 512:(chh + 1) * 512],
                        att_bar + [e_c, ev_wo[-1] if ev_wo else None], "sD3")
            e_c = P.emit("vector", lambda q, chh=chh: q.tensor_copy(
                wsb6[:, chh * 512:(chh + 1) * 512], wst6[:]), [e_d], "sV")
        for db in range(8):
            P.dmat("scalar", woTv[:, db, sl * 128:(sl + 1) * 128],
                   wsb6[:, db * 128:(db + 1) * 128], [e_c], "sD2")
        ev_wo.append(P.tip("sD2"))
    ev_woT = ev_wo[-1]
    xres = [sb(f"xres{i}", 128, 1024, F32, ph6, side="right") for i in range(2)]
    osb = [sb(f"osb{i}", 128, 1024, F32, ph6, side="right") for i in range(2)]
    aoin = [sb(f"aoin{i}", 128, 8 * 128, BF16, ph6, side="right")
            for i in range(2)]
    aov = [a.ap().rearrange("p (b c) -> p b c", c=128) for a in aoin]
    ev_osb_free = [None, None]
    ev_aoin_free = [None, None]
    ev_xres = [None, None]
    ev_aoin = [None, None]
    for qb in range(16):
        bi = qb % 2
        ev_xres[bi] = P.dma("sync", xres[bi][:], x_h[qb * 128:(qb + 1) * 128, :],
                            att_bar + [ev_osb_free[bi]], "sD")
        ev_aoin[bi] = P.dma("sync", aov[bi],
                            aoscr_h[:, :, qb * 128:(qb + 1) * 128]
                            .rearrange("b p c -> p b c"),
                            att_bar + ev_out_dma[-1:] + [ev_aoin_free[bi]],
                            "sD")
        for dh in range(2):
            pb = 6 + dh
            e_mm = None
            for eb in range(8):
                e_mm = P.emit("tensor", lambda q, eb=eb, qb=qb, dh=dh, pb=pb,
                              bi=bi: q.matmul(
                    psum[pb].ap()[:, :], aov[bi][:, eb, :],
                    woTv[:, eb, dh * 512:(dh + 1) * 512],
                    start=(eb == 0), stop=(eb == 7)),
                    [ev_aoin[bi], ev_woT, ev_psum_free.get(pb)],
                    "sPE" if eb == 7 else None)
            e_m = P.emit("vector", lambda q, qb=qb, dh=dh, bi=bi, pb=pb:
                         q.tensor_tensor(
                osb[bi][:, dh * 512:(dh + 1) * 512], psum[pb].ap()[:, :],
                gv[:, qb, dh * 512:(dh + 1) * 512], ALU.mult),
                [e_mm, ev_silu[qb], ev_osb_free[bi]], "sV")
            e_a = P.emit("vector", lambda q, qb=qb, dh=dh, bi=bi:
                         q.tensor_tensor(
                osb[bi][:, dh * 512:(dh + 1) * 512],
                osb[bi][:, dh * 512:(dh + 1) * 512],
                xres[bi][:, dh * 512:(dh + 1) * 512], ALU.add),
                [e_m, ev_xres[bi]], "sV")
            ev_psum_free[pb] = e_a
        e_st = P.dma("sync", out_h[qb * 128:(qb + 1) * 128, :], osb[bi][:],
                     [P.tip("sV")], "sO")
        ev_osb_free[bi] = e_st
        ev_aoin_free[bi] = e_mm
    P.emit("gpsimd", lambda q: q.nop(), [P.tip("sO")], None)

    ph6.close()
    P.realize(blk, sems)
    st.close()
    return nc


def pb_ap(psum, pb, w=512):
    return psum[pb].ap()[:, 0:w]


_cache = {}


def kernel(L0_attn_x, L0_attn_Wq, L0_attn_Wk, L0_attn_Wv, L0_attn_Wo,
           L0_ffn_W1, L0_ffn_W2, L0_ffn_Wg, L0_ffn_bg):
    bf = ml_dtypes.bfloat16
    x = np.ascontiguousarray(np.asarray(L0_attn_x, np.float32))
    consts = {}
    p = np.arange(128)[:, None]
    c = np.arange(128)[None, :]
    consts["diagm"] = np.where(c <= p, 0.0, NEGC).astype(np.float32)
    consts["diagmb"] = np.where(c <= p, 0.0, NEGC).astype(bf)
    consts["identb"] = np.eye(128).astype(bf)
    consts["u128"] = (np.arange(128)[:, None] < np.arange(128)[None, :]) \
        .astype(bf)
    consts["u16"] = (np.arange(16)[:, None] < np.arange(16)[None, :]) \
        .astype(np.float32)
    bselm = np.zeros((128, 16, 16), bf)
    for tb in range(16):
        bselm[:, tb, tb] = 1
    consts["bsel"] = bselm.reshape(128, 256)
    consts["ones1"] = np.ones((1, 128), bf)
    consts["ones1f"] = np.ones((1, 128), np.float32)
    consts["onesrow"] = np.ones((1, 512), bf)
    consts["ecrow"] = (np.arange(E_) * float(CAP)).astype(bf)[None, :]
    consts["iota512"] = np.broadcast_to(
        np.arange(512, dtype=np.float32)[None, :], (128, 512)).copy()
    consts["iotac"] = (np.arange(128)[:, None]
                       + 128.0 * np.arange(32)[None, :]).astype(np.float32)
    consts["ident"] = np.eye(128, dtype=np.float32)

    nc = build_program()
    in_maps = []
    for core in range(8):
        b, par = core // 2, core % 2
        xb = x[b].reshape(32, 128, D_)
        own = xb[par::2].reshape(TOK, D_)
        peer = xb[1 - par::2].reshape(TOK, D_)
        im = dict(consts)
        im["x"] = np.ascontiguousarray(np.concatenate([own, peer], 0))
        im["wq"] = np.asarray(L0_attn_Wq, np.float32)
        im["wk"] = np.asarray(L0_attn_Wk, np.float32)
        im["wv"] = np.asarray(L0_attn_Wv, np.float32)
        im["wo"] = np.asarray(L0_attn_Wo, np.float32)
        im["w1"] = np.asarray(L0_ffn_W1, np.float32)
        im["w2"] = np.asarray(L0_ffn_W2, np.float32)
        im["wg"] = np.asarray(L0_ffn_Wg, np.float32)
        im["bg"] = np.asarray(L0_ffn_bg, np.float32)[None, :]
        im["peerm"] = np.full((128, 128), NEGC if par == 0 else 0.0,
                              np.float32)
        im["peermb"] = im["peerm"].astype(bf)
        in_maps.append(im)
    res = run_bass_kernel_spmd(nc, in_maps, core_ids=list(range(8)))
    out = np.empty((B_, L_, D_), np.float32)
    for core in range(8):
        b, par = core // 2, core % 2
        oc = np.asarray(res.results[core]["out"]).reshape(16, 128, D_)
        out[b].reshape(32, 128, D_)[par::2] = oc
    return out


# version 11
# speedup vs baseline: 1.6930x; 1.0578x over previous
import numpy as np
import ml_dtypes
import concourse.bass as bass
import concourse.mybir as mybir
from concourse.bass_utils import run_bass_kernel_spmd
from contextlib import ExitStack

F32 = mybir.dt.float32
BF16 = mybir.dt.bfloat16
AF = mybir.ActivationFunctionType
ALU = mybir.AluOpType
AX = mybir.AxisListType

B_, L_, D_ = 4, 4096, 1024
E_, HID_ = 8, 4096
TOK = 2048
CAP = 512
NEGC = -1.0e9
JP = 8
JF = 2
EPS = 1e-5
GC1 = 0.044715
GC2 = 1.5957691216057308

ENGINES = ("sync", "tensor", "vector", "scalar", "gpsimd")


class Prog:
    def __init__(self):
        self.ops = {e: [] for e in ENGINES}
        self.cnt = {s: 0 for s in ("sD", "sD2", "sD3", "sPE", "sV", "sA",
                                   "sG", "sO")}

    def emit(self, eng, fn, waits=(), inc=None, amt=1):
        waits = [w for w in waits if w is not None]
        self.ops[eng].append((waits, fn, inc, amt))
        if inc is not None:
            self.cnt[inc] += amt
            return (inc, self.cnt[inc])
        return None

    def tip(self, sem):
        return (sem, self.cnt[sem])

    def dma(self, eng, out, in_, waits=(), sem="sD", **kw):
        return self.emit(
            eng, lambda q, o=out, i=in_, kw=kw: q.dma_start(out=o, in_=i, **kw),
            waits, sem, 16)

    def dmat(self, eng, out, in_, waits=(), sem="sD"):
        return self.emit(
            eng, lambda q, o=out, i=in_: q.dma_start_transpose(o, i),
            waits, sem, 16)

    def realize(self, block, sems):
        for eng in ENGINES:
            ops = self.ops[eng]
            if not ops:
                continue

            def body(q, ops=ops):
                seen = {}
                for waits, fn, inc, amt in ops:
                    for (s, v) in waits:
                        if seen.get(s, 0) < v:
                            q.wait_ge(sems[s], v)
                            seen[s] = v
                    inst = fn(q)
                    if inc is not None:
                        inst.then_inc(sems[inc], amt)
            getattr(block, eng)(body)


def build_program():
    nc = bass.Bass("TRN2", target_bir_lowering=False, debug=False)
    dt = nc.dram_tensor
    x_h = dt("x", [L_, D_], F32, kind="ExternalInput").ap()
    wq_h = dt("wq", [D_, D_], F32, kind="ExternalInput").ap()
    wk_h = dt("wk", [D_, D_], F32, kind="ExternalInput").ap()
    wv_h = dt("wv", [D_, D_], F32, kind="ExternalInput").ap()
    wo_h = dt("wo", [D_, D_], F32, kind="ExternalInput").ap()
    w1_h = dt("w1", [E_, D_, HID_], F32, kind="ExternalInput").ap()
    w2_h = dt("w2", [E_, HID_, D_], F32, kind="ExternalInput").ap()
    wg_h = dt("wg", [D_, E_], F32, kind="ExternalInput").ap()
    bg_h = dt("bg", [1, E_], F32, kind="ExternalInput").ap()
    diagm_h = dt("diagm", [128, 128], F32, kind="ExternalInput").ap()
    diagmb_h = dt("diagmb", [128, 128], BF16, kind="ExternalInput").ap()
    peermb_h = dt("peermb", [128, 128], BF16, kind="ExternalInput").ap()
    identb_h = dt("identb", [128, 128], BF16, kind="ExternalInput").ap()
    peerm_h = dt("peerm", [128, 128], F32, kind="ExternalInput").ap()
    u128_h = dt("u128", [128, 128], BF16, kind="ExternalInput").ap()
    u16_h = dt("u16", [16, 16], F32, kind="ExternalInput").ap()
    bsel_h = dt("bsel", [128, 16 * 16], BF16, kind="ExternalInput").ap()
    ones1_h = dt("ones1", [1, 128], BF16, kind="ExternalInput").ap()
    ones1f_h = dt("ones1f", [1, 128], F32, kind="ExternalInput").ap()
    onesrow_h = dt("onesrow", [1, 512], BF16, kind="ExternalInput").ap()
    ecrow_h = dt("ecrow", [1, E_], BF16, kind="ExternalInput").ap()
    iota512_h = dt("iota512", [128, 512], F32, kind="ExternalInput").ap()
    iotac_h = dt("iotac", [128, 32], F32, kind="ExternalInput").ap()
    ident_h = dt("ident", [128, 128], F32, kind="ExternalInput").ap()
    out_h = dt("out", [TOK, D_], F32, kind="ExternalOutput").ap()

    P = Prog()
    st = ExitStack()

    def sb(name, p, f, dtp, stk=None, side="left"):
        return (stk or st).enter_context(
            nc.sbuf_tensor("s_" + name, [p, f], dtp, side=side))

    phx = ExitStack()
    xn = sb("xn", 128, 32 * 1024, BF16, phx, side="right")
    diagm = sb("diagm", 128, 128, F32)
    diagmb = sb("diagmb", 128, 128, BF16)
    peermb = sb("peermb", 128, 128, BF16)
    identb = sb("identb", 128, 128, BF16)
    peerm = sb("peerm", 128, 128, F32)
    ident = sb("ident", 128, 128, F32)
    pos = sb("pos", 128, 16, F32)
    tfc = sb("tfc", 128, 32, F32)
    small = sb("small", 128, 64, F32)

    xv = xn.ap().rearrange("p (b c) -> p b c", c=1024)

    psum = [st.enter_context(nc.psum_tensor(f"pp{i}", [128, 512], F32))
            for i in range(8)]

    semst = st.enter_context(ExitStack())
    sems = {s: semst.enter_context(nc.semaphore("m_" + s))
            for s in ("sD", "sD2", "sD3", "sPE", "sV", "sA", "sG", "sO")}
    blk = st.enter_context(nc.Block())

    ph123 = ExitStack()
    ones1 = sb("ones1", 1, 128, BF16, ph123, side="right")
    ones1f = sb("ones1f", 1, 128, F32, ph123, side="right")
    onesrow = sb("onesrow", 1, 512, BF16, ph123, side="right")
    ecrow = sb("ecrow", 1, E_, BF16, ph123, side="right")
    u128 = sb("u128", 128, 128, BF16, ph123, side="right")
    u16 = sb("u16", 16, 16, F32, ph123, side="right")
    bsel = sb("bsel", 128, 16 * 16, BF16, ph123, side="right")
    iota512 = sb("iota512", 128, 512, F32, ph123, side="right")
    iotac = sb("iotac", 128, 32, F32, ph123, side="right")
    pos_bc = sb("pos_bc", 128, TOK, F32, ph123, side="right")
    for dst, src in [(diagm, diagm_h), (diagmb, diagmb_h),
                     (peermb, peermb_h), (identb, identb_h),
                     (peerm, peerm_h), (ones1, ones1_h),
                     (ones1f, ones1f_h), (onesrow, onesrow_h),
                     (ecrow, ecrow_h), (ident, ident_h)]:
        P.dma("gpsimd", dst[:], src, sem="sD3")
    ev_consts = P.tip("sD3")
    bselv = bsel.ap().rearrange("p (b c) -> p b c", c=16)
    for dst, src in [(u128, u128_h), (bsel, bsel_h), (iota512, iota512_h),
                     (iotac, iotac_h)]:
        P.dma("gpsimd", dst[:], src, sem="sD3")
    P.dma("gpsimd", u16.ap(), u16_h, sem="sD3")
    ev_consts2 = P.tip("sD3")

    ph12 = ExitStack()
    xt = [sb(f"xt{i}", 128, 1024, F32, ph12, side="right") for i in range(2)]
    lnsq = sb("lnsq", 128, 1024, BF16, ph12, side="right")
    xnf = sb("xnf", 128, 1024, F32, ph12, side="right")
    xlo = sb("xlo", 128, 1024, BF16, ph12, side="right")
    xnqT = sb("xnqT", 128, 8 * TOK, BF16, ph12, side="right")
    xloT = sb("xloT", 128, 8 * TOK, BF16, ph12, side="right")
    wgs = sb("wgs", 128, 8 * E_, F32, ph12, side="right")
    wghi = sb("wghi", 128, 8 * E_, BF16, ph12, side="right")
    wglo = sb("wglo", 128, 8 * E_, BF16, ph12, side="right")
    wglof = sb("wglof", 128, 8 * E_, F32, ph12, side="right")
    bgs = sb("bgs", 1, 4 * E_, F32, ph12, side="right")
    bghi = sb("bghi", 1, E_, BF16, ph12, side="right")
    bglo = sb("bglo", 1, E_, BF16, ph12, side="right")
    lgT = sb("lgT", 8, TOK, F32, ph12, side="right")
    lg = sb("lg", 128, 16 * 8, F32, ph12, side="right")
    oh = sb("oh", 128, 16 * 8, F32, ph12, side="right")
    ohb = sb("ohb", 128, 16 * 8, BF16, ph12, side="right")
    rk = sb("rk", 128, 16 * 8, F32, ph12, side="right")
    cnt_sb = sb("cnt_sb", 16, E_, F32, ph12, side="right")
    off_sb = sb("off_sb", 16, E_, F32, ph12, side="right")
    offrow = sb("offrow", 1, 16 * E_, F32, ph12, side="right")

    posT = sb("posT", 16, 128, F32, ph12, side="right")
    posrow = lgT

    e_wg = P.dma("gpsimd", wgs.ap().rearrange("p (b c) -> p b c", c=E_),
                 wg_h.rearrange("(b p) c -> p b c", p=128), sem="sD3")
    e_bg = P.dma("gpsimd", bgs[0:1, 0:E_], bg_h, sem="sD3")
    e_wghi = P.emit("vector", lambda q: q.tensor_copy(wghi[:], wgs[:]),
                    [e_wg], "sV")
    e2 = P.emit("vector", lambda q: q.tensor_tensor(
        wglof[:], wgs[:], wghi[:], ALU.subtract), [e_wghi], "sV")
    e_wglo = P.emit("vector", lambda q: q.tensor_copy(wglo[:], wglof[:]),
                    [e2], "sV")
    e_bghi = P.emit("vector", lambda q: q.tensor_copy(
        bghi[:], bgs[0:1, 0:E_]), [e_bg], "sV")
    e2 = P.emit("vector", lambda q: q.tensor_tensor(
        bgs[0:1, E_:2 * E_], bgs[0:1, 0:E_], bghi[:], ALU.subtract),
        [e_bghi], "sV")
    e_bglo = P.emit("vector", lambda q: q.tensor_copy(
        bglo[:], bgs[0:1, E_:2 * E_]), [e2], "sV")

    xnqTv = xnqT.ap().rearrange("p (b c) -> p b c", c=TOK)
    xloTv = xloT.ap().rearrange("p (b c) -> p b c", c=TOK)
    prev_xt = [None, None]
    prev_lo_dmat = None
    ev_xn = [None] * 32
    for t in range(32):
        bi = t % 2
        e_x = P.dma("sync", xt[bi][:], x_h[t * 128:(t + 1) * 128, :],
                    [prev_xt[bi]], "sD")
        c0, c1 = 2 * bi, 2 * bi + 1
        e_s1 = P.emit("vector", lambda q, b=bi, c=c0: q.reduce_sum(
            small[:, c:c + 1], xt[b][:], AX.X), [e_x], "sV")
        e_s2 = P.emit("scalar", lambda q, b=bi, c=c1: q.activation(
            lnsq[:], xt[b][:], AF.Square, accum_out=small[:, c:c + 1]),
            [e_x], "sA")
        e_m = P.emit("vector", lambda q, c=c0: q.tensor_scalar(
            small[:, c + 4:c + 5], small[:, c:c + 1], 1.0 / D_, None,
            ALU.mult), [e_s1], "sV")
        e_ms = P.emit("scalar", lambda q, c=c0: q.activation(
            small[:, c + 8:c + 9], small[:, c + 4:c + 5], AF.Square),
            [e_m], "sA")
        e_v = P.emit("vector", lambda q, c0=c0, c1=c1: q.tensor_scalar(
            small[:, c1 + 8:c1 + 9], small[:, c1:c1 + 1], 1.0 / D_,
            small[:, c0 + 8:c0 + 9], ALU.mult, ALU.subtract),
            [e_s2, e_ms], "sV")
        e_ve = P.emit("vector", lambda q, c1=c1: q.tensor_scalar(
            small[:, c1 + 8:c1 + 9], small[:, c1 + 8:c1 + 9], EPS, None,
            ALU.add), [e_v], "sV")
        e_sq = P.emit("scalar", lambda q, c1=c1: q.activation(
            small[:, c1 + 12:c1 + 13], small[:, c1 + 8:c1 + 9], AF.Sqrt),
            [e_ve], "sA")
        e_rs = P.emit("vector", lambda q, c1=c1: q.reciprocal(
            small[:, c1 + 16:c1 + 17], small[:, c1 + 12:c1 + 13]),
            [e_sq], "sV")
        ev_xn[t] = P.emit("vector", lambda q, b=bi, t=t, c0=c0, c1=c1:
                          q.tensor_scalar(
            xv[:, t, :], xt[b][:], small[:, c0 + 4:c0 + 5],
            small[:, c1 + 16:c1 + 17], ALU.subtract, ALU.mult),
            [e_rs], "sV")
        prev_xt[bi] = ev_xn[t]
        if t < 16:
            e_f = P.emit("vector", lambda q, b=bi, c0=c0, c1=c1:
                         q.tensor_scalar(
                xnf[:], xt[b][:], small[:, c0 + 4:c0 + 5],
                small[:, c1 + 16:c1 + 17], ALU.subtract, ALU.mult),
                [e_rs], "sV")
            e_lo = P.emit("vector", lambda q, t=t: q.tensor_tensor(
                xlo[:], xnf[:], xv[:, t, :], ALU.subtract),
                [e_f, ev_xn[t], prev_lo_dmat], "sV")
            prev_xt[bi] = e_lo
            for db in range(8):
                P.dmat("sync", xnqTv[:, db, t * 128:(t + 1) * 128],
                       xv[:, t, db * 128:(db + 1) * 128], [ev_xn[t]], "sD")
                P.dmat("sync", xloTv[:, db, t * 128:(t + 1) * 128],
                       xlo[:, db * 128:(db + 1) * 128], [e_lo], "sD")
            prev_lo_dmat = P.tip("sD")
    ev_xT_all = P.tip("sD")

    wghv = wghi.ap().rearrange("p (b c) -> p b c", c=E_)
    wglv = wglo.ap().rearrange("p (b c) -> p b c", c=E_)
    lg_ev = []
    for ch in range(4):
        tl = ch * 512
        for db in range(8):
            P.emit("tensor", lambda q, db=db, tl=tl, f=(db == 0): q.matmul(
                psum[0].ap()[0:8, :], wghv[:, db, :],
                xnqTv[:, db, tl:tl + 512], start=f, stop=False),
                [ev_xT_all, e_wghi, lg_ev[-1] if lg_ev else None], None)
            P.emit("tensor", lambda q, db=db, tl=tl: q.matmul(
                psum[0].ap()[0:8, :], wglv[:, db, :],
                xnqTv[:, db, tl:tl + 512], start=False, stop=False),
                [e_wglo], None)
            P.emit("tensor", lambda q, db=db, tl=tl: q.matmul(
                psum[0].ap()[0:8, :], wghv[:, db, :],
                xloTv[:, db, tl:tl + 512], start=False, stop=False), [], None)
        P.emit("tensor", lambda q: q.matmul(
            psum[0].ap()[0:8, :], bghi[:], onesrow[:], start=False,
            stop=False), [e_bghi, ev_consts], None)
        e_mm = P.emit("tensor", lambda q: q.matmul(
            psum[0].ap()[0:8, :], bglo[:], onesrow[:], start=False,
            stop=True), [e_bglo], "sPE")
        lg_ev.append(P.emit("vector", lambda q, tl=tl: q.tensor_copy(
            lgT[:, tl:tl + 512], psum[0].ap()[0:8, :]), [e_mm], "sV"))

    lgv = lg.ap().rearrange("p (b c) -> p b c", c=8)
    ohv = oh.ap().rearrange("p (b c) -> p b c", c=8)
    ohbv = ohb.ap().rearrange("p (b c) -> p b c", c=8)
    rkv = rk.ap().rearrange("p (b c) -> p b c", c=8)
    ev_oh = [None] * 16
    ev_ohb = [None] * 16
    prev_tp = None
    for b in range(16):
        e_tp = P.emit("tensor", lambda q, b=b: q.transpose(
            psum[1].ap()[:, 0:8], lgT[:, b * 128:(b + 1) * 128],
            ident.ap()[0:8, 0:8]),
            [lg_ev[b // 4], ev_consts, prev_tp], "sPE")
        e_cp = P.emit("vector", lambda q, b=b: q.tensor_copy(
            lgv[:, b, :], psum[1].ap()[:, 0:8]), [e_tp], "sV")
        prev_tp = e_cp
        sc = 32 + (b % 8)
        e_mx = P.emit("vector", lambda q, b=b, sc=sc: q.tensor_reduce(
            small[:, sc:sc + 1], lgv[:, b, :], AX.X, ALU.max), [e_cp], "sV")
        ev_oh[b] = P.emit("vector", lambda q, b=b, sc=sc: q.tensor_scalar(
            ohv[:, b, :], lgv[:, b, :], small[:, sc:sc + 1], None, ALU.is_ge),
            [e_mx], "sV")
        ev_ohb[b] = P.emit("vector", lambda q, b=b: q.tensor_copy(
            ohbv[:, b, :], ohv[:, b, :]), [ev_oh[b]], "sV")
    for b in range(16):
        P.emit("tensor", lambda q, b=b: q.matmul(
            psum[2].ap()[0:16, 0:8], bselv[:, b, :], ohbv[:, b, :],
            start=(b == 0), stop=(b == 15)),
            [ev_ohb[b], ev_consts2], "sPE" if b == 15 else None)
    e_cnt = P.emit("vector", lambda q: q.tensor_copy(
        cnt_sb[:], psum[2].ap()[0:16, 0:8]), [P.tip("sPE")], "sV")
    e_omm = P.emit("tensor", lambda q: q.matmul(
        psum[3].ap()[0:16, 0:8], u16[:], cnt_sb[:], start=True, stop=True),
        [e_cnt], "sPE")
    e_off = P.emit("vector", lambda q: q.tensor_copy(
        off_sb[:], psum[3].ap()[0:16, 0:8]), [e_omm], "sV")
    e_orow = P.dma("sync", offrow.ap().rearrange("a (b c) -> a b c", c=E_),
                   off_sb.ap(), [e_off], "sD")

    ev_pos = [None] * 16
    for b in range(16):
        pp = psum[4 + b % 2]
        P.emit("tensor", lambda q, b=b, pp=pp: q.matmul(
            pp.ap()[:, 0:8], u128[:], ohbv[:, b, :], start=True, stop=False),
            [ev_ohb[b], ev_consts2, ev_pos[b - 2] if b >= 2 else None], None)
        P.emit("tensor", lambda q, b=b, pp=pp: q.matmul(
            pp.ap()[:, 0:8], ones1f[:], offrow[:, b * 8:(b + 1) * 8],
            start=False, stop=False), [e_orow], None)
        e3 = P.emit("tensor", lambda q, pp=pp: q.matmul(
            pp.ap()[:, 0:8], ones1[:], ecrow[:], start=False, stop=True),
            [ev_consts], "sPE")
        e_rk = P.emit("vector", lambda q, b=b, pp=pp: q.tensor_copy(
            rkv[:, b, :], pp.ap()[:, 0:8]), [e3], "sV")
        e_pr = P.emit("vector", lambda q, b=b: q.tensor_tensor(
            rkv[:, b, :], rkv[:, b, :], ohv[:, b, :], ALU.mult),
            [e_rk, ev_oh[b]], "sV")
        ev_pos[b] = P.emit("vector", lambda q, b=b: q.reduce_sum(
            pos[:, b:b + 1], rkv[:, b, :], AX.X), [e_pr], "sV")
    e_pt = P.emit("tensor", lambda q: q.transpose(
        psum[5].ap()[0:16, 0:128], pos[:], ident[:]),
        ev_pos + [P.tip("sPE")], "sPE")
    e_ptc = P.emit("vector", lambda q: q.tensor_copy(
        posT[:], psum[5].ap()[0:16, 0:128]), [e_pt], "sV")
    e_prw = P.dma("sync",
                  posrow.ap()[0:1, :].rearrange("a (b c) -> a b c", c=128),
                  posT.ap(), [e_ptc], "sD")
    ev_pbc = []
    for ch in range(4):
        e_b = P.emit("tensor", lambda q, ch=ch: q.matmul(
            psum[6].ap()[:, :], ones1f[:],
            posrow.ap()[0:1, ch * 512:(ch + 1) * 512],
            start=True, stop=True),
            [e_prw, ev_consts, ev_pbc[-1] if ev_pbc else None], "sPE")
        ev_pbc.append(P.emit("vector", lambda q, ch=ch: q.tensor_copy(
            pos_bc[:, ch * 512:(ch + 1) * 512], psum[6].ap()[:, :]),
            [e_b], "sV"))
    ev_posbc = ev_pbc[-1]

    ph12.close()
    g = sb("g", 128, 16 * 1024, BF16)
    gv = g.ap().rearrange("p (b c) -> p b c", c=1024)
    ph3 = ExitStack()
    m1s = sb("m1s", 128, 512, BF16, ph3, side="right")
    posadj = sb("posadj", 128, 16, F32, ph3, side="right")
    xns = sb("xns", 128, 8 * 512, BF16, ph3, side="right")
    w1st = [sb(f"w1st{i}", 128, 8 * 256, F32, ph3, side="right") for i in range(2)]
    w1bf = [sb(f"w1bf{i}", 128, 8 * 256, BF16, ph3, side="right") for i in range(2)]
    ghf = sb("ghf", 128, 32 * 512, BF16, ph3, side="right")
    gx2 = sb("gx2", 128, 1024, F32, ph3, side="right")
    gxa = sb("gxa", 128, 1024, F32, ph3, side="right")
    gxs = gx2
    w2st = [sb(f"w2st{i}", 128, 1024, F32, ph3, side="right") for i in range(2)]
    w2bf = [sb(f"w2bf{i}", 128, 1024, BF16, ph3, side="right") for i in range(2)]
    ysb = sb("ysb", 128, 4 * 1024, BF16, ph3, side="right")
    m1t = [sb(f"m1t{i}", 128, 128, BF16, ph3, side="right") for i in range(4)]
    xnsv = xns.ap().rearrange("p (b c) -> p b c", c=512)
    w1sv = [w.ap().rearrange("p (b c) -> p b c", c=256) for w in w1st]
    w1bv = [w.ap().rearrange("p (b c) -> p b c", c=256) for w in w1bf]
    ghv = ghf.ap().rearrange("p (b c) -> p b c", c=512)
    ysv = ysb.ap().rearrange("p (b c) -> p b c", c=1024)

    bar12 = [P.tip("sV"), P.tip("sA"), P.tip("sPE"), P.tip("sD"),
             P.tip("sD3"), P.tip("sD2")]
    e_gz = P.emit("vector", lambda q: q.memset(g[:], 0.0), [], "sV")
    ev_g = [e_gz] * 16
    ev_m1_free = bar12[0]
    ev_xns_free = None
    ev_w1free = list(bar12[:2])
    ev_w2free = list(bar12[:2])
    ev_gh_w2done = None
    ev_y_used = None
    ev_gelu_free = [None, None]
    ev_psum_free = {i: None for i in range(8)}
    ev_psum_free[6] = ev_posbc
    for e in range(E_):
        ev_sort = []
        for tb in range(16):
            e_a = P.emit("vector", lambda q, tb=tb, e=e: q.tensor_scalar(
                posadj[:, tb:tb + 1], pos[:, tb:tb + 1], -512.0 * e, None,
                ALU.add),
                (bar12 if e == 0 else []) + [ev_pos[tb] if e == 0 else None],
                "sV")
            e_m1 = P.emit("vector", lambda q, tb=tb: q.tensor_scalar(
                m1s[:], iota512[:], posadj[:, tb:tb + 1], None,
                ALU.is_equal), [e_a, ev_consts2, ev_m1_free], "sV")
            e_mm = None
            for db in range(8):
                e_mm = P.emit("tensor", lambda q, db=db, tb=tb: q.matmul(
                    psum[db].ap()[:, :], xv[:, tb, db * 128:(db + 1) * 128],
                    m1s[:], start=(tb == 0), stop=(tb == 15),
                    skip_group_check=True),
                    [e_m1, ev_xn[tb] if e == 0 else None, ev_xns_free,
                     ev_psum_free[db]], "sPE" if db == 7 else None)
            ev_m1_free = e_mm
        e_srt = P.tip("sPE")
        for db in range(8):
            ev_sort.append(P.emit("vector", lambda q, db=db: q.tensor_copy(
                xnsv[:, db, :], psum[db].ap()[:, :]), [e_srt], "sV"))
            ev_psum_free[db] = ev_sort[-1]

        ev_gelu_done = []
        for sl in range(16):
            sbi = sl % 2
            h0 = sl * 256
            e_w1d = P.dma("gpsimd", w1sv[sbi],
                          w1_h[e, :, h0:h0 + 256]
                          .rearrange("(b p) c -> p b c", p=128),
                          (bar12 if (e == 0 and sl < 2) else [])
                          + [ev_w1free[sbi]], "sD3")
            e_w1c = P.emit("vector", lambda q, sbi=sbi: q.tensor_copy(
                w1bf[sbi][:], w1st[sbi][:]), [e_w1d], "sV")
            for hb in range(2):
                hloc = sl * 2 + hb
                pb = hloc % 2
                e_mm = None
                for db in range(8):
                    e_mm = P.emit("tensor",
                                  lambda q, db=db, sbi=sbi, hb=hb, pb=pb:
                                  q.matmul(
                        psum[pb].ap()[:, :],
                        w1bv[sbi][:, db, hb * 128:(hb + 1) * 128],
                        xnsv[:, db, :], start=(db == 0), stop=(db == 7)),
                        [e_w1c, ev_sort[db], ev_psum_free[pb],
                         ev_gh_w2done], "sPE" if db == 7 else None)
                par = hloc % 2
                cs = slice(par * 512, par * 512 + 512)
                e1 = P.emit("scalar", lambda q, pb=pb, cs=cs: q.activation(
                    gx2[:, cs], psum[pb].ap()[:, :], AF.Square),
                    [e_mm, ev_gelu_free[par]], "sA")
                e2 = P.emit("vector", lambda q, cs=cs: q.tensor_scalar(
                    gxa[:, cs], gx2[:, cs], GC1, 1.0, ALU.mult, ALU.add),
                    [e1], "sV")
                e3 = P.emit("vector", lambda q, pb=pb, cs=cs: q.tensor_tensor(
                    gxs[:, cs], gxa[:, cs], psum[pb].ap()[:, :], ALU.mult),
                    [e2], "sV")
                e4 = P.emit("scalar", lambda q, cs=cs: q.activation(
                    gxa[:, cs], gxs[:, cs], AF.Sigmoid, scale=GC2),
                    [e3], "sA")
                e5 = P.emit("vector", lambda q, pb=pb, cs=cs, hloc=hloc:
                            q.tensor_tensor(
                    ghv[:, hloc, :], gxa[:, cs], psum[pb].ap()[:, :],
                    ALU.mult), [e4, ev_gh_w2done], "sV")
                ev_psum_free[pb] = e5
                ev_gelu_free[par] = e5
                ev_gelu_done.append(e5)
            ev_w1free[sbi] = ev_gelu_done[-1]

        e_mm = None
        for hbw in range(32):
            sbi = hbw % 2
            e_w2d = P.dma("sync", w2st[sbi][:],
                          w2_h[e, hbw * 128:(hbw + 1) * 128, :],
                          (bar12 if (e == 0 and hbw < 2) else [])
                          + [ev_w2free[sbi]], "sD")
            e_w2c = P.emit("vector", lambda q, sbi=sbi: q.tensor_copy(
                w2bf[sbi][:], w2st[sbi][:]), [e_w2d], "sV")
            for i4 in range(4):
                for dh in range(2):
                    pb = i4 * 2 + dh
                    e_mm = P.emit("tensor",
                                  lambda q, sbi=sbi, hbw=hbw, i4=i4, dh=dh,
                                  pb=pb: q.matmul(
                        psum[pb].ap()[:, :],
                        ghv[:, hbw, i4 * 128:(i4 + 1) * 128],
                        w2bf[sbi][:, dh * 512:(dh + 1) * 512],
                        start=(hbw == 0), stop=(hbw == 31),
                        skip_group_check=True),
                        [e_w2c, ev_gelu_done[hbw], ev_psum_free[pb],
                         ev_y_used],
                        "sPE" if (hbw == 31 or (i4 == 3 and dh == 1))
                        else None)
            ev_w2free[sbi] = e_mm
        ev_gh_w2done = e_mm
        e_pe_done = P.tip("sPE")
        ev_yev = []
        for i4 in range(4):
            for dh in range(2):
                pb = i4 * 2 + dh
                e_ev = P.emit("vector", lambda q, i4=i4, dh=dh, pb=pb:
                              q.tensor_copy(
                    ysv[:, i4, dh * 512:(dh + 1) * 512],
                    psum[pb].ap()[:, :]), [e_pe_done, ev_y_used], "sV")
                ev_psum_free[pb] = e_ev
                ev_yev.append(e_ev)
        e_y = ev_yev[-1]
        ev_un = []
        prev_grp = None
        for tb in range(16):
            for dh in range(2):
                pb = 4 + (tb * 2 + dh) % 2
                e_mm = None
                for ib in range(4):
                    mb = m1t[ib]
                    if dh == 0:
                        e_mt = P.emit("vector",
                                      lambda q, tb=tb, ib=ib, e=e, mb=mb:
                                      q.tensor_scalar(
                            mb[:], pos_bc[:, tb * 128:(tb + 1) * 128],
                            iotac[:, 4 * e + ib:4 * e + ib + 1], None,
                            ALU.is_equal),
                            [ev_posbc, ev_consts2, prev_grp], "sV")
                    else:
                        e_mt = None
                    e_mm = P.emit("tensor", lambda q, ib=ib, dh=dh, mb=mb,
                                  pb=pb: q.matmul(
                        pb_ap(psum, pb), mb[:],
                        ysv[:, ib, dh * 512:(dh + 1) * 512],
                        start=(ib == 0), stop=(ib == 3)),
                        [e_mt, e_y, ev_psum_free[pb]],
                        "sPE" if ib == 3 else None)
                prev_grp = e_mm if dh == 1 else prev_grp
                e_add = P.emit("vector", lambda q, tb=tb, dh=dh, pb=pb:
                               q.tensor_tensor(
                    gv[:, tb, dh * 512:(dh + 1) * 512], pb_ap(psum, pb),
                    gv[:, tb, dh * 512:(dh + 1) * 512], ALU.add),
                    [e_mm, ev_g[tb]], "sV")
                ev_psum_free[pb] = e_add
                ev_g[tb] = e_add
                ev_un.append(e_add)
        ev_y_used = ev_un[-1]
        ev_xns_free = P.tip("sPE")
    ev_silu = [None] * 16
    for tb in range(16):
        scr = gx2 if tb % 2 == 0 else gxa
        e_s = P.emit("scalar", lambda q, tb=tb, scr=scr: q.activation(
            scr[:, 0:1024], gv[:, tb, :], AF.Sigmoid),
            [ev_g[tb], ev_silu[tb - 2] if tb >= 2 else None], "sA")
        ev_silu[tb] = P.emit("vector", lambda q, tb=tb, scr=scr: q.tensor_tensor(
            gv[:, tb, :], gv[:, tb, :], scr[:, 0:1024], ALU.mult),
            [e_s], "sV")

    ph3.close()
    ph123.close()

    ffn_bar = [P.tip("sV"), P.tip("sA"), P.tip("sPE"), P.tip("sG"),
               P.tip("sD3"), P.tip("sD"), P.tip("sD2")]
    xnT = sb("xnT", 128, 8 * 4096, BF16)
    xnTv = xnT.ap().rearrange("p (b c) -> p b c", c=4096)
    for t in range(32):
        for db in range(8):
            P.dmat("sync" if (t + db) % 2 == 0 else "scalar",
                   xnTv[:, db, t * 128:(t + 1) * 128],
                   xv[:, t, db * 128:(db + 1) * 128], ffn_bar + [ev_xn[t]],
                   "sD" if (t + db) % 2 == 0 else "sD2")
    ev_xnT = [P.tip("sD"), P.tip("sD2")]
    phx.close()

    aoscr_h = dt("aoscr", [8, 128, TOK], BF16, kind="Internal").ap()
    ph5 = ExitStack()
    wst = sb("wst", 128, 512, F32, ph5, side="right")
    wsb = sb("wsb", 128, 1024, BF16, ph5, side="right")
    wqT = sb("wqT", 128, 8 * 128, BF16, ph5, side="right")
    wkT = sb("wkT", 128, 8 * 128, BF16, ph5, side="right")
    wvT = sb("wvT", 128, 8 * 128, BF16, ph5, side="right")
    ktp = sb("ktp", 128, 4096, BF16, ph5, side="right")
    vbuf = sb("vbuf", 128, 32 * 128, BF16, ph5, side="right")
    qtp = sb("qtp", 128, 256, BF16, ph5, side="right")
    zb = sb("zb", 128, 12288, F32, ph5, side="right")
    ratt = [sb(f"ratt{i}", 128, 4352, BF16, ph5, side="right") for i in range(2)]
    attT = sb("attT", 128, 4352, BF16, ph5, side="right")
    p8 = sb("p8", 128, 1536, BF16, ph5, side="right")
    rp = sb("rp", 128, 1536, BF16, ph5, side="right")
    tau2 = sb("tau2", 128, 8, BF16, ph5, side="right")
    tauf = sb("tauf", 128, 8, F32, ph5, side="right")
    tau_hl = [sb(f"tau_hl{i}", 2, 128, BF16, ph5, side="right")
              for i in range(4)]
    ones2 = sb("ones2", 2, 512, BF16, ph5, side="right")
    aot = [sb(f"aot{i}", 128, 128, BF16, ph5, side="right") for i in range(2)]
    gx_dummy = sb("gx_dummy", 128, 128, F32, ph5, side="right")
    wqTv = wqT.ap().rearrange("p (b c) -> p b c", c=128)
    wkTv = wkT.ap().rearrange("p (b c) -> p b c", c=128)
    wvTv = wvT.ap().rearrange("p (b c) -> p b c", c=128)
    vbv = vbuf.ap().rearrange("p (b c) -> p b c", c=128)

    e_o2 = P.emit("vector", lambda q: q.memset(ones2[:], 1.0), ffn_bar, "sV")

    ev_proj_free = None
    ev_zfree = {}
    ev_p8_free = {}
    ev_rp_free = {}
    ev_prevc = {}
    ev_aot_free = [None, None]
    QKB = [0, 1, 2, 3]
    PVB = [4, 5]
    e_lastpv = None
    ev_out_dma = []
    for pr in range(8):
        ev_w = []
        e_c = None
        for (wh, wTbv) in ((wq_h, wqTv), (wk_h, wkTv), (wv_h, wvTv)):
            for chh in range(2):
                e_d = P.dma("gpsimd", wst[:],
                            wh[pr * 128:(pr + 1) * 128,
                               chh * 512:(chh + 1) * 512],
                            ffn_bar + ev_xnT +
                            [e_c, ev_w[-1] if ev_w else ev_proj_free], "sD3")
                e_c = P.emit("vector", lambda q, chh=chh: q.tensor_copy(
                    wsb[:, chh * 512:(chh + 1) * 512], wst[:]), [e_d], "sV")
            for db in range(8):
                P.dmat("scalar", wTbv[:, db, :],
                       wsb[:, db * 128:(db + 1) * 128], [e_c], "sD2")
            ev_w.append(P.tip("sD2"))
        ev_wT = ev_w[-1]
        ev_kt = []
        for ch in range(8):
            pb = ch % 2
            e_mm = None
            for db in range(8):
                e_mm = P.emit("tensor", lambda q, ch=ch, db=db, pb=pb: q.matmul(
                    pb_ap(psum, pb), wkTv[:, db, :],
                    xnTv[:, db, ch * 512:(ch + 1) * 512],
                    start=(db == 0), stop=(db == 7)),
                    ([ev_wT, ev_proj_free] + ev_xnT +
                     [ev_psum_free.get(pb)]), "sPE" if db == 7 else None)
            e_ev = P.emit("vector", lambda q, ch=ch, pb=pb: q.tensor_copy(
                ktp[:, ch * 512:(ch + 1) * 512], pb_ap(psum, pb)), [e_mm], "sV")
            ev_psum_free[pb] = e_ev
            ev_kt.append(e_ev)
        ev_vt = []
        for tb in range(32):
            pb = 2 + tb % 2
            e_mm = None
            for db in range(8):
                e_mm = P.emit("tensor", lambda q, tb=tb, db=db, pb=pb: q.matmul(
                    pb_ap(psum, pb, 128), xnTv[:, db, tb * 128:(tb + 1) * 128],
                    wvTv[:, db, :], start=(db == 0), stop=(db == 7)),
                    [ev_wT, ev_proj_free, ev_psum_free.get(pb)],
                    "sPE" if db == 7 else None)
            e_ev = P.emit("vector", lambda q, tb=tb, pb=pb: q.tensor_copy(
                vbv[:, tb, :], pb_ap(psum, pb, 128)), [e_mm], "sV")
            ev_psum_free[pb] = e_ev
            ev_vt.append(e_ev)
        ev_proj = ev_kt + ev_vt + [ev_wT, e_o2]

        for qp in range(8):
            W1, W2 = 128 * (qp + 1), 128 * (16 - qp)
            qbs = [qp, 15 - qp]
            Ws = [W1, W2]
            chains = [(0, 0), (0, 1), (1, 0), (1, 1)]
            zos = {(h, qx): (h * 2048 if qx == 0 else 4096 + h * 4096)
                   for h, qx in chains}
            ros = {(h, qx): (0 if qx == 0 else 2 * W1) for h, qx in chains}
            pos8 = {(h, qx): (h * 256 if qx == 0 else 512 + h * 512)
                    for h, qx in chains}
            ev_qt = []
            for qx in range(2):
                pb = PVB[qx]
                e_mm = None
                for db in range(8):
                    e_mm = P.emit("tensor", lambda q, qx=qx, db=db, pb=pb,
                                  i=qbs[qx]: q.matmul(
                        pb_ap(psum, pb, 128), wqTv[:, db, :],
                        xnTv[:, db, i * 128:(i + 1) * 128],
                        start=(db == 0), stop=(db == 7)),
                        ev_proj + [ev_psum_free.get(pb)],
                        "sPE" if db == 7 else None)
                e_ev = P.emit("vector", lambda q, qx=qx, pb=pb: q.tensor_scalar(
                    qtp[:, qx * 128:(qx + 1) * 128], pb_ap(psum, pb, 128),
                    0.125, None, ALU.mult),
                    [e_mm, ev_qt[-1] if ev_qt else None], "sV")
                ev_psum_free[pb] = e_ev
                ev_qt.append(e_ev)
            ev_evac = {}
            for (h, qx) in chains:
                c = 2 * h + qx
                i = qbs[qx]
                nb = i + 1
                zo = zos[(h, qx)]
                pb = QKB[c]
                qsl = slice(64 * h, 64 * h + 64)
                prev_evac = ev_psum_free.get(pb)
                ev_evac[c] = []
                for rng in range(2):
                    kbase = rng * 16
                    ro = zo + rng * Ws[qx]
                    for cst in range(0, nb, 4):
                        cw = min(4, nb - cst) * 128
                        e_mm = P.emit(
                            "tensor",
                            lambda q, qx=qx, cst=cst, cw=cw, kbase=kbase,
                            pb=pb, qsl=qsl, h=h: q.matmul(
                                pb_ap(psum, pb, cw),
                                qtp.ap()[qsl, qx * 128:(qx + 1) * 128],
                                ktp.ap()[qsl,
                                         (kbase + cst) * 128:
                                         (kbase + cst) * 128 + cw],
                                start=True, stop=True,
                                tile_position=(64 * h, 0)),
                            ev_qt + [prev_evac], "sPE")
                        last = cst + cw // 128 == nb
                        zwar = ev_zfree.get(c)
                        if not last:
                            e_ev = P.emit(
                                "scalar",
                                lambda q, ro=ro, cst=cst, cw=cw, pb=pb:
                                q.activation(
                                    zb[:, ro + cst * 128:ro + cst * 128 + cw],
                                    pb_ap(psum, pb, cw), AF.Identity),
                                [e_mm, zwar], "sA")
                        else:
                            e_ev0 = None
                            if cw > 128:
                                e_ev0 = P.emit(
                                    "scalar",
                                    lambda q, ro=ro, cst=cst, cw=cw, pb=pb:
                                    q.activation(
                                        zb[:, ro + cst * 128:
                                           ro + cst * 128 + cw - 128],
                                        pb_ap(psum, pb, cw - 128),
                                        AF.Identity), [e_mm, zwar], "sA")
                                ev_evac[c].append(e_ev0)
                            mask = diagm if rng == 0 else peerm
                            e_ev = P.emit(
                                "vector",
                                lambda q, ro=ro, i=i, pb=pb, cw=cw,
                                mask=mask: q.tensor_tensor(
                                    zb[:, ro + i * 128:ro + (i + 1) * 128],
                                    psum[pb].ap()[:, cw - 128:cw],
                                    mask[:], ALU.add),
                                [e_mm, e_ev0, zwar], "sV")
                        ev_evac[c].append(e_ev)
                        prev_evac = e_ev
                ev_psum_free[pb] = prev_evac
            ev_upd = {}
            for (h, qx) in chains:
                c = 2 * h + qx
                Wx2 = 2 * Ws[qx]
                zo = zos[(h, qx)]
                po = pos8[(h, qx)]
                e_p8 = P.emit("vector", lambda q, zo=zo, po=po, Wx2=Wx2:
                              q.tensor_reduce(
                    p8[:, po:po + Wx2 // 8],
                    zb.ap()[:, zo:zo + Wx2]
                    .rearrange("p (n w) -> p n w", w=8), AX.X, ALU.max),
                    ev_evac[c] + [ev_p8_free.get(c)], "sV")
                e_mx = P.emit("vector", lambda q, po=po, Wx2=Wx2, c=c:
                              q.tensor_reduce(
                    small[:, 40 + c:41 + c], p8[:, po:po + Wx2 // 8],
                    AX.X, ALU.max), [e_p8], "sV")
                ev_upd[c] = P.emit("vector", lambda q, c=c: q.tensor_scalar(
                    tfc[:, c:c + 1], small[:, 40 + c:41 + c], -1.0, 1.0,
                    ALU.mult, ALU.add), [e_mx], "sV")
            ev_c = dict(ev_prevc)
            for j in range(JP + JF):
                pooled = j < JP
                prev_c = dict(ev_c)
                ev_f = {}
                ev_c = {}
                for (h, qx) in chains:
                    c = 2 * h + qx
                    sib = 2 * h + (1 - qx)
                    Wx2 = 2 * Ws[qx]
                    zo = zos[(h, qx)]
                    ro = ros[(h, qx)]
                    po = pos8[(h, qx)]
                    if pooled:
                        srcap = p8.ap()[:, po:po + Wx2 // 8]
                        dstap = rp.ap()[:, po:po + Wx2 // 8]
                    else:
                        srcap = zb.ap()[:, zo:zo + Wx2]
                        dstap = ratt[h].ap()[:, ro:ro + Wx2]
                    ev_f[c] = P.emit(
                        "scalar", lambda q, srcap=srcap, dstap=dstap, c=c:
                        q.activation(dstap, srcap, AF.Relu,
                                     bias=tfc[:, c:c + 1],
                                     accum_out=tfc[:, 8 + c:9 + c]),
                        [ev_upd[c], prev_c.get(c),
                         prev_c.get(sib) if not pooled else None,
                         ev_rp_free.get(c) if pooled else None], "sA")
                    if pooled:
                        ev_p8_free[c] = ev_f[c]
                    ev_c[c] = P.emit(
                        "vector", lambda q, dstap=dstap, c=c:
                        q.tensor_scalar(
                            dstap, dstap, 0.0, None, ALU.is_gt, ALU.add,
                            accum_out=tfc[:, 16 + c:17 + c]),
                        [ev_f[c]], "sV")
                    if pooled:
                        ev_rp_free[c] = ev_c[c]
                    if qx == 1:
                        cl0 = 2 * h
                        e_rc = P.emit("vector", lambda q, cl0=cl0: q.reciprocal(
                            tfc[:, 24 + cl0:26 + cl0], tfc[:, 16 + cl0:18 + cl0]),
                            [ev_c[cl0], ev_c[cl0 + 1]], "sV")
                        e_d = P.emit("vector", lambda q, cl0=cl0:
                                     q.scalar_tensor_tensor(
                            small[:, 48 + cl0:50 + cl0],
                            tfc[:, 8 + cl0:10 + cl0], -1.0,
                            tfc[:, 24 + cl0:26 + cl0],
                            ALU.add, ALU.mult), [e_rc], "sV")
                        e_u = P.emit("vector", lambda q, cl0=cl0: q.tensor_tensor(
                            tfc[:, cl0:cl0 + 2], tfc[:, cl0:cl0 + 2],
                            small[:, 48 + cl0:50 + cl0], ALU.subtract),
                            [e_d], "sV")
                        ev_upd[cl0] = e_u
                        ev_upd[cl0 + 1] = e_u
            ev_prevc = dict(ev_c)
            ev_thl = {}
            prev_tp5 = None
            for (h, qx) in chains:
                c = 2 * h + qx
                e1 = P.emit("vector", lambda q, c=c: q.tensor_copy(
                    tau2[:, 2 * c:2 * c + 1], tfc[:, c:c + 1]),
                    [ev_upd[c], ev_c[c]], "sV")
                e2 = P.emit("vector", lambda q, c=c: q.tensor_copy(
                    tauf[:, 2 * c:2 * c + 1], tau2[:, 2 * c:2 * c + 1]),
                    [e1], "sV")
                e3 = P.emit("vector", lambda q, c=c: q.tensor_tensor(
                    tauf[:, 2 * c + 1:2 * c + 2], tfc[:, c:c + 1],
                    tauf[:, 2 * c:2 * c + 1], ALU.subtract), [e2], "sV")
                e_tp = P.emit("tensor", lambda q, c=c: q.transpose(
                    psum[6].ap()[0:2, 0:128], tauf[:, 2 * c:2 * c + 2],
                    ident[:]),
                    [e3, prev_tp5, ev_psum_free.get(6)], "sPE")
                e_ev = P.emit("vector", lambda q, c=c: q.tensor_copy(
                    tau_hl[c][:], psum[6].ap()[0:2, 0:128]), [e_tp], "sV")
                prev_tp5 = e_ev
                ev_psum_free[6] = e_ev
                ev_thl[c] = e_ev
            for (h, qx) in chains:
                c = 2 * h + qx
                i = qbs[qx]
                nb = i + 1
                ro = ros[(h, qx)]
                pb = QKB[c]
                qsl = slice(64 * h, 64 * h + 64)
                ev_qk2 = []
                for rng in range(2):
                    kbase = rng * 16
                    ro2 = ro + rng * Ws[qx]
                    for cst in range(0, nb, 4):
                        cw = min(4, nb - cst) * 128
                        for j4 in range(cw // 128):
                            kb = kbase + cst + j4
                            masked = (kb == kbase + i)
                            P.emit("tensor",
                                   lambda q, kb=kb, j4=j4, pb=pb, qsl=qsl,
                                   h=h, qx=qx: q.matmul(
                                psum[pb].ap()[:, j4 * 128:(j4 + 1) * 128],
                                ktp.ap()[qsl, kb * 128:(kb + 1) * 128],
                                qtp.ap()[qsl, qx * 128:(qx + 1) * 128],
                                start=True, stop=False,
                                tile_position=(64 * h, 0),
                                skip_group_check=True),
                                [ev_thl[c], ev_psum_free.get(pb)], None)
                            P.emit("tensor",
                                   lambda q, j4=j4, pb=pb, c=c, masked=masked:
                                   q.matmul(
                                psum[pb].ap()[:, j4 * 128:(j4 + 1) * 128],
                                ones2[0:2, 0:128], tau_hl[c][:],
                                start=False, stop=not masked,
                                tile_position=(0, 0),
                                skip_group_check=True),
                                [ev_thl[c]],
                                None if masked else
                                ("sPE" if j4 == cw // 128 - 1 else None))
                            if masked:
                                mb_ = diagmb if rng == 0 else peermb
                                P.emit("tensor",
                                       lambda q, j4=j4, pb=pb, mb_=mb_:
                                       q.matmul(
                                    psum[pb].ap()[:, j4 * 128:(j4 + 1) * 128],
                                    mb_[:], identb[:],
                                    start=False, stop=True,
                                    tile_position=(0, 0),
                                    skip_group_check=True),
                                    [ev_consts],
                                    "sPE" if j4 == cw // 128 - 1 else None)
                        e_mm2 = P.tip("sPE")
                        eng = "scalar" if (cst // 4) % 2 == 0 else "vector"
                        if eng == "scalar":
                            e_ev2 = P.emit(
                                "scalar",
                                lambda q, ro2=ro2, cst=cst, cw=cw, pb=pb:
                                q.activation(
                                    attT[:, ro2 + cst * 128:
                                         ro2 + cst * 128 + cw],
                                    pb_ap(psum, pb, cw), AF.Relu),
                                [e_mm2], "sA")
                        else:
                            e_ev2 = P.emit(
                                "vector",
                                lambda q, ro2=ro2, cst=cst, cw=cw, pb=pb:
                                q.tensor_scalar(
                                    attT[:, ro2 + cst * 128:
                                         ro2 + cst * 128 + cw],
                                    pb_ap(psum, pb, cw), 0.0, None, ALU.max),
                                [e_mm2], "sV")
                        ev_qk2.append(e_ev2)
                        ev_psum_free[pb] = e_ev2
                pvb = PVB[qx]
                tbs = list(range(0, i + 1)) + list(range(16, 16 + i + 1))
                e_mm = None
                for jb, tb in enumerate(tbs):
                    e_mm = P.emit("tensor", lambda q, tb=tb, jb=jb, ro=ro,
                                  h=h, pvb=pvb: q.matmul(
                        psum[pvb].ap()[64 * h:64 * h + 64, 0:128],
                        vbv[:, tb, 64 * h:64 * h + 64],
                        attT[:, ro + jb * 128:ro + (jb + 1) * 128],
                        start=(jb == 0), stop=(jb == len(tbs) - 1),
                        skip_group_check=True),
                        ev_qk2 + [ev_psum_free.get((pvb, h))],
                        "sPE" if jb == len(tbs) - 1 else None)
                e_lastpv = e_mm
                if h == 1:
                    qb = qbs[qx]
                    bi = qx
                    e_ev = P.emit("vector", lambda q, bi=bi, pvb=pvb:
                                  q.tensor_copy(aot[bi][:],
                                                psum[pvb].ap()[:, 0:128]),
                        [e_mm, ev_aot_free[bi]], "sV")
                    ev_psum_free[(pvb, 0)] = e_ev
                    ev_psum_free[(pvb, 1)] = e_ev
                    ev_psum_free[pvb] = e_ev
                    e_st = P.dma("sync", aoscr_h[pr, :, qb * 128:(qb + 1) * 128],
                                 aot[bi][:], [e_ev], "sD")
                    ev_aot_free[bi] = e_st
                    ev_out_dma.append(e_st)
        ev_proj_free = e_lastpv

    att_bar = [P.tip("sV"), P.tip("sA"), P.tip("sPE"), P.tip("sG"),
               P.tip("sD3"), P.tip("sD"), P.tip("sD2")]
    ph5.close()
    ph6 = ExitStack()
    woT = sb("woT", 128, 8 * 1024, BF16, ph6, side="right")
    wst6 = sb("wst6", 128, 512, F32, ph6, side="right")
    wsb6 = sb("wsb6", 128, 1024, BF16, ph6, side="right")
    woTv = woT.ap().rearrange("p (b c) -> p b c", c=1024)
    ev_wo = []
    e_c = None
    for sl in range(8):
        for chh in range(2):
            e_d = P.dma("gpsimd", wst6[:],
                        wo_h[sl * 128:(sl + 1) * 128,
                             chh * 512:(chh + 1) * 512],
                        att_bar + [e_c, ev_wo[-1] if ev_wo else None], "sD3")
            e_c = P.emit("vector", lambda q, chh=chh: q.tensor_copy(
                wsb6[:, chh * 512:(chh + 1) * 512], wst6[:]), [e_d], "sV")
        for db in range(8):
            P.dmat("scalar", woTv[:, db, sl * 128:(sl + 1) * 128],
                   wsb6[:, db * 128:(db + 1) * 128], [e_c], "sD2")
        ev_wo.append(P.tip("sD2"))
    ev_woT = ev_wo[-1]
    xres = [sb(f"xres{i}", 128, 1024, F32, ph6, side="right") for i in range(2)]
    osb = [sb(f"osb{i}", 128, 1024, F32, ph6, side="right") for i in range(2)]
    aoin = [sb(f"aoin{i}", 128, 8 * 128, BF16, ph6, side="right")
            for i in range(2)]
    aov = [a.ap().rearrange("p (b c) -> p b c", c=128) for a in aoin]
    ev_osb_free = [None, None]
    ev_aoin_free = [None, None]
    ev_xres = [None, None]
    ev_aoin = [None, None]
    for qb in range(16):
        bi = qb % 2
        ev_xres[bi] = P.dma("sync", xres[bi][:], x_h[qb * 128:(qb + 1) * 128, :],
                            att_bar + [ev_osb_free[bi]], "sD")
        ev_aoin[bi] = P.dma("sync", aov[bi],
                            aoscr_h[:, :, qb * 128:(qb + 1) * 128]
                            .rearrange("b p c -> p b c"),
                            att_bar + ev_out_dma[-1:] + [ev_aoin_free[bi]],
                            "sD")
        for dh in range(2):
            pb = 6 + dh
            e_mm = None
            for eb in range(8):
                e_mm = P.emit("tensor", lambda q, eb=eb, qb=qb, dh=dh, pb=pb,
                              bi=bi: q.matmul(
                    psum[pb].ap()[:, :], aov[bi][:, eb, :],
                    woTv[:, eb, dh * 512:(dh + 1) * 512],
                    start=(eb == 0), stop=(eb == 7)),
                    [ev_aoin[bi], ev_woT, ev_psum_free.get(pb)],
                    "sPE" if eb == 7 else None)
            e_m = P.emit("vector", lambda q, qb=qb, dh=dh, bi=bi, pb=pb:
                         q.tensor_tensor(
                osb[bi][:, dh * 512:(dh + 1) * 512], psum[pb].ap()[:, :],
                gv[:, qb, dh * 512:(dh + 1) * 512], ALU.mult),
                [e_mm, ev_silu[qb], ev_osb_free[bi]], "sV")
            e_a = P.emit("vector", lambda q, qb=qb, dh=dh, bi=bi:
                         q.tensor_tensor(
                osb[bi][:, dh * 512:(dh + 1) * 512],
                osb[bi][:, dh * 512:(dh + 1) * 512],
                xres[bi][:, dh * 512:(dh + 1) * 512], ALU.add),
                [e_m, ev_xres[bi]], "sV")
            ev_psum_free[pb] = e_a
        e_st = P.dma("sync", out_h[qb * 128:(qb + 1) * 128, :], osb[bi][:],
                     [P.tip("sV")], "sO")
        ev_osb_free[bi] = e_st
        ev_aoin_free[bi] = e_mm
    P.emit("gpsimd", lambda q: q.nop(), [P.tip("sO")], None)

    ph6.close()
    P.realize(blk, sems)
    st.close()
    return nc


def pb_ap(psum, pb, w=512):
    return psum[pb].ap()[:, 0:w]


_cache = {}


def kernel(L0_attn_x, L0_attn_Wq, L0_attn_Wk, L0_attn_Wv, L0_attn_Wo,
           L0_ffn_W1, L0_ffn_W2, L0_ffn_Wg, L0_ffn_bg):
    bf = ml_dtypes.bfloat16
    x = np.ascontiguousarray(np.asarray(L0_attn_x, np.float32))
    consts = {}
    p = np.arange(128)[:, None]
    c = np.arange(128)[None, :]
    consts["diagm"] = np.where(c <= p, 0.0, NEGC).astype(np.float32)
    consts["diagmb"] = np.where(c <= p, 0.0, NEGC).astype(bf)
    consts["identb"] = np.eye(128).astype(bf)
    consts["u128"] = (np.arange(128)[:, None] < np.arange(128)[None, :]) \
        .astype(bf)
    consts["u16"] = (np.arange(16)[:, None] < np.arange(16)[None, :]) \
        .astype(np.float32)
    bselm = np.zeros((128, 16, 16), bf)
    for tb in range(16):
        bselm[:, tb, tb] = 1
    consts["bsel"] = bselm.reshape(128, 256)
    consts["ones1"] = np.ones((1, 128), bf)
    consts["ones1f"] = np.ones((1, 128), np.float32)
    consts["onesrow"] = np.ones((1, 512), bf)
    consts["ecrow"] = (np.arange(E_) * float(CAP)).astype(bf)[None, :]
    consts["iota512"] = np.broadcast_to(
        np.arange(512, dtype=np.float32)[None, :], (128, 512)).copy()
    consts["iotac"] = (np.arange(128)[:, None]
                       + 128.0 * np.arange(32)[None, :]).astype(np.float32)
    consts["ident"] = np.eye(128, dtype=np.float32)

    nc = build_program()
    in_maps = []
    for core in range(8):
        b, par = core // 2, core % 2
        xb = x[b].reshape(32, 128, D_)
        own = xb[par::2].reshape(TOK, D_)
        peer = xb[1 - par::2].reshape(TOK, D_)
        im = dict(consts)
        im["x"] = np.ascontiguousarray(np.concatenate([own, peer], 0))
        im["wq"] = np.asarray(L0_attn_Wq, np.float32)
        im["wk"] = np.asarray(L0_attn_Wk, np.float32)
        im["wv"] = np.asarray(L0_attn_Wv, np.float32)
        im["wo"] = np.asarray(L0_attn_Wo, np.float32)
        im["w1"] = np.asarray(L0_ffn_W1, np.float32)
        im["w2"] = np.asarray(L0_ffn_W2, np.float32)
        im["wg"] = np.asarray(L0_ffn_Wg, np.float32)
        im["bg"] = np.asarray(L0_ffn_bg, np.float32)[None, :]
        im["peerm"] = np.full((128, 128), NEGC if par == 0 else 0.0,
                              np.float32)
        im["peermb"] = im["peerm"].astype(bf)
        in_maps.append(im)
    res = run_bass_kernel_spmd(nc, in_maps, core_ids=list(range(8)))
    out = np.empty((B_, L_, D_), np.float32)
    for core in range(8):
        b, par = core // 2, core % 2
        oc = np.asarray(res.results[core]["out"]).reshape(16, 128, D_)
        out[b].reshape(32, 128, D_)[par::2] = oc
    return out


# version 18 (final)
# speedup vs baseline: 1.7890x; 1.0567x over previous
import numpy as np
import ml_dtypes
import concourse.bass as bass
import concourse.mybir as mybir
from concourse.bass_utils import run_bass_kernel_spmd
from contextlib import ExitStack

F32 = mybir.dt.float32
BF16 = mybir.dt.bfloat16
AF = mybir.ActivationFunctionType
ALU = mybir.AluOpType
AX = mybir.AxisListType

B_, L_, D_ = 4, 4096, 1024
E_, HID_ = 8, 4096
TOK = 2048
CAP = 512
NEGC = -1.0e9
JP = 6
JF = 2
EPS = 1e-5
GC1 = 0.044715
GC2 = 1.5957691216057308

ENGINES = ("sync", "tensor", "vector", "scalar", "gpsimd")


class Prog:
    def __init__(self):
        self.ops = {e: [] for e in ENGINES}
        self.cnt = {s: 0 for s in ("sD", "sD2", "sD3", "sPE", "sV", "sA",
                                   "sG", "sO")}

    def emit(self, eng, fn, waits=(), inc=None, amt=1):
        waits = [w for w in waits if w is not None]
        self.ops[eng].append((waits, fn, inc, amt))
        if inc is not None:
            self.cnt[inc] += amt
            return (inc, self.cnt[inc])
        return None

    def tip(self, sem):
        return (sem, self.cnt[sem])

    def dma(self, eng, out, in_, waits=(), sem="sD", **kw):
        return self.emit(
            eng, lambda q, o=out, i=in_, kw=kw: q.dma_start(out=o, in_=i, **kw),
            waits, sem, 16)

    def dmat(self, eng, out, in_, waits=(), sem="sD"):
        return self.emit(
            eng, lambda q, o=out, i=in_: q.dma_start_transpose(o, i),
            waits, sem, 16)

    def realize(self, block, sems):
        for eng in ENGINES:
            ops = self.ops[eng]
            if not ops:
                continue

            def body(q, ops=ops):
                seen = {}
                for waits, fn, inc, amt in ops:
                    for (s, v) in waits:
                        if seen.get(s, 0) < v:
                            q.wait_ge(sems[s], v)
                            seen[s] = v
                    inst = fn(q)
                    if inc is not None:
                        inst.then_inc(sems[inc], amt)
            getattr(block, eng)(body)


def build_program():
    nc = bass.Bass("TRN2", target_bir_lowering=False, debug=False)
    dt = nc.dram_tensor
    x_h = dt("x", [L_, D_], F32, kind="ExternalInput").ap()
    wq_h = dt("wq", [D_, D_], F32, kind="ExternalInput").ap()
    wk_h = dt("wk", [D_, D_], F32, kind="ExternalInput").ap()
    wv_h = dt("wv", [D_, D_], F32, kind="ExternalInput").ap()
    wo_h = dt("wo", [D_, D_], F32, kind="ExternalInput").ap()
    w1_h = dt("w1", [E_, D_, HID_], F32, kind="ExternalInput").ap()
    w2_h = dt("w2", [E_, HID_, D_], F32, kind="ExternalInput").ap()
    wg_h = dt("wg", [D_, E_], F32, kind="ExternalInput").ap()
    bg_h = dt("bg", [1, E_], F32, kind="ExternalInput").ap()
    diagm_h = dt("diagm", [128, 128], F32, kind="ExternalInput").ap()
    diagmb_h = dt("diagmb", [128, 128], BF16, kind="ExternalInput").ap()
    peermb_h = dt("peermb", [128, 128], BF16, kind="ExternalInput").ap()
    identb_h = dt("identb", [128, 128], BF16, kind="ExternalInput").ap()
    peerm_h = dt("peerm", [128, 128], F32, kind="ExternalInput").ap()
    u128_h = dt("u128", [128, 128], BF16, kind="ExternalInput").ap()
    u16_h = dt("u16", [16, 16], F32, kind="ExternalInput").ap()
    bsel_h = dt("bsel", [128, 16 * 16], BF16, kind="ExternalInput").ap()
    ones1_h = dt("ones1", [1, 128], BF16, kind="ExternalInput").ap()
    ones1f_h = dt("ones1f", [1, 128], F32, kind="ExternalInput").ap()
    onesrow_h = dt("onesrow", [1, 512], BF16, kind="ExternalInput").ap()
    ecrow_h = dt("ecrow", [1, E_], BF16, kind="ExternalInput").ap()
    iota512_h = dt("iota512", [128, 512], F32, kind="ExternalInput").ap()
    iotac_h = dt("iotac", [128, 32], F32, kind="ExternalInput").ap()
    ident_h = dt("ident", [128, 128], F32, kind="ExternalInput").ap()
    out_h = dt("out", [TOK, D_], F32, kind="ExternalOutput").ap()
    xnTd_h = dt("xnTd", [8, 128, L_], BF16, kind="Internal").ap()

    P = Prog()
    st = ExitStack()

    def sb(name, p, f, dtp, stk=None, side="left"):
        return (stk or st).enter_context(
            nc.sbuf_tensor("s_" + name, [p, f], dtp, side=side))

    phx = ExitStack()
    xn = sb("xn", 128, 32 * 1024, BF16, phx, side="right")
    diagm = sb("diagm", 128, 128, F32)
    diagmb = sb("diagmb", 128, 128, BF16)
    peermb = sb("peermb", 128, 128, BF16)
    identb = sb("identb", 128, 128, BF16)
    peerm = sb("peerm", 128, 128, F32)
    ident = sb("ident", 128, 128, F32)
    pos = sb("pos", 128, 16, F32)
    tfc = sb("tfc", 128, 64, F32)
    small = sb("small", 128, 64, F32)

    xv = xn.ap().rearrange("p (b c) -> p b c", c=1024)

    psum = [st.enter_context(nc.psum_tensor(f"pp{i}", [128, 512], F32))
            for i in range(8)]

    semst = st.enter_context(ExitStack())
    sems = {s: semst.enter_context(nc.semaphore("m_" + s))
            for s in ("sD", "sD2", "sD3", "sPE", "sV", "sA", "sG", "sO")}
    blk = st.enter_context(nc.Block())

    ph123 = ExitStack()
    ones1 = sb("ones1", 1, 128, BF16, ph123, side="right")
    ones1f = sb("ones1f", 1, 128, F32, ph123, side="right")
    onesrow = sb("onesrow", 1, 512, BF16, ph123, side="right")
    ecrow = sb("ecrow", 1, E_, BF16, ph123, side="right")
    u128 = sb("u128", 128, 128, BF16, ph123, side="right")
    u16 = sb("u16", 16, 16, F32, ph123, side="right")
    bsel = sb("bsel", 128, 16 * 16, BF16, ph123, side="right")
    iota512 = sb("iota512", 128, 512, F32, ph123, side="right")
    iotac = sb("iotac", 128, 32, F32, ph123, side="right")
    pos_bc = sb("pos_bc", 128, TOK, F32, ph123, side="right")
    for dst, src in [(diagm, diagm_h), (diagmb, diagmb_h),
                     (peermb, peermb_h), (identb, identb_h),
                     (peerm, peerm_h), (ones1, ones1_h),
                     (ones1f, ones1f_h), (onesrow, onesrow_h),
                     (ecrow, ecrow_h), (ident, ident_h)]:
        P.dma("gpsimd", dst[:], src, sem="sD3")
    ev_consts = P.tip("sD3")
    bselv = bsel.ap().rearrange("p (b c) -> p b c", c=16)
    for dst, src in [(u128, u128_h), (bsel, bsel_h), (iota512, iota512_h),
                     (iotac, iotac_h)]:
        P.dma("gpsimd", dst[:], src, sem="sD3")
    P.dma("gpsimd", u16.ap(), u16_h, sem="sD3")
    ev_consts2 = P.tip("sD3")

    ph12 = ExitStack()
    xt = [sb(f"xt{i}", 128, 1024, F32, ph12, side="right") for i in range(2)]
    lnsq = sb("lnsq", 128, 1024, BF16, ph12, side="right")
    xnf = sb("xnf", 128, 1024, F32, ph12, side="right")
    xlo = sb("xlo", 128, 1024, BF16, ph12, side="right")
    xnqT = sb("xnqT", 128, 8 * TOK, BF16, ph12, side="right")
    tstage = [sb(f"tstage{i}", 128, 8 * 128, BF16, ph12, side="right")
              for i in range(2)]
    xloT = sb("xloT", 128, 8 * TOK, BF16, ph12, side="right")
    wgs = sb("wgs", 128, 8 * E_, F32, ph12, side="right")
    wghi = sb("wghi", 128, 8 * E_, BF16, ph12, side="right")
    wglo = sb("wglo", 128, 8 * E_, BF16, ph12, side="right")
    wglof = sb("wglof", 128, 8 * E_, F32, ph12, side="right")
    bgs = sb("bgs", 1, 4 * E_, F32, ph12, side="right")
    bghi = sb("bghi", 1, E_, BF16, ph12, side="right")
    bglo = sb("bglo", 1, E_, BF16, ph12, side="right")
    lgT = sb("lgT", 8, TOK, F32, ph12, side="right")
    lg = sb("lg", 128, 16 * 8, F32, ph12, side="right")
    oh = sb("oh", 128, 16 * 8, F32, ph12, side="right")
    ohb = sb("ohb", 128, 16 * 8, BF16, ph12, side="right")
    rk = sb("rk", 128, 16 * 8, F32, ph12, side="right")
    cnt_sb = sb("cnt_sb", 16, E_, F32, ph12, side="right")
    off_sb = sb("off_sb", 16, E_, F32, ph12, side="right")
    offrow = sb("offrow", 1, 16 * E_, F32, ph12, side="right")

    posT = sb("posT", 16, 128, F32, ph12, side="right")
    posrow = lgT

    e_wg = P.dma("gpsimd", wgs.ap().rearrange("p (b c) -> p b c", c=E_),
                 wg_h.rearrange("(b p) c -> p b c", p=128), sem="sD3")
    e_bg = P.dma("gpsimd", bgs[0:1, 0:E_], bg_h, sem="sD3")
    e_wghi = P.emit("vector", lambda q: q.tensor_copy(wghi[:], wgs[:]),
                    [e_wg], "sV")
    e2 = P.emit("vector", lambda q: q.tensor_tensor(
        wglof[:], wgs[:], wghi[:], ALU.subtract), [e_wghi], "sV")
    e_wglo = P.emit("vector", lambda q: q.tensor_copy(wglo[:], wglof[:]),
                    [e2], "sV")
    e_bghi = P.emit("vector", lambda q: q.tensor_copy(
        bghi[:], bgs[0:1, 0:E_]), [e_bg], "sV")
    e2 = P.emit("vector", lambda q: q.tensor_tensor(
        bgs[0:1, E_:2 * E_], bgs[0:1, 0:E_], bghi[:], ALU.subtract),
        [e_bghi], "sV")
    e_bglo = P.emit("vector", lambda q: q.tensor_copy(
        bglo[:], bgs[0:1, E_:2 * E_]), [e2], "sV")

    xnqTv = xnqT.ap().rearrange("p (b c) -> p b c", c=TOK)
    xloTv = xloT.ap().rearrange("p (b c) -> p b c", c=TOK)
    prev_xt = [None, None]
    prev_lo_dmat = None
    ev_tstage = [None, None]
    ev_xn = [None] * 32
    for t in range(32):
        bi = t % 2
        e_x = P.dma("sync", xt[bi][:], x_h[t * 128:(t + 1) * 128, :],
                    [prev_xt[bi]], "sD")
        c0, c1 = 2 * bi, 2 * bi + 1
        e_s1 = P.emit("vector", lambda q, b=bi, c=c0: q.reduce_sum(
            small[:, c:c + 1], xt[b][:], AX.X), [e_x], "sV")
        e_s2 = P.emit("scalar", lambda q, b=bi, c=c1: q.activation(
            lnsq[:], xt[b][:], AF.Square, accum_out=small[:, c:c + 1]),
            [e_x], "sA")
        e_m = P.emit("vector", lambda q, c=c0: q.tensor_scalar(
            small[:, c + 4:c + 5], small[:, c:c + 1], 1.0 / D_, None,
            ALU.mult), [e_s1], "sV")
        e_ms = P.emit("scalar", lambda q, c=c0: q.activation(
            small[:, c + 8:c + 9], small[:, c + 4:c + 5], AF.Square),
            [e_m], "sA")
        e_v = P.emit("vector", lambda q, c0=c0, c1=c1: q.tensor_scalar(
            small[:, c1 + 8:c1 + 9], small[:, c1:c1 + 1], 1.0 / D_,
            small[:, c0 + 8:c0 + 9], ALU.mult, ALU.subtract),
            [e_s2, e_ms], "sV")
        e_ve = P.emit("vector", lambda q, c1=c1: q.tensor_scalar(
            small[:, c1 + 8:c1 + 9], small[:, c1 + 8:c1 + 9], EPS, None,
            ALU.add), [e_v], "sV")
        e_sq = P.emit("scalar", lambda q, c1=c1: q.activation(
            small[:, c1 + 12:c1 + 13], small[:, c1 + 8:c1 + 9], AF.Sqrt),
            [e_ve], "sA")
        e_rs = P.emit("vector", lambda q, c1=c1: q.reciprocal(
            small[:, c1 + 16:c1 + 17], small[:, c1 + 12:c1 + 13]),
            [e_sq], "sV")
        ev_xn[t] = P.emit("vector", lambda q, b=bi, t=t, c0=c0, c1=c1:
                          q.tensor_scalar(
            xv[:, t, :], xt[b][:], small[:, c0 + 4:c0 + 5],
            small[:, c1 + 16:c1 + 17], ALU.subtract, ALU.mult),
            [e_rs], "sV")
        prev_xt[bi] = ev_xn[t]
        tsv = tstage[bi].ap().rearrange("p (b c) -> p b c", c=128)
        for db in range(8):
            P.dmat("sync" if (t + db) % 2 == 0 else "scalar",
                   tsv[:, db, :], xv[:, t, db * 128:(db + 1) * 128],
                   [ev_xn[t], ev_tstage[bi]],
                   "sD" if (t + db) % 2 == 0 else "sD2")
        ev_tstage[bi] = P.dma(
            "sync",
            xnTd_h[:, :, t * 128:(t + 1) * 128].rearrange("b p c -> p b c"),
            tsv, [P.tip("sD"), P.tip("sD2")], "sD")
        if t < 16:
            e_f = P.emit("vector", lambda q, b=bi, c0=c0, c1=c1:
                         q.tensor_scalar(
                xnf[:], xt[b][:], small[:, c0 + 4:c0 + 5],
                small[:, c1 + 16:c1 + 17], ALU.subtract, ALU.mult),
                [e_rs], "sV")
            e_lo = P.emit("vector", lambda q, t=t: q.tensor_tensor(
                xlo[:], xnf[:], xv[:, t, :], ALU.subtract),
                [e_f, ev_xn[t], prev_lo_dmat], "sV")
            prev_xt[bi] = e_lo
            for db in range(8):
                P.dmat("sync", xnqTv[:, db, t * 128:(t + 1) * 128],
                       xv[:, t, db * 128:(db + 1) * 128], [ev_xn[t]], "sD")
                P.dmat("sync", xloTv[:, db, t * 128:(t + 1) * 128],
                       xlo[:, db * 128:(db + 1) * 128], [e_lo], "sD")
            prev_lo_dmat = P.tip("sD")
    ev_xT_all = P.tip("sD")
    ev_xnTd = [P.tip("sD"), P.tip("sD2")]

    wghv = wghi.ap().rearrange("p (b c) -> p b c", c=E_)
    wglv = wglo.ap().rearrange("p (b c) -> p b c", c=E_)
    lg_ev = []
    for ch in range(4):
        tl = ch * 512
        for db in range(8):
            P.emit("tensor", lambda q, db=db, tl=tl, f=(db == 0): q.matmul(
                psum[0].ap()[0:8, :], wghv[:, db, :],
                xnqTv[:, db, tl:tl + 512], start=f, stop=False),
                [ev_xT_all, e_wghi, lg_ev[-1] if lg_ev else None], None)
            P.emit("tensor", lambda q, db=db, tl=tl: q.matmul(
                psum[0].ap()[0:8, :], wglv[:, db, :],
                xnqTv[:, db, tl:tl + 512], start=False, stop=False),
                [e_wglo], None)
            P.emit("tensor", lambda q, db=db, tl=tl: q.matmul(
                psum[0].ap()[0:8, :], wghv[:, db, :],
                xloTv[:, db, tl:tl + 512], start=False, stop=False), [], None)
        P.emit("tensor", lambda q: q.matmul(
            psum[0].ap()[0:8, :], bghi[:], onesrow[:], start=False,
            stop=False), [e_bghi, ev_consts], None)
        e_mm = P.emit("tensor", lambda q: q.matmul(
            psum[0].ap()[0:8, :], bglo[:], onesrow[:], start=False,
            stop=True), [e_bglo], "sPE")
        lg_ev.append(P.emit("vector", lambda q, tl=tl: q.tensor_copy(
            lgT[:, tl:tl + 512], psum[0].ap()[0:8, :]), [e_mm], "sV"))

    lgv = lg.ap().rearrange("p (b c) -> p b c", c=8)
    ohv = oh.ap().rearrange("p (b c) -> p b c", c=8)
    ohbv = ohb.ap().rearrange("p (b c) -> p b c", c=8)
    rkv = rk.ap().rearrange("p (b c) -> p b c", c=8)
    ev_oh = [None] * 16
    ev_ohb = [None] * 16
    prev_tp = None
    for b in range(16):
        e_tp = P.emit("tensor", lambda q, b=b: q.transpose(
            psum[1].ap()[:, 0:8], lgT[:, b * 128:(b + 1) * 128],
            ident.ap()[0:8, 0:8]),
            [lg_ev[b // 4], ev_consts, prev_tp], "sPE")
        e_cp = P.emit("vector", lambda q, b=b: q.tensor_copy(
            lgv[:, b, :], psum[1].ap()[:, 0:8]), [e_tp], "sV")
        prev_tp = e_cp
        sc = 32 + (b % 8)
        e_mx = P.emit("vector", lambda q, b=b, sc=sc: q.tensor_reduce(
            small[:, sc:sc + 1], lgv[:, b, :], AX.X, ALU.max), [e_cp], "sV")
        ev_oh[b] = P.emit("vector", lambda q, b=b, sc=sc: q.tensor_scalar(
            ohv[:, b, :], lgv[:, b, :], small[:, sc:sc + 1], None, ALU.is_ge),
            [e_mx], "sV")
        ev_ohb[b] = P.emit("vector", lambda q, b=b: q.tensor_copy(
            ohbv[:, b, :], ohv[:, b, :]), [ev_oh[b]], "sV")
    for b in range(16):
        P.emit("tensor", lambda q, b=b: q.matmul(
            psum[2].ap()[0:16, 0:8], bselv[:, b, :], ohbv[:, b, :],
            start=(b == 0), stop=(b == 15)),
            [ev_ohb[b], ev_consts2], "sPE" if b == 15 else None)
    e_cnt = P.emit("vector", lambda q: q.tensor_copy(
        cnt_sb[:], psum[2].ap()[0:16, 0:8]), [P.tip("sPE")], "sV")
    e_omm = P.emit("tensor", lambda q: q.matmul(
        psum[3].ap()[0:16, 0:8], u16[:], cnt_sb[:], start=True, stop=True),
        [e_cnt], "sPE")
    e_off = P.emit("vector", lambda q: q.tensor_copy(
        off_sb[:], psum[3].ap()[0:16, 0:8]), [e_omm], "sV")
    e_orow = P.dma("sync", offrow.ap().rearrange("a (b c) -> a b c", c=E_),
                   off_sb.ap(), [e_off], "sD")

    ev_pos = [None] * 16
    for b in range(16):
        pp = psum[4 + b % 2]
        P.emit("tensor", lambda q, b=b, pp=pp: q.matmul(
            pp.ap()[:, 0:8], u128[:], ohbv[:, b, :], start=True, stop=False),
            [ev_ohb[b], ev_consts2, ev_pos[b - 2] if b >= 2 else None], None)
        P.emit("tensor", lambda q, b=b, pp=pp: q.matmul(
            pp.ap()[:, 0:8], ones1f[:], offrow[:, b * 8:(b + 1) * 8],
            start=False, stop=False), [e_orow], None)
        e3 = P.emit("tensor", lambda q, pp=pp: q.matmul(
            pp.ap()[:, 0:8], ones1[:], ecrow[:], start=False, stop=True),
            [ev_consts], "sPE")
        e_rk = P.emit("vector", lambda q, b=b, pp=pp: q.tensor_copy(
            rkv[:, b, :], pp.ap()[:, 0:8]), [e3], "sV")
        e_pr = P.emit("vector", lambda q, b=b: q.tensor_tensor(
            rkv[:, b, :], rkv[:, b, :], ohv[:, b, :], ALU.mult),
            [e_rk, ev_oh[b]], "sV")
        ev_pos[b] = P.emit("vector", lambda q, b=b: q.reduce_sum(
            pos[:, b:b + 1], rkv[:, b, :], AX.X), [e_pr], "sV")
    e_pt = P.emit("tensor", lambda q: q.transpose(
        psum[5].ap()[0:16, 0:128], pos[:], ident[:]),
        ev_pos + [P.tip("sPE")], "sPE")
    e_ptc = P.emit("vector", lambda q: q.tensor_copy(
        posT[:], psum[5].ap()[0:16, 0:128]), [e_pt], "sV")
    e_prw = P.dma("sync",
                  posrow.ap()[0:1, :].rearrange("a (b c) -> a b c", c=128),
                  posT.ap(), [e_ptc], "sD")
    ev_pbc = []
    for ch in range(4):
        e_b = P.emit("tensor", lambda q, ch=ch: q.matmul(
            psum[6].ap()[:, :], ones1f[:],
            posrow.ap()[0:1, ch * 512:(ch + 1) * 512],
            start=True, stop=True),
            [e_prw, ev_consts, ev_pbc[-1] if ev_pbc else None], "sPE")
        ev_pbc.append(P.emit("vector", lambda q, ch=ch: q.tensor_copy(
            pos_bc[:, ch * 512:(ch + 1) * 512], psum[6].ap()[:, :]),
            [e_b], "sV"))
    ev_posbc = ev_pbc[-1]

    ph12.close()
    g = sb("g", 128, 16 * 1024, BF16)
    gv = g.ap().rearrange("p (b c) -> p b c", c=1024)
    ph3 = ExitStack()
    m1s = sb("m1s", 128, 512, BF16, ph3, side="right")
    posadj = sb("posadj", 128, 16, F32, ph3, side="right")
    xns = sb("xns", 128, 8 * 512, BF16, ph3, side="right")
    w1st = [sb(f"w1st{i}", 128, 8 * 256, F32, ph3, side="right") for i in range(2)]
    w1bf = [sb(f"w1bf{i}", 128, 8 * 256, BF16, ph3, side="right") for i in range(2)]
    ghf = sb("ghf", 128, 32 * 512, BF16, ph3, side="right")
    gx2 = sb("gx2", 128, 1024, F32, ph3, side="right")
    gxa = sb("gxa", 128, 1024, F32, ph3, side="right")
    gxs = gx2
    w2st = [sb(f"w2st{i}", 128, 1024, F32, ph3, side="right") for i in range(2)]
    w2bf = [sb(f"w2bf{i}", 128, 1024, BF16, ph3, side="right") for i in range(2)]
    ysb = sb("ysb", 128, 4 * 1024, BF16, ph3, side="right")
    m1t = [sb(f"m1t{i}", 128, 128, BF16, ph3, side="right") for i in range(4)]
    xnsv = xns.ap().rearrange("p (b c) -> p b c", c=512)
    w1sv = [w.ap().rearrange("p (b c) -> p b c", c=256) for w in w1st]
    w1bv = [w.ap().rearrange("p (b c) -> p b c", c=256) for w in w1bf]
    ghv = ghf.ap().rearrange("p (b c) -> p b c", c=512)
    ysv = ysb.ap().rearrange("p (b c) -> p b c", c=1024)

    bar12 = [P.tip("sV"), P.tip("sA"), P.tip("sPE"), P.tip("sD"),
             P.tip("sD3"), P.tip("sD2")]
    e_gz = P.emit("vector", lambda q: q.memset(g[:], 0.0), [], "sV")
    ev_g = [e_gz] * 16
    ev_m1_free = bar12[0]
    ev_xns_free = None
    ev_w1free = list(bar12[:2])
    ev_w2free = list(bar12[:2])
    ev_gh_w2done = None
    ev_y_used = None
    ev_gelu_free = [None, None]
    ev_psum_free = {i: None for i in range(8)}
    ev_psum_free[6] = ev_posbc
    for e in range(E_):
        ev_sort = []
        for tb in range(16):
            e_a = P.emit("vector", lambda q, tb=tb, e=e: q.tensor_scalar(
                posadj[:, tb:tb + 1], pos[:, tb:tb + 1], -512.0 * e, None,
                ALU.add),
                (bar12 if e == 0 else []) + [ev_pos[tb] if e == 0 else None],
                "sV")
            e_m1 = P.emit("vector", lambda q, tb=tb: q.tensor_scalar(
                m1s[:], iota512[:], posadj[:, tb:tb + 1], None,
                ALU.is_equal), [e_a, ev_consts2, ev_m1_free], "sV")
            e_mm = None
            for db in range(8):
                e_mm = P.emit("tensor", lambda q, db=db, tb=tb: q.matmul(
                    psum[db].ap()[:, :], xv[:, tb, db * 128:(db + 1) * 128],
                    m1s[:], start=(tb == 0), stop=(tb == 15),
                    skip_group_check=True),
                    [e_m1, ev_xn[tb] if e == 0 else None, ev_xns_free,
                     ev_psum_free[db]], "sPE" if db == 7 else None)
            ev_m1_free = e_mm
        e_srt = P.tip("sPE")
        for db in range(8):
            ev_sort.append(P.emit("vector", lambda q, db=db: q.tensor_copy(
                xnsv[:, db, :], psum[db].ap()[:, :]), [e_srt], "sV"))
            ev_psum_free[db] = ev_sort[-1]

        ev_gelu_done = []
        for sl in range(16):
            sbi = sl % 2
            h0 = sl * 256
            e_w1d = P.dma("gpsimd", w1sv[sbi],
                          w1_h[e, :, h0:h0 + 256]
                          .rearrange("(b p) c -> p b c", p=128),
                          (bar12 if (e == 0 and sl < 2) else [])
                          + [ev_w1free[sbi]], "sD3")
            e_w1c = P.emit("vector", lambda q, sbi=sbi: q.tensor_copy(
                w1bf[sbi][:], w1st[sbi][:]), [e_w1d], "sV")
            for hb in range(2):
                hloc = sl * 2 + hb
                pb = hloc % 2
                e_mm = None
                for db in range(8):
                    e_mm = P.emit("tensor",
                                  lambda q, db=db, sbi=sbi, hb=hb, pb=pb:
                                  q.matmul(
                        psum[pb].ap()[:, :],
                        w1bv[sbi][:, db, hb * 128:(hb + 1) * 128],
                        xnsv[:, db, :], start=(db == 0), stop=(db == 7)),
                        [e_w1c, ev_sort[db], ev_psum_free[pb],
                         ev_gh_w2done], "sPE" if db == 7 else None)
                par = hloc % 2
                cs = slice(par * 512, par * 512 + 512)
                e1 = P.emit("scalar", lambda q, pb=pb, cs=cs: q.activation(
                    gx2[:, cs], psum[pb].ap()[:, :], AF.Square),
                    [e_mm, ev_gelu_free[par]], "sA")
                e2 = P.emit("vector", lambda q, cs=cs: q.tensor_scalar(
                    gxa[:, cs], gx2[:, cs], GC1, 1.0, ALU.mult, ALU.add),
                    [e1], "sV")
                e3 = P.emit("vector", lambda q, pb=pb, cs=cs: q.tensor_tensor(
                    gxs[:, cs], gxa[:, cs], psum[pb].ap()[:, :], ALU.mult),
                    [e2], "sV")
                e4 = P.emit("scalar", lambda q, cs=cs: q.activation(
                    gxa[:, cs], gxs[:, cs], AF.Sigmoid, scale=GC2),
                    [e3], "sA")
                e5 = P.emit("vector", lambda q, pb=pb, cs=cs, hloc=hloc:
                            q.tensor_tensor(
                    ghv[:, hloc, :], gxa[:, cs], psum[pb].ap()[:, :],
                    ALU.mult), [e4, ev_gh_w2done], "sV")
                ev_psum_free[pb] = e5
                ev_gelu_free[par] = e5
                ev_gelu_done.append(e5)
            ev_w1free[sbi] = ev_gelu_done[-1]

        e_mm = None
        for hbw in range(32):
            sbi = hbw % 2
            e_w2d = P.dma("sync", w2st[sbi][:],
                          w2_h[e, hbw * 128:(hbw + 1) * 128, :],
                          (bar12 if (e == 0 and hbw < 2) else [])
                          + [ev_w2free[sbi]], "sD")
            e_w2c = P.emit("vector", lambda q, sbi=sbi: q.tensor_copy(
                w2bf[sbi][:], w2st[sbi][:]), [e_w2d], "sV")
            for i4 in range(4):
                for dh in range(2):
                    pb = i4 * 2 + dh
                    e_mm = P.emit("tensor",
                                  lambda q, sbi=sbi, hbw=hbw, i4=i4, dh=dh,
                                  pb=pb: q.matmul(
                        psum[pb].ap()[:, :],
                        ghv[:, hbw, i4 * 128:(i4 + 1) * 128],
                        w2bf[sbi][:, dh * 512:(dh + 1) * 512],
                        start=(hbw == 0), stop=(hbw == 31),
                        skip_group_check=True),
                        [e_w2c, ev_gelu_done[hbw], ev_psum_free[pb],
                         ev_y_used],
                        "sPE" if (hbw == 31 or (i4 == 3 and dh == 1))
                        else None)
            ev_w2free[sbi] = e_mm
        ev_gh_w2done = e_mm
        e_pe_done = P.tip("sPE")
        ev_yev = []
        for i4 in range(4):
            for dh in range(2):
                pb = i4 * 2 + dh
                e_ev = P.emit("vector", lambda q, i4=i4, dh=dh, pb=pb:
                              q.tensor_copy(
                    ysv[:, i4, dh * 512:(dh + 1) * 512],
                    psum[pb].ap()[:, :]), [e_pe_done, ev_y_used], "sV")
                ev_psum_free[pb] = e_ev
                ev_yev.append(e_ev)
        e_y = ev_yev[-1]
        ev_un = []
        prev_grp = None
        for tb in range(16):
            for dh in range(2):
                pb = 4 + (tb * 2 + dh) % 2
                e_mm = None
                for ib in range(4):
                    mb = m1t[ib]
                    if dh == 0:
                        e_mt = P.emit("vector",
                                      lambda q, tb=tb, ib=ib, e=e, mb=mb:
                                      q.tensor_scalar(
                            mb[:], pos_bc[:, tb * 128:(tb + 1) * 128],
                            iotac[:, 4 * e + ib:4 * e + ib + 1], None,
                            ALU.is_equal),
                            [ev_posbc, ev_consts2, prev_grp], "sV")
                    else:
                        e_mt = None
                    e_mm = P.emit("tensor", lambda q, ib=ib, dh=dh, mb=mb,
                                  pb=pb: q.matmul(
                        pb_ap(psum, pb), mb[:],
                        ysv[:, ib, dh * 512:(dh + 1) * 512],
                        start=(ib == 0), stop=(ib == 3)),
                        [e_mt, e_y, ev_psum_free[pb]],
                        "sPE" if ib == 3 else None)
                prev_grp = e_mm if dh == 1 else prev_grp
                e_add = P.emit("vector", lambda q, tb=tb, dh=dh, pb=pb:
                               q.tensor_tensor(
                    gv[:, tb, dh * 512:(dh + 1) * 512], pb_ap(psum, pb),
                    gv[:, tb, dh * 512:(dh + 1) * 512], ALU.add),
                    [e_mm, ev_g[tb]], "sV")
                ev_psum_free[pb] = e_add
                ev_g[tb] = e_add
                ev_un.append(e_add)
        ev_y_used = ev_un[-1]
        ev_xns_free = P.tip("sPE")
    ev_silu = [None] * 16
    for tb in range(16):
        scr = gx2 if tb % 2 == 0 else gxa
        e_s = P.emit("scalar", lambda q, tb=tb, scr=scr: q.activation(
            scr[:, 0:1024], gv[:, tb, :], AF.Sigmoid),
            [ev_g[tb], ev_silu[tb - 2] if tb >= 2 else None], "sA")
        ev_silu[tb] = P.emit("vector", lambda q, tb=tb, scr=scr: q.tensor_tensor(
            gv[:, tb, :], gv[:, tb, :], scr[:, 0:1024], ALU.mult),
            [e_s], "sV")

    ph3.close()
    ph123.close()

    ffn_bar = [P.tip("sV"), P.tip("sA"), P.tip("sPE"), P.tip("sG"),
               P.tip("sD3"), P.tip("sD"), P.tip("sD2")]
    phx.close()

    aoscr_h = dt("aoscr", [8, 128, TOK], BF16, kind="Internal").ap()
    ph5 = ExitStack()
    wst = sb("wst", 128, 128, F32, ph5, side="right")
    wsb = sb("wsb", 128, 512, BF16, ph5, side="right")
    wqT = sb("wqT", 128, 8 * 128, BF16, ph5, side="right")
    wkT = sb("wkT", 128, 8 * 128, BF16, ph5, side="right")
    wvT = sb("wvT", 128, 8 * 128, BF16, ph5, side="right")
    ktp = sb("ktp", 128, 4096, BF16, ph5, side="right")
    vbuf = sb("vbuf", 128, 32 * 128, BF16, ph5, side="right")
    xch = [sb(f"xch{i}", 128, 8 * 256, BF16, ph5, side="right")
           for i in range(2)]
    xq0 = sb("xq0", 128, 8 * 256, BF16, ph5, side="right")
    xq = [xq0, xq0]
    qtp = sb("qtp", 128, 3 * 256, BF16, ph5, side="right")
    zb = sb("zb", 128, 2 * 12288, F32, ph5, side="right")
    ratt = [sb(f"ratt{i}", 128, 4352, BF16, ph5, side="right") for i in range(2)]
    attT = sb("attT", 128, 4352, BF16, ph5, side="right")
    p8 = sb("p8", 128, 2 * 1536, BF16, ph5, side="right")
    rp = sb("rp", 128, 2 * 1536, BF16, ph5, side="right")
    tau2 = sb("tau2", 128, 256, BF16, ph5, side="right")
    tauf = sb("tauf", 128, 16, F32, ph5, side="right")
    tau_hlb = [sb(f"tau_hlb{pp}", 128, 128, BF16, ph5, side="right")
               for pp in range(2)]
    ones2 = sb("ones2", 128, 128, BF16, ph5, side="right")
    aot = [sb(f"aot{i}", 128, 128, BF16, ph5, side="right") for i in range(2)]
    wqTv = wqT.ap().rearrange("p (b c) -> p b c", c=128)
    wkTv = wkT.ap().rearrange("p (b c) -> p b c", c=128)
    wvTv = wvT.ap().rearrange("p (b c) -> p b c", c=128)
    vbv = vbuf.ap().rearrange("p (b c) -> p b c", c=128)
    xchv = [x.ap().rearrange("p (b c) -> p b c", c=256) for x in xch]
    xqv = [x.ap().rearrange("p (b c) -> p b c", c=256) for x in xq]

    e_o2 = P.emit("vector", lambda q: q.memset(ones2[:], 1.0), ffn_bar, "sV")
    P.emit("vector", lambda q: q.memset(tau2[:], 0.0), ffn_bar, "sV")

    ev_proj_free = None
    ev_zfree = {}
    ev_p8_free = {}
    ev_rp_free = {}
    ev_prevc = {}
    ev_aot_free = [None, None]
    ev_xch_free = [None, None]
    ev_xq_free = [None, None]
    ev_qtp_free = [None, None, None]
    ev_tau_free = {}
    ev_thl_prev = {}
    QKB = [0, 1, 2, 0]
    QK2B = [3, 4, 5, 3]
    PVB = [6, 7]
    e_lastpv = None
    ev_out_dma = []
    for pr in range(8):
        ev_w = []
        e_c = None
        for (wh, wTbv) in ((wq_h, wqTv), (wk_h, wkTv), (wv_h, wvTv)):
            for half in range(2):
                for chh in range(4):
                    e_d = P.dma("gpsimd", wst[:],
                                wh[pr * 128:(pr + 1) * 128,
                                   half * 512 + chh * 128:
                                   half * 512 + (chh + 1) * 128],
                                ffn_bar + [e_c, P.tip("sD2"),
                                           ev_w[-1] if ev_w
                                           else ev_proj_free], "sD3")
                    e_c = P.emit("vector", lambda q, chh=chh: q.tensor_copy(
                        wsb[:, chh * 128:(chh + 1) * 128], wst[:]),
                        [e_d], "sV")
                for db in range(4):
                    P.dmat("scalar", wTbv[:, half * 4 + db, :],
                           wsb[:, db * 128:(db + 1) * 128], [e_c], "sD2")
            ev_w.append(P.tip("sD2"))
        ev_wT = ev_w[-1]
        ev_kt = []
        ev_vt = []
        for ch in range(16):
            cb = ch % 2
            e_x = P.dma("sync", xchv[cb],
                        xnTd_h[:, :, ch * 256:(ch + 1) * 256]
                        .rearrange("b p c -> p b c"),
                        ffn_bar + ev_xnTd + [ev_xch_free[cb], ev_wT], "sD")
            pb = ch % 2
            e_mm = None
            for db in range(8):
                e_mm = P.emit("tensor", lambda q, db=db, pb=pb, cb=cb: q.matmul(
                    pb_ap(psum, pb, 256), wkTv[:, db, :], xchv[cb][:, db, :],
                    start=(db == 0), stop=(db == 7)),
                    [e_x, ev_wT, ev_proj_free, ev_psum_free.get(pb)],
                    "sPE" if db == 7 else None)
            e_ev = P.emit("vector", lambda q, ch=ch, pb=pb: q.tensor_copy(
                ktp[:, ch * 256:(ch + 1) * 256], pb_ap(psum, pb, 256)),
                [e_mm], "sV")
            ev_psum_free[pb] = e_ev
            ev_kt.append(e_ev)
            for tb4 in range(2):
                tb = ch * 2 + tb4
                pb2 = 2
                e_mm = None
                for db in range(8):
                    e_mm = P.emit("tensor",
                                  lambda q, db=db, tb4=tb4, pb2=pb2, cb=cb:
                                  q.matmul(
                        pb_ap(psum, pb2, 128),
                        xchv[cb][:, db, tb4 * 128:(tb4 + 1) * 128],
                        wvTv[:, db, :], start=(db == 0), stop=(db == 7)),
                        [e_x, ev_wT, ev_proj_free, ev_psum_free.get(pb2)],
                        "sPE" if db == 7 else None)
                e_ev = P.emit("vector", lambda q, tb=tb, pb2=pb2: q.tensor_copy(
                    vbv[:, tb, :], pb_ap(psum, pb2, 128)), [e_mm], "sV")
                ev_psum_free[pb2] = e_ev
                ev_vt.append(e_ev)
            ev_xch_free[cb] = P.tip("sPE")
        ev_proj = ev_kt + ev_vt + [ev_wT, e_o2]

        def front_gen(qp, st):
            nonlocal e_c
            PP = qp % 2
            PQ = qp % 3
            W1, W2 = 128 * (qp + 1), 128 * (16 - qp)
            qbs = [qp, 15 - qp]
            Ws = [W1, W2]
            chains = [(0, 0), (0, 1), (1, 0), (1, 1)]
            zos = {(h, qx): PP * 12288 +
                   (h * 2048 if qx == 0 else 4096 + h * 4096)
                   for h, qx in chains}
            ros = {(h, qx): (0 if qx == 0 else 2 * W1) for h, qx in chains}
            pos8 = {(h, qx): PP * 1536 + (h * 256 if qx == 0 else 512 + h * 512)
                    for h, qx in chains}
            uc = {c: PP * 4 + c for c in range(4)}
            e_xq = P.dma("sync",
                         xqv[PP].rearrange("p b (u c) -> p b u c", u=2)
                         [:, :, 0, :],
                         xnTd_h[:, :, qbs[0] * 128:(qbs[0] + 1) * 128]
                         .rearrange("b p c -> p b c"),
                         ffn_bar + ev_xnTd + [ev_xq_free[PP]], "sD")
            e_xq = P.dma("sync",
                         xqv[PP].rearrange("p b (u c) -> p b u c", u=2)
                         [:, :, 1, :],
                         xnTd_h[:, :, qbs[1] * 128:(qbs[1] + 1) * 128]
                         .rearrange("b p c -> p b c"),
                         [e_xq], "sD")
            ev_qt = []
            for qx in range(2):
                pb = QK2B[qx]
                e_mm = None
                for db in range(8):
                    e_mm = P.emit("tensor", lambda q, qx=qx, db=db, pb=pb,
                                  PP=PP: q.matmul(
                        pb_ap(psum, pb, 128), wqTv[:, db, :],
                        xqv[PP][:, db, qx * 128:(qx + 1) * 128],
                        start=(db == 0), stop=(db == 7)),
                        ev_proj + [e_xq, ev_psum_free.get(pb)],
                        "sPE" if db == 7 else None)
                e_ev = P.emit("vector", lambda q, qx=qx, pb=pb, PQ=PQ:
                              q.tensor_scalar(
                    qtp[:, PQ * 256 + qx * 128:PQ * 256 + (qx + 1) * 128],
                    pb_ap(psum, pb, 128), 0.125, None, ALU.mult),
                    [e_mm, ev_qtp_free[PQ], ev_qt[-1] if ev_qt else None],
                    "sV")
                ev_psum_free[pb] = e_ev
                ev_qt.append(e_ev)
            ev_xq_free[0] = P.tip("sPE")
            ev_xq_free[1] = ev_xq_free[0]
            yield
            ev_evac = {}
            for (h, qx) in chains:
                c = 2 * h + qx
                i = qbs[qx]
                nb = i + 1
                zo = zos[(h, qx)]
                pb = QKB[c]
                qsl = slice(64 * h, 64 * h + 64)
                prev_evac = ev_psum_free.get(pb)
                ev_evac[c] = []
                for rng in range(2):
                    kbase = rng * 16
                    ro = zo + rng * Ws[qx]
                    for cst in range(0, nb, 4):
                        cw = min(4, nb - cst) * 128
                        e_mm = P.emit(
                            "tensor",
                            lambda q, qx=qx, cst=cst, cw=cw, kbase=kbase,
                            pb=pb, qsl=qsl, h=h, PQ=PQ: q.matmul(
                                pb_ap(psum, pb, cw),
                                qtp.ap()[qsl,
                                         PQ * 256 + qx * 128:
                                         PQ * 256 + (qx + 1) * 128],
                                ktp.ap()[qsl,
                                         (kbase + cst) * 128:
                                         (kbase + cst) * 128 + cw],
                                start=True, stop=True,
                                tile_position=(64 * h, 0)),
                            ev_qt + [prev_evac], "sPE")
                        last = cst + cw // 128 == nb
                        zwar = ev_zfree.get((c, PP))
                        if not last:
                            e_ev = P.emit(
                                "scalar",
                                lambda q, ro=ro, cst=cst, cw=cw, pb=pb:
                                q.activation(
                                    zb[:, ro + cst * 128:ro + cst * 128 + cw],
                                    pb_ap(psum, pb, cw), AF.Identity),
                                [e_mm, zwar], "sA")
                        else:
                            e_ev0 = None
                            if cw > 128:
                                e_ev0 = P.emit(
                                    "scalar",
                                    lambda q, ro=ro, cst=cst, cw=cw, pb=pb:
                                    q.activation(
                                        zb[:, ro + cst * 128:
                                           ro + cst * 128 + cw - 128],
                                        pb_ap(psum, pb, cw - 128),
                                        AF.Identity), [e_mm, zwar], "sA")
                                ev_evac[c].append(e_ev0)
                            mask = diagm if rng == 0 else peerm
                            e_ev = P.emit(
                                "vector",
                                lambda q, ro=ro, i=i, pb=pb, cw=cw,
                                mask=mask: q.tensor_tensor(
                                    zb[:, ro + i * 128:ro + (i + 1) * 128],
                                    psum[pb].ap()[:, cw - 128:cw],
                                    mask[:], ALU.add),
                                [e_mm, e_ev0, zwar], "sV")
                        ev_evac[c].append(e_ev)
                        prev_evac = e_ev
                ev_psum_free[pb] = prev_evac
                yield
            ev_upd = {}
            for (h, qx) in chains:
                c = 2 * h + qx
                Wx2 = 2 * Ws[qx]
                zo = zos[(h, qx)]
                po = pos8[(h, qx)]
                e_p8 = P.emit("vector", lambda q, zo=zo, po=po, Wx2=Wx2:
                              q.tensor_reduce(
                    p8[:, po:po + Wx2 // 8],
                    zb.ap()[:, zo:zo + Wx2]
                    .rearrange("p (n w) -> p n w", w=8), AX.X, ALU.max),
                    ev_evac[c] + [ev_p8_free.get((c, PP))], "sV")
                e_mx = P.emit("vector", lambda q, po=po, Wx2=Wx2, c=c, PP=PP:
                              q.tensor_reduce(
                    small[:, 32 + PP * 4 + c:33 + PP * 4 + c],
                    p8[:, po:po + Wx2 // 8], AX.X, ALU.max), [e_p8], "sV")
                ev_upd[c] = P.emit("vector", lambda q, c=c, PP=PP:
                                   q.tensor_scalar(
                    tfc[:, PP * 4 + c:PP * 4 + c + 1],
                    small[:, 32 + PP * 4 + c:33 + PP * 4 + c],
                    -1.0, 1.0, ALU.mult, ALU.add), [e_mx], "sV")
            st.update(dict(qp=qp, PP=PP, qbs=qbs, Ws=Ws, chains=chains,
                           zos=zos, ros=ros, pos8=pos8, ev_upd=ev_upd,
                           W1=W1, W2=W2))

        def backA_gen(st):
            nonlocal ev_prevc
            PP = st['PP']; qbs = st['qbs']; Ws = st['Ws']
            chains = st['chains']; zos = st['zos']; ros = st['ros']
            pos8 = st['pos8']; ev_upd = st['ev_upd']
            W1 = st['W1']; W2 = st['W2']
            ev_c = {k: v for (k, v) in ev_prevc.items()}
            for j in range(JP + JF):
                pooled = j < JP
                prev_c = dict(ev_c)
                ev_f = {}
                ev_c = {}
                for (h, qx) in chains:
                    c = 2 * h + qx
                    sib = 2 * h + (1 - qx)
                    Wx2 = 2 * Ws[qx]
                    zo = zos[(h, qx)]
                    ro = ros[(h, qx)]
                    po = pos8[(h, qx)]
                    if pooled:
                        srcap = p8.ap()[:, po:po + Wx2 // 8]
                        dstap = rp.ap()[:, po:po + Wx2 // 8]
                    else:
                        srcap = zb.ap()[:, zo:zo + Wx2]
                        dstap = ratt[h].ap()[:, ro:ro + Wx2]
                    ev_f[c] = P.emit(
                        "scalar", lambda q, srcap=srcap, dstap=dstap, c=c,
                        PP=PP: q.activation(
                            dstap, srcap, AF.Relu,
                            bias=tfc[:, PP * 4 + c:PP * 4 + c + 1],
                            accum_out=tfc[:, 16 + PP * 4 + c:17 + PP * 4 + c]),
                        [ev_upd[c]] +
                        ([ev_rp_free.get((c, PP))] if pooled else
                         [prev_c.get(c), prev_c.get(sib)]), "sA")
                    if pooled:
                        ev_p8_free[(c, PP)] = ev_f[c]
                    ev_c[c] = P.emit(
                        "vector", lambda q, dstap=dstap, c=c, PP=PP:
                        q.tensor_scalar(
                            dstap, dstap, 0.0, None, ALU.is_gt, ALU.add,
                            accum_out=tfc[:, 32 + PP * 4 + c:
                                          33 + PP * 4 + c]),
                        [ev_f[c]], "sV")
                    if pooled:
                        ev_rp_free[(c, PP)] = ev_c[c]
                    if qx == 1:
                        cl0 = PP * 4 + 2 * h
                        e_rc = P.emit("vector", lambda q, cl0=cl0:
                                      q.reciprocal(
                            tfc[:, 48 + cl0:50 + cl0],
                            tfc[:, 32 + cl0:34 + cl0]),
                            [ev_c[2 * h], ev_c[2 * h + 1]], "sV")
                        e_d = P.emit("vector", lambda q, cl0=cl0:
                                     q.scalar_tensor_tensor(
                            small[:, 48 + cl0:50 + cl0],
                            tfc[:, 16 + cl0:18 + cl0], -1.0,
                            tfc[:, 48 + cl0:50 + cl0],
                            ALU.add, ALU.mult), [e_rc], "sV")
                        e_u = P.emit("vector", lambda q, cl0=cl0:
                                     q.tensor_tensor(
                            tfc[:, cl0:cl0 + 2], tfc[:, cl0:cl0 + 2],
                            small[:, 48 + cl0:50 + cl0], ALU.subtract),
                            [e_d], "sV")
                        ev_upd[2 * h] = e_u
                        ev_upd[2 * h + 1] = e_u
                yield
            ev_prevc = dict(ev_c)
            e_t4 = []
            for (h, qx) in chains:
                c = 2 * h + qx
                u0 = PP * 4 + c
                e1 = P.emit("vector", lambda q, c=c, u0=u0, PP=PP:
                            q.tensor_copy(
                    tau2[:, PP * 128 + 32 * c:PP * 128 + 32 * c + 1],
                    tfc[:, u0:u0 + 1]),
                    [ev_upd[c], ev_c[c], ev_thl_prev.get(PP)], "sV")
                e2 = P.emit("vector", lambda q, c=c, PP=PP: q.tensor_copy(
                    tauf[:, PP * 8 + 2 * c:PP * 8 + 2 * c + 1],
                    tau2[:, PP * 128 + 32 * c:PP * 128 + 32 * c + 1]),
                    [e1], "sV")
                e3 = P.emit("vector", lambda q, c=c, u0=u0, PP=PP:
                            q.tensor_tensor(
                    tauf[:, PP * 8 + 2 * c + 1:PP * 8 + 2 * c + 2],
                    tfc[:, u0:u0 + 1],
                    tauf[:, PP * 8 + 2 * c:PP * 8 + 2 * c + 1],
                    ALU.subtract), [e2], "sV")
                e4 = P.emit("vector", lambda q, c=c, PP=PP: q.tensor_copy(
                    tau2[:, PP * 128 + 32 * c + 1:PP * 128 + 32 * c + 2],
                    tauf[:, PP * 8 + 2 * c + 1:PP * 8 + 2 * c + 2]),
                    [e3], "sV")
                e_t4.append(e4)
            e_thl0 = P.dmat("scalar", tau_hlb[PP][:],
                            tau2[:, PP * 128:PP * 128 + 128],
                            e_t4 + [ev_tau_free.get(PP)], "sD2")
            ev_thl_prev[PP] = e_thl0
            st['ev_thl'] = e_thl0
            for (h, qx) in chains:
                ev_zfree[(2 * h + qx, PP)] = ev_f[2 * h + qx]

        def backB_gen(st):
            nonlocal e_lastpv
            PP = st['PP']; qbs = st['qbs']; Ws = st['Ws']
            chains = st['chains']; ros = st['ros']
            PQ = st['qp'] % 3
            e_thl0 = st['ev_thl']
            ev_thl = {c: e_thl0 for c in range(4)}
            for (h, qx) in chains:
                c = 2 * h + qx
                i = qbs[qx]
                nb = i + 1
                ro = ros[(h, qx)]
                pb = QK2B[c]
                qsl = slice(64 * h, 64 * h + 64)
                ev_qk2 = []
                for rng in range(2):
                    kbase = rng * 16
                    ro2 = ro + rng * Ws[qx]
                    for cst in range(0, nb, 4):
                        cw = min(4, nb - cst) * 128
                        for j4 in range(cw // 128):
                            kb = kbase + cst + j4
                            masked = (kb == kbase + i)
                            P.emit("tensor",
                                   lambda q, kb=kb, j4=j4, pb=pb, qsl=qsl,
                                   h=h, qx=qx, PP=PP: q.matmul(
                                psum[pb].ap()[:, j4 * 128:(j4 + 1) * 128],
                                ktp.ap()[qsl, kb * 128:(kb + 1) * 128],
                                qtp.ap()[qsl,
                                         PQ * 256 + qx * 128:
                                         PQ * 256 + (qx + 1) * 128],
                                start=True, stop=False,
                                tile_position=(64 * h, 0),
                                skip_group_check=True),
                                [ev_thl[c], ev_psum_free.get(pb)], None)
                            P.emit("tensor",
                                   lambda q, j4=j4, pb=pb, c=c, PP=PP,
                                   masked=masked: q.matmul(
                                psum[pb].ap()[:, j4 * 128:(j4 + 1) * 128],
                                ones2.ap()[32 * c:32 * c + 2, 0:128],
                                tau_hlb[PP].ap()[32 * c:32 * c + 2, :],
                                start=False, stop=not masked,
                                tile_position=(32 * c, 0),
                                skip_group_check=True),
                                [ev_thl[c]],
                                None if masked else
                                ("sPE" if j4 == cw // 128 - 1 else None))
                            if masked:
                                mb_ = diagmb if rng == 0 else peermb
                                P.emit("tensor",
                                       lambda q, j4=j4, pb=pb, mb_=mb_:
                                       q.matmul(
                                    psum[pb].ap()[:, j4 * 128:(j4 + 1) * 128],
                                    mb_[:], identb[:],
                                    start=False, stop=True,
                                    tile_position=(0, 0),
                                    skip_group_check=True),
                                    [ev_consts],
                                    "sPE" if j4 == cw // 128 - 1 else None)
                        e_mm2 = P.tip("sPE")
                        eng = "scalar" if (cst // 4) % 2 == 0 else "vector"
                        if eng == "scalar":
                            e_ev2 = P.emit(
                                "scalar",
                                lambda q, ro2=ro2, cst=cst, cw=cw, pb=pb:
                                q.activation(
                                    attT[:, ro2 + cst * 128:
                                         ro2 + cst * 128 + cw],
                                    pb_ap(psum, pb, cw), AF.Relu),
                                [e_mm2], "sA")
                        else:
                            e_ev2 = P.emit(
                                "vector",
                                lambda q, ro2=ro2, cst=cst, cw=cw, pb=pb:
                                q.tensor_scalar(
                                    attT[:, ro2 + cst * 128:
                                         ro2 + cst * 128 + cw],
                                    pb_ap(psum, pb, cw), 0.0, None, ALU.max),
                                [e_mm2], "sV")
                        ev_qk2.append(e_ev2)
                        ev_psum_free[pb] = e_ev2
                    yield
                pvb = PVB[qx]
                tbs = list(range(0, i + 1)) + list(range(16, 16 + i + 1))
                e_mm = None
                for jb, tb in enumerate(tbs):
                    e_mm = P.emit("tensor", lambda q, tb=tb, jb=jb, ro=ro,
                                  h=h, pvb=pvb: q.matmul(
                        psum[pvb].ap()[64 * h:64 * h + 64, 0:128],
                        vbv[:, tb, 64 * h:64 * h + 64],
                        attT[:, ro + jb * 128:ro + (jb + 1) * 128],
                        start=(jb == 0), stop=(jb == len(tbs) - 1),
                        skip_group_check=True),
                        ev_qk2 + [ev_psum_free.get((pvb, h))],
                        "sPE" if jb == len(tbs) - 1 else None)
                e_lastpv = e_mm
                if h == 1:
                    qb = qbs[qx]
                    bi = qx
                    e_ev = P.emit("vector", lambda q, bi=bi, pvb=pvb:
                                  q.tensor_copy(aot[bi][:],
                                                psum[pvb].ap()[:, 0:128]),
                        [e_mm, ev_aot_free[bi]], "sV")
                    ev_psum_free[(pvb, 0)] = e_ev
                    ev_psum_free[(pvb, 1)] = e_ev
                    e_st = P.dma("sync", aoscr_h[pr, :, qb * 128:(qb + 1) * 128],
                                 aot[bi][:], [e_ev], "sD")
                    ev_aot_free[bi] = e_st
                    ev_out_dma.append(e_st)
                yield
            ev_qtp_free[PQ] = e_lastpv
            ev_tau_free[PP] = e_mm2

        def rr(*gens):
            live = [g for g in gens if g is not None]
            while live:
                nxt = []
                for g in live:
                    try:
                        next(g)
                        nxt.append(g)
                    except StopIteration:
                        pass
                live = nxt
        sts = {qp: {} for qp in range(8)}
        rr(front_gen(0, sts[0]))
        rr(backA_gen(sts[0]), front_gen(1, sts[1]))
        for qp in range(2, 8):
            rr(backA_gen(sts[qp - 1]), front_gen(qp, sts[qp]),
               backB_gen(sts[qp - 2]))
        rr(backA_gen(sts[7]), backB_gen(sts[6]))
        rr(backB_gen(sts[7]))
        ev_proj_free = e_lastpv

    att_bar = [P.tip("sV"), P.tip("sA"), P.tip("sPE"), P.tip("sG"),
               P.tip("sD3"), P.tip("sD"), P.tip("sD2")]
    ph5.close()
    ph6 = ExitStack()
    woT = sb("woT", 128, 8 * 1024, BF16, ph6, side="right")
    wst6 = sb("wst6", 128, 512, F32, ph6, side="right")
    wsb6 = sb("wsb6", 128, 1024, BF16, ph6, side="right")
    woTv = woT.ap().rearrange("p (b c) -> p b c", c=1024)
    ev_wo = []
    e_c = None
    for sl in range(8):
        for chh in range(2):
            e_d = P.dma("gpsimd", wst6[:],
                        wo_h[sl * 128:(sl + 1) * 128,
                             chh * 512:(chh + 1) * 512],
                        att_bar + [e_c, ev_wo[-1] if ev_wo else None], "sD3")
            e_c = P.emit("vector", lambda q, chh=chh: q.tensor_copy(
                wsb6[:, chh * 512:(chh + 1) * 512], wst6[:]), [e_d], "sV")
        for db in range(8):
            P.dmat("scalar", woTv[:, db, sl * 128:(sl + 1) * 128],
                   wsb6[:, db * 128:(db + 1) * 128], [e_c], "sD2")
        ev_wo.append(P.tip("sD2"))
    ev_woT = ev_wo[-1]
    xres = [sb(f"xres{i}", 128, 1024, F32, ph6, side="right") for i in range(2)]
    osb = [sb(f"osb{i}", 128, 1024, F32, ph6, side="right") for i in range(2)]
    aoin = [sb(f"aoin{i}", 128, 8 * 128, BF16, ph6, side="right")
            for i in range(2)]
    aov = [a.ap().rearrange("p (b c) -> p b c", c=128) for a in aoin]
    ev_osb_free = [None, None]
    ev_aoin_free = [None, None]
    ev_xres = [None, None]
    ev_aoin = [None, None]
    for qb in range(16):
        bi = qb % 2
        ev_xres[bi] = P.dma("sync", xres[bi][:], x_h[qb * 128:(qb + 1) * 128, :],
                            att_bar + [ev_osb_free[bi]], "sD")
        ev_aoin[bi] = P.dma("sync", aov[bi],
                            aoscr_h[:, :, qb * 128:(qb + 1) * 128]
                            .rearrange("b p c -> p b c"),
                            att_bar + ev_out_dma[-1:] + [ev_aoin_free[bi]],
                            "sD")
        for dh in range(2):
            pb = 6 + dh
            e_mm = None
            for eb in range(8):
                e_mm = P.emit("tensor", lambda q, eb=eb, qb=qb, dh=dh, pb=pb,
                              bi=bi: q.matmul(
                    psum[pb].ap()[:, :], aov[bi][:, eb, :],
                    woTv[:, eb, dh * 512:(dh + 1) * 512],
                    start=(eb == 0), stop=(eb == 7)),
                    [ev_aoin[bi], ev_woT, ev_psum_free.get(pb)],
                    "sPE" if eb == 7 else None)
            e_m = P.emit("vector", lambda q, qb=qb, dh=dh, bi=bi, pb=pb:
                         q.tensor_tensor(
                osb[bi][:, dh * 512:(dh + 1) * 512], psum[pb].ap()[:, :],
                gv[:, qb, dh * 512:(dh + 1) * 512], ALU.mult),
                [e_mm, ev_silu[qb], ev_osb_free[bi]], "sV")
            e_a = P.emit("vector", lambda q, qb=qb, dh=dh, bi=bi:
                         q.tensor_tensor(
                osb[bi][:, dh * 512:(dh + 1) * 512],
                osb[bi][:, dh * 512:(dh + 1) * 512],
                xres[bi][:, dh * 512:(dh + 1) * 512], ALU.add),
                [e_m, ev_xres[bi]], "sV")
            ev_psum_free[pb] = e_a
        e_st = P.dma("sync", out_h[qb * 128:(qb + 1) * 128, :], osb[bi][:],
                     [P.tip("sV")], "sO")
        ev_osb_free[bi] = e_st
        ev_aoin_free[bi] = e_mm
    P.emit("gpsimd", lambda q: q.nop(), [P.tip("sO")], None)

    ph6.close()
    P.realize(blk, sems)
    st.close()
    return nc


def pb_ap(psum, pb, w=512):
    return psum[pb].ap()[:, 0:w]


_cache = {}


def kernel(L0_attn_x, L0_attn_Wq, L0_attn_Wk, L0_attn_Wv, L0_attn_Wo,
           L0_ffn_W1, L0_ffn_W2, L0_ffn_Wg, L0_ffn_bg):
    bf = ml_dtypes.bfloat16
    x = np.ascontiguousarray(np.asarray(L0_attn_x, np.float32))
    consts = {}
    p = np.arange(128)[:, None]
    c = np.arange(128)[None, :]
    consts["diagm"] = np.where(c <= p, 0.0, NEGC).astype(np.float32)
    consts["diagmb"] = np.where(c <= p, 0.0, NEGC).astype(bf)
    consts["identb"] = np.eye(128).astype(bf)
    consts["u128"] = (np.arange(128)[:, None] < np.arange(128)[None, :]) \
        .astype(bf)
    consts["u16"] = (np.arange(16)[:, None] < np.arange(16)[None, :]) \
        .astype(np.float32)
    bselm = np.zeros((128, 16, 16), bf)
    for tb in range(16):
        bselm[:, tb, tb] = 1
    consts["bsel"] = bselm.reshape(128, 256)
    consts["ones1"] = np.ones((1, 128), bf)
    consts["ones1f"] = np.ones((1, 128), np.float32)
    consts["onesrow"] = np.ones((1, 512), bf)
    consts["ecrow"] = (np.arange(E_) * float(CAP)).astype(bf)[None, :]
    consts["iota512"] = np.broadcast_to(
        np.arange(512, dtype=np.float32)[None, :], (128, 512)).copy()
    consts["iotac"] = (np.arange(128)[:, None]
                       + 128.0 * np.arange(32)[None, :]).astype(np.float32)
    consts["ident"] = np.eye(128, dtype=np.float32)

    nc = build_program()
    in_maps = []
    for core in range(8):
        b, par = core // 2, core % 2
        xb = x[b].reshape(32, 128, D_)
        own = xb[par::2].reshape(TOK, D_)
        peer = xb[1 - par::2].reshape(TOK, D_)
        im = dict(consts)
        im["x"] = np.ascontiguousarray(np.concatenate([own, peer], 0))
        im["wq"] = np.asarray(L0_attn_Wq, np.float32)
        im["wk"] = np.asarray(L0_attn_Wk, np.float32)
        im["wv"] = np.asarray(L0_attn_Wv, np.float32)
        im["wo"] = np.asarray(L0_attn_Wo, np.float32)
        im["w1"] = np.asarray(L0_ffn_W1, np.float32)
        im["w2"] = np.asarray(L0_ffn_W2, np.float32)
        im["wg"] = np.asarray(L0_ffn_Wg, np.float32)
        im["bg"] = np.asarray(L0_ffn_bg, np.float32)[None, :]
        im["peerm"] = np.full((128, 128), NEGC if par == 0 else 0.0,
                              np.float32)
        im["peermb"] = im["peerm"].astype(bf)
        in_maps.append(im)
    res = run_bass_kernel_spmd(nc, in_maps, core_ids=list(range(8)))
    out = np.empty((B_, L_, D_), np.float32)
    for core in range(8):
        b, par = core // 2, core % 2
        oc = np.asarray(res.results[core]["out"]).reshape(16, 128, D_)
        out[b].reshape(32, 128, D_)[par::2] = oc
    return out
